# version 04
# baseline (speedup 1.0000x reference)
import contextlib
import numpy as np
import concourse.bass as bass
import concourse.mybir as mybir
from concourse.bass_utils import run_bass_kernel_spmd

F32 = mybir.dt.float32
BF16 = mybir.dt.bfloat16
AF = mybir.ActivationFunctionType
ALU = mybir.AluOpType
AX = mybir.AxisListType

T = 2048
D = 1024
NT = T // 128
ND = D // 128
NCH = T // 512
TP1 = T + 1
NEG = -1e30


class Build:

    def __init__(self, nc):
        self.nc = nc
        self.sems = {}
        self.cnt = {}

    def sem(self, name):
        if name not in self.sems:
            self.sems[name] = self.nc.semaphore(name).__enter__()
            self.cnt[name] = 0
        return self.sems[name]

    def n(self, name):
        self.sem(name)
        return self.cnt[name]

    def inc(self, name, inst, k=1):
        inst.then_inc(self.sem(name), k)
        self.cnt[name] += k
        return self.cnt[name]

    def wait(self, eng, name, val=None):
        self.sem(name)
        v = self.cnt[name] if val is None else val
        if v > 0:
            eng.wait_ge(self.sems[name], v)

    def _chain(self, name, eng, emit, waits):
        self.sem(name)
        for wname, wval in waits:
            self.wait(eng, wname, wval)
        if self.cnt[name] > 0:
            eng.wait_ge(self.sems[name], self.cnt[name])
        return self.inc(name, emit())

    def V(self, emit, waits=()):
        return self._chain("V", self.nc.vector, emit, waits)

    def S(self, emit, waits=()):
        return self._chain("S", self.nc.scalar, emit, waits)

    def G(self, emit, waits=()):
        return self._chain("G", self.nc.gpsimd, emit, waits)

    def TT(self, emit, waits=(), tag=True):
        for wname, wval in waits:
            self.wait(self.nc.tensor, wname, wval)
        inst = emit()
        if tag:
            return self.inc("T", inst)
        return self.n("T")

    def dma(self, queue, name, out, in_, waits=(), **kw):
        for wname, wval in waits:
            self.wait(queue, wname, wval)
        if queue is self.nc.gpsimd:
            self.wait(queue, name)
        inst = queue.dma_start(out=out, in_=in_, **kw)
        return self.inc(name, inst, 16)



def build_phase_a(debug=(), upto="y"):
    nc = bass.Bass("TRN2", target_bir_lowering=False)
    B = Build(nc)
    result = {}

    x_h = nc.dram_tensor("x", [T, D], F32, kind="ExternalInput").ap()
    wq0_h = nc.dram_tensor("wq0", [8, D, 128], F32, kind="ExternalInput").ap()
    wk0_h = nc.dram_tensor("wk0", [8, D, 128], F32, kind="ExternalInput").ap()
    wv0_h = nc.dram_tensor("wv0", [8, D, 128], F32, kind="ExternalInput").ap()
    wo0_h = nc.dram_tensor("wo0t", [D, D], F32, kind="ExternalInput").ap()
    wr0_h = nc.dram_tensor("wr0t", [D, D], F32, kind="ExternalInput").ap()
    wkf_h = nc.dram_tensor("wk0ft", [D, D], F32, kind="ExternalInput").ap()
    wvf_h = nc.dram_tensor("wv0ft", [D, D], F32, kind="ExternalInput").ap()
    mur_h = nc.dram_tensor("mur", [128, 8], F32, kind="ExternalInput").ap()
    muk_h = nc.dram_tensor("muk", [128, 8], F32, kind="ExternalInput").ap()
    ident_h = nc.dram_tensor("identity_i", [128, 128], F32, kind="ExternalInput").ap()
    mL0_h = nc.dram_tensor("maskL0T", [4, 128, 512], F32, kind="ExternalInput").ap()
    ohc16_h = nc.dram_tensor("ohc16_i", [128, 256], F32, kind="ExternalInput").ap()
    oh16_h = nc.dram_tensor("oh16_i", [16, 2048], F32, kind="ExternalInput").ap()
    onesrow_h = nc.dram_tensor("onesrow_i", [1, 128], F32, kind="ExternalInput").ap()

    y_h = nc.dram_tensor("y", [T, D], F32, kind="ExternalOutput").ap()
    zscr_h = nc.dram_tensor("zscr", [16, 64, T], BF16, kind="Internal").ap()

    def dbg_tensor(name, shape):
        return nc.dram_tensor("dbg_" + name, shape, F32, kind="ExternalOutput").ap()

    with contextlib.ExitStack() as _st:
        en = _st.enter_context
        xnDT = en(nc.sbuf_tensor("xnDT", [128, ND * TP1], BF16))
        ident = en(nc.sbuf_tensor("ident", [128, 128], BF16))
        onesrow = en(nc.sbuf_tensor("onesrow", [1, 128], BF16))
        negMb = en(nc.sbuf_tensor("negMb", [128, 16], F32))
        small = en(nc.sbuf_tensor("small", [128, 64], F32))
        smallb = en(nc.sbuf_tensor("smallb", [16, 4], BF16))
        block = en(nc.Block())
        E = (nc, B, debug, upto, x_h, wq0_h, wk0_h, wv0_h, wo0_h, wr0_h, wkf_h,
             wvf_h, mur_h, muk_h, ident_h, mL0_h, ohc16_h, oh16_h, onesrow_h,
             y_h, zscr_h, dbg_tensor, xnDT, ident, onesrow,
             negMb, small, smallb)
        @block.gpsimd
        def _(gpsimd):
            _emit_phase_a(E)
    return nc


def _emit_phase_a(E):
    (nc, B, debug, upto, x_h, wq0_h, wk0_h, wv0_h, wo0_h, wr0_h, wkf_h, wvf_h,
     mur_h, muk_h, ident_h, mL0_h, ohc16_h, oh16_h, onesrow_h, y_h, zscr_h,
     dbg_tensor, xnDT, ident, onesrow, negMb, small, smallb) = E
    if True:
        if True:
            B.dma(nc.gpsimd, "ldc", ident[:], ident_h)
            B.dma(nc.gpsimd, "ldc", onesrow[:], onesrow_h)

            with contextlib.ExitStack() as _st0:
                en0 = _st0.enter_context
                qP = en0(nc.sbuf_tensor("qP", [128, 8 * T], BF16))
                kP = en0(nc.sbuf_tensor("kP", [128, 8 * T], BF16))
                vTN = en0(nc.sbuf_tensor("vTN", [128, NT * 1040], BF16))
                wq0s = en0(nc.sbuf_tensor("wq0s", [128, 64 * 128], BF16))
                wk0s = en0(nc.sbuf_tensor("wk0s", [128, 64 * 128], BF16))
                wv0s = en0(nc.sbuf_tensor("wv0s", [128, 64 * 128], BF16))
                mskL0 = en0(nc.sbuf_tensor("mskL0", [128, 4 * 512], BF16))
                ohc16 = en0(nc.sbuf_tensor("ohc16", [128, 256], BF16))
                oh16 = en0(nc.sbuf_tensor("oh16", [16, 2048], BF16))
                for w_s, w_hh in ((wq0s, wq0_h), (wk0s, wk0_h), (wv0s, wv0_h)):
                    B.dma(nc.gpsimd, "ldw",
                          w_s[:].rearrange("p (a k c) -> p a k c", a=8, k=8),
                          w_hh.rearrange("a (k p) c -> p a k c", p=128))
                B.dma(nc.gpsimd, "ldc", mskL0[:].rearrange("p (m c) -> p m c", m=4),
                      mL0_h.rearrange("m p c -> p m c"))
                B.dma(nc.gpsimd, "ldc", ohc16[:], ohc16_h)
                B.dma(nc.gpsimd, "ldc", oh16[:], oh16_h)

                with contextlib.ExitStack() as _st1:
                    en1 = _st1.enter_context
                    lnscr = en1(nc.sbuf_tensor("lnscr", [128, D], F32))
                    st8 = en1(nc.sbuf_tensor("st8", [128, 8], F32))
                    xn_t = en1(nc.sbuf_tensor("xn_t", [128, 2 * D], BF16))
                    xld = en1(nc.sbuf_tensor("xld", [128, 2 * D], F32))
                    ptr = en1(nc.psum_tensor("ptr", [128, 2 * 1024], BF16))
                    B.wait(nc.tensor, "ldc")
                    B.V(lambda: nc.vector.memset(small[:, 22:23], 1e-5))
                    copyV = {}
                    trT = {}
                    applyS = {}
                    B.dma(nc.sync, "ldxA", xld[:, 0:D],
                          x_h.rearrange("(t p) d -> p t d", p=128)[:, 0, :])
                    for it in range(NT):
                        buf = it % 2
                        if it + 1 < NT:
                            B.dma(nc.sync, "ldxA" if (it + 1) % 2 == 0 else "ldxB",
                                  xld[:, (1 - buf) * D:(2 - buf) * D],
                                  x_h.rearrange("(t p) d -> p t d", p=128)[:, it + 1, :],
                                  waits=[("S", applyS.get(1 - buf, 0))])
                        xt = xld[:, buf * D:(buf + 1) * D]
                        ldsem = "ldxA" if buf == 0 else "ldxB"
                        B.wait(nc.vector, ldsem, 16 * (it // 2 + 1))
                        B.wait(nc.scalar, ldsem, 16 * (it // 2 + 1))
                        B.V(lambda: nc.vector.tensor_tensor(lnscr[:], xt, xt, ALU.mult))
                        B.V(lambda: nc.vector.tensor_reduce(st8[:, 7:8], lnscr[:], AX.X, ALU.add))
                        B.V(lambda: nc.vector.tensor_scalar(st8[:, 1:2], st8[:, 7:8], 1.0 / D, None, ALU.mult))
                        B.V(lambda: nc.vector.tensor_reduce(st8[:, 0:1], xt, AX.X, ALU.add))
                        B.V(lambda: nc.vector.tensor_scalar(st8[:, 2:3], st8[:, 0:1], 1.0 / D, None, ALU.mult))
                        B.V(lambda: nc.vector.tensor_tensor(st8[:, 3:4], st8[:, 2:3], st8[:, 2:3], ALU.mult))
                        vn = B.V(lambda: nc.vector.tensor_tensor(st8[:, 4:5], st8[:, 1:2], st8[:, 3:4], ALU.subtract))
                        B.S(lambda: nc.scalar.activation(st8[:, 5:6], st8[:, 4:5], AF.Ln,
                                                          bias=small[:, 22:23]),
                            waits=[("V", vn)])
                        B.S(lambda: nc.scalar.activation(st8[:, 6:7], st8[:, 5:6], AF.Exp, scale=-0.5))
                        sn = B.n("S")
                        B.V(lambda: nc.vector.tensor_tensor(st8[:, 7:8], st8[:, 2:3], st8[:, 6:7], ALU.mult),
                            waits=[("S", sn)])
                        vn = B.V(lambda: nc.vector.tensor_scalar(st8[:, 3:4], st8[:, 7:8], -1.0, None, ALU.mult))
                        sn = B.S(lambda: nc.scalar.activation(
                            xn_t[:, buf * D:(buf + 1) * D], xt, AF.Identity,
                            bias=st8[:, 3:4], scale=st8[:, 6:7]),
                            waits=[("V", vn), ("T", trT.get(buf, 0))])
                        applyS[buf] = sn
                        nc.tensor.wait_ge(B.sem("S"), sn)
                        if buf in copyV:
                            nc.tensor.wait_ge(B.sem("V"), copyV[buf])
                        for jd in range(ND):
                            tn = B.TT(lambda: nc.tensor.transpose(
                                ptr[:, buf * 1024 + jd * 128: buf * 1024 + (jd + 1) * 128],
                                xn_t[:, buf * D + jd * 128: buf * D + (jd + 1) * 128],
                                ident[:]))
                        trT[buf] = tn
                        for jd in range(ND):
                            vn = B.V(lambda: nc.vector.tensor_copy(
                                xnDT[:, jd * TP1 + 1 + it * 128: jd * TP1 + 1 + (it + 1) * 128],
                                ptr[:, buf * 1024 + jd * 128: buf * 1024 + (jd + 1) * 128]),
                                waits=[("T", tn)])
                        copyV[buf] = vn
                    for jd in range(ND):
                        B.V(lambda: nc.vector.memset(xnDT[:, jd * TP1: jd * TP1 + 1], 0.0))

                if "xn" in debug:
                    B.dma(nc.gpsimd, "dbg", dbg_tensor("xn", [128, ND * TP1]), xnDT[:],
                          waits=[("V", None)])
                if upto == "s1":
                    B.wait(nc.sync, "V")
                    B.wait(nc.sync, "S")
                    return

                with contextlib.ExitStack() as _st2:
                    en2 = _st2.enter_context
                    pmm = en2(nc.psum_tensor("pmm", [128, 2 * 512], F32))
                    ptrv = en2(nc.psum_tensor("ptrv", [128, 2 * 1024], BF16))
                    sqs = en2(nc.sbuf_tensor("sqs", [128, 2 * 512], BF16))
                    vscr = en2(nc.sbuf_tensor("vscr", [128, T], BF16))
                    B.wait(nc.tensor, "ldw")
                    B.wait(nc.tensor, "V")
                    B.wait(nc.tensor, "S")
                    B.V(lambda: nc.vector.memset(vTN[:], 1.0), waits=[("ldc", None)])
                    evV = {}
                    evS = {}
                    for pi in range(8):
                        for which, ws in ((0, wq0s), (1, wk0s), (2, wv0s)):
                            for ch in range(NCH):
                                buf = (pi * 12 + which * 4 + ch) % 2
                                nc.tensor.wait_ge(B.sem("V"), evV.get(buf, 0))
                                ps = pmm[:, buf * 512:(buf + 1) * 512]
                                for kd in range(ND):
                                    tn = B.TT(lambda: nc.tensor.matmul(
                                        ps,
                                        ws[:, (pi * 8 + kd) * 128:(pi * 8 + kd) * 128 + 128],
                                        xnDT[:, kd * TP1 + 1 + ch * 512: kd * TP1 + 1 + (ch + 1) * 512],
                                        start=(kd == 0), stop=(kd == ND - 1)),
                                        tag=(kd == ND - 1))
                                dst = (qP, kP, None)[which]
                                if which == 1:
                                    evV[buf] = B.V(lambda: nc.vector.tensor_scalar(
                                        dst[:, pi * T + ch * 512: pi * T + (ch + 1) * 512],
                                        ps, 0.125, None, ALU.mult), waits=[("T", tn)])
                                elif which == 0:
                                    evV[buf] = B.V(lambda: nc.vector.tensor_copy(
                                        dst[:, pi * T + ch * 512: pi * T + (ch + 1) * 512],
                                        ps), waits=[("T", tn)])
                                else:
                                    evV[buf] = B.V(lambda: nc.vector.tensor_copy(
                                        vscr[:, ch * 512:(ch + 1) * 512],
                                        ps), waits=[("T", tn)])
                        vprev = B.n("V")
                        nc.tensor.wait_ge(B.sem("V"), vprev)
                        tcopyV = {}
                        for tc in range(NT):
                            pbuf = tc % 2
                            nc.tensor.wait_ge(B.sem("V"), tcopyV.get(pbuf, 0))
                            tn = B.TT(lambda: nc.tensor.transpose(
                                ptrv[:, pbuf * 1024: pbuf * 1024 + 128],
                                vscr[:, tc * 128: (tc + 1) * 128],
                                ident[:]))
                            B.V(lambda: nc.vector.tensor_copy(
                                vTN[:, tc * 1040 + (2 * pi) * 65: tc * 1040 + (2 * pi) * 65 + 64],
                                ptrv[:, pbuf * 1024: pbuf * 1024 + 64]),
                                waits=[("T", tn)])
                            tcopyV[pbuf] = B.V(lambda: nc.vector.tensor_copy(
                                vTN[:, tc * 1040 + (2 * pi + 1) * 65: tc * 1040 + (2 * pi + 1) * 65 + 64],
                                ptrv[:, pbuf * 1024 + 64: pbuf * 1024 + 128]))

                    sq_guard = {}
                    for which, src in ((0, qP), (1, kP)):
                        for ch in range(NCH):
                            pbuf = ch % 2
                            ps = pmm[:, pbuf * 512:(pbuf + 1) * 512]
                            nc.tensor.wait_ge(B.sem("V"), sq_guard.get(("ps", pbuf), 0))
                            for pi in range(8):
                                sbuf_i = pi % 2
                                vn = B.V(lambda: nc.vector.tensor_tensor(
                                    sqs[:, sbuf_i * 512:(sbuf_i + 1) * 512],
                                    src[:, pi * T + ch * 512: pi * T + (ch + 1) * 512],
                                    src[:, pi * T + ch * 512: pi * T + (ch + 1) * 512],
                                    ALU.mult),
                                    waits=[("T", sq_guard.get(("sq", sbuf_i), 0))])
                                for hh in range(2):
                                    h = 2 * pi + hh
                                    tn = B.TT(lambda: nc.tensor.matmul(
                                        ps[0:16, :], ohc16[:, h * 16:(h + 1) * 16],
                                        sqs[:, sbuf_i * 512:(sbuf_i + 1) * 512],
                                        start=(pi == 0 and hh == 0),
                                        stop=(pi == 7 and hh == 1)),
                                        waits=([("V", vn)] if hh == 0 else ()),
                                        tag=(hh == 1))
                                sq_guard[("sq", sbuf_i)] = B.n("T")
                            sq_guard[("ps", pbuf)] = B.V(lambda: nc.vector.tensor_reduce(
                                small[0:16, which * 4 + ch: which * 4 + ch + 1],
                                ps[0:16, :], AX.X, ALU.max), waits=[("T", tn)])
                        B.V(lambda: nc.vector.tensor_reduce(
                            small[0:16, 16 + which: 17 + which],
                            small[0:16, which * 4: which * 4 + 4], AX.X, ALU.max))
                    B.V(lambda: nc.vector.tensor_tensor(
                        small[0:16, 18:19], small[0:16, 16:17], small[0:16, 17:18], ALU.mult))
                    vn = B.n("V")
                    sn = B.S(lambda: nc.scalar.activation(small[0:16, 20:21], small[0:16, 18:19], AF.Sqrt),
                             waits=[("V", vn)])
                    vn = B.V(lambda: nc.vector.tensor_scalar(
                        smallb[0:16, 0:1], small[0:16, 20:21], -1.0, None, ALU.mult),
                        waits=[("S", sn)])
                    repV = 0
                    for h in range(16):
                        pbuf = h % 2
                        nc.tensor.wait_ge(B.sem("V"), max(repV, vn))
                        tn = B.TT(lambda: nc.tensor.matmul(
                            pmm[:, pbuf * 512: pbuf * 512 + 1],
                            oh16[:, h * 128:(h + 1) * 128],
                            smallb[0:16, 0:1], start=True, stop=True))
                        repV = B.V(lambda: nc.vector.tensor_copy(
                            negMb[:, h:h + 1], pmm[:, pbuf * 512: pbuf * 512 + 1]),
                            waits=[("T", tn)])
                    nc.tensor.wait_ge(B.sem("V"), B.n("V"))

                if "qkv" in debug:
                    B.dma(nc.gpsimd, "dbg", dbg_tensor("q", [128, 8 * T]), qP[:], waits=[("V", None)])
                    B.dma(nc.gpsimd, "dbg", dbg_tensor("k", [128, 8 * T]), kP[:], waits=[("V", None)])
                    B.dma(nc.gpsimd, "dbg", dbg_tensor("v", [128, NT * 1040]), vTN[:], waits=[("V", None)])
                    B.dma(nc.gpsimd, "dbg", dbg_tensor("negm", [128, 16]), negMb[:], waits=[("V", None)])

                with contextlib.ExitStack() as _st3:
                    en3 = _st3.enter_context
                    psc = en3(nc.psum_tensor("psc", [128, 2 * 512], F32))
                    pscB = en3(nc.psum_tensor("pscB", [128, 2 * 512], F32))
                    pav = en3(nc.psum_tensor("pav", [128, 2 * 512], F32))
                    prb = en3(nc.psum_tensor("prb", [128, 512], F32))
                    etile = en3(nc.sbuf_tensor("etile", [128, 4 * 512], BF16))
                    rrow = en3(nc.sbuf_tensor("rrow", [1, 512], F32))
                    rrowb = en3(nc.sbuf_tensor("rrowb", [1, 512], BF16))
                    rb = en3(nc.sbuf_tensor("rb", [128, 512], F32))
                    zev = en3(nc.sbuf_tensor("zev", [128, 512], BF16))
                    B.wait(nc.tensor, "V")
                    expS = {}
                    avT = {}
                    zread = 0
                    zevG = 0
                    last_tr = 0
                    last_rb = 0
                    for pi in range(8):
                        hA, hB = 2 * pi, 2 * pi + 1
                        for qc in range(NCH):
                            nkc = 4 * (qc + 1)
                            for kc in range(nkc):
                                sb = kc % 2
                                last_mm = {}
                                for half, (base, psall) in ((0, (0, psc)), (1, (64, pscB))):
                                    nc.tensor.wait_ge(B.sem("S"), expS.get((half, sb), 0))
                                    ps = psall[:, sb * 512:(sb + 1) * 512]
                                    diag = kc >= 4 * qc
                                    i1 = nc.tensor.matmul(
                                        ps,
                                        kP[base:base + 64, pi * T + kc * 128: pi * T + (kc + 1) * 128],
                                        qP[base:base + 64, pi * T + qc * 512: pi * T + (qc + 1) * 512],
                                        start=True, stop=not diag,
                                        tile_position=(base, 0))
                                    if diag:
                                        m = kc - 4 * qc
                                        i1 = nc.tensor.matmul(
                                            ps, ident[:],
                                            mskL0[:, m * 512:(m + 1) * 512],
                                            start=False, stop=True)
                                    last_mm[half] = i1
                                tn = B.inc("T", last_mm[1])
                                for half, (psall, hh) in ((0, (psc, hA)), (1, (pscB, hB))):
                                    ps = psall[:, sb * 512:(sb + 1) * 512]
                                    expS[(half, sb)] = B.S(lambda: nc.scalar.activation(
                                        etile[:, (2 * sb + half) * 512:(2 * sb + half + 1) * 512],
                                        ps, AF.Exp, bias=negMb[:, hh:hh + 1]),
                                        waits=[("T", tn)])
                                for half, hh in ((0, hA), (1, hB)):
                                    if kc == 0:
                                        nc.tensor.wait_ge(B.sem("V"), zread)
                                        nc.tensor.wait_ge(B.sem("S"), zevG)
                                    avT[half] = B.TT(lambda: nc.tensor.matmul(
                                        pav[0:65, half * 512:(half + 1) * 512],
                                        vTN[:, kc * 1040 + hh * 65: kc * 1040 + (hh + 1) * 65],
                                        etile[:, (2 * sb + half) * 512:(2 * sb + half + 1) * 512],
                                        start=(kc == 0), stop=(kc == nkc - 1)),
                                        waits=[("S", expS[(half, sb)])],
                                        tag=(kc == nkc - 1))
                            for half, hh in ((0, hA), (1, hB)):
                                pa = pav[0:65, half * 512:(half + 1) * 512]
                                sn = B.S(lambda: nc.scalar.activation(
                                    rrow[0:1, :], pa[64:65, :], AF.Ln),
                                    waits=[("T", avT[half])])
                                zevG = sn
                                sn = B.S(lambda: nc.scalar.activation(
                                    rrowb[0:1, :], rrow[0:1, :], AF.Exp, scale=-1.0),
                                    waits=[("T", last_tr)])
                                tr = B.TT(lambda: nc.tensor.matmul(
                                    prb[:], onesrow[:], rrowb[0:1, :],
                                    start=True, stop=True),
                                    waits=[("S", sn), ("V", last_rb)])
                                last_tr = tr
                                vn = B.V(lambda: nc.vector.tensor_copy(rb[:], prb[:]),
                                         waits=[("T", tr)])
                                last_rb = vn
                                zread = B.V(lambda: nc.vector.tensor_tensor(
                                    zev[0:64, :], pa[0:64, :], rb[0:64, :], ALU.mult),
                                    waits=[("zst", None)])
                                B.dma(nc.gpsimd, "zst",
                                      zscr_h[hh, :, qc * 512:(qc + 1) * 512],
                                      zev[0:64, :], waits=[("V", zread)])

            with contextlib.ExitStack() as _st4:
                en4 = _st4.enter_context
                sDT = en4(nc.sbuf_tensor("sDT", [128, 8 * T], BF16))
                wr0s = en4(nc.sbuf_tensor("wr0s", [128, 64 * 128], BF16))
                wkfs = en4(nc.sbuf_tensor("wkfs", [128, 64 * 128], BF16))
                wvfs = en4(nc.sbuf_tensor("wvfs", [128, 64 * 128], BF16))
                muR = en4(nc.sbuf_tensor("muR", [128, 8], F32))
                muK = en4(nc.sbuf_tensor("muK", [128, 8], F32))
                _st4b = en4(contextlib.ExitStack())
                en4b = _st4b.enter_context
                zP = en4b(nc.sbuf_tensor("zP", [128, 8 * T], BF16))
                wo0s = en4b(nc.sbuf_tensor("wo0s", [128, 64 * 128], BF16))
                pm2 = en4b(nc.psum_tensor("pm2", [128, 2 * 512], F32))
                nT_s3 = B.n("T")
                first = True
                for w_s, w_hh in ((wo0s, wo0_h), (wr0s, wr0_h), (wkfs, wkf_h), (wvfs, wvf_h)):
                    B.dma(nc.gpsimd, "ldw2",
                          w_s[:].rearrange("p (a c) -> p a c", a=8),
                          w_hh.rearrange("(a p) c -> p a c", p=128),
                          waits=([("T", nT_s3)] if first else ()))
                    first = False
                B.dma(nc.sync, "ldw2b", muR[:], mur_h, waits=[("T", nT_s3)])
                B.dma(nc.sync, "ldw2b", muK[:], muk_h)
                B.dma(nc.sync, "zld",
                      zP[:].rearrange("p (a t) -> p a t", a=8),
                      zscr_h.rearrange("(a b) e t -> (b e) a t", b=2),
                      waits=[("zst", None), ("T", nT_s3)])
                if "z" in debug:
                    B.dma(nc.gpsimd, "dbg", dbg_tensor("z", [128, 8 * T]), zP[:],
                          waits=[("zld", None)])
                B.wait(nc.tensor, "ldw2")
                B.wait(nc.tensor, "ldw2b")
                B.wait(nc.tensor, "zld")
                B.wait(nc.tensor, "V")
                B.wait(nc.tensor, "S")
                evV = {}
                for dj in range(ND):
                    for ch in range(NCH):
                        buf = (dj * 4 + ch) % 2
                        nc.tensor.wait_ge(B.sem("V"), evV.get(buf, 0))
                        ps = pm2[:, buf * 512:(buf + 1) * 512]
                        for pi in range(8):
                            tn = B.TT(lambda: nc.tensor.matmul(
                                ps, wo0s[:, (pi * 8 + dj) * 128:(pi * 8 + dj) * 128 + 128],
                                zP[:, pi * T + ch * 512: pi * T + (ch + 1) * 512],
                                start=(pi == 0), stop=(pi == 7)), tag=(pi == 7))
                        evV[buf] = B.V(lambda: nc.vector.tensor_copy(
                            sDT[:, dj * T + ch * 512: dj * T + (ch + 1) * 512], ps),
                            waits=[("T", tn)])

                if "s" in debug:
                    B.dma(nc.gpsimd, "dbg", dbg_tensor("s", [128, 8 * T]), sDT[:],
                          waits=[("V", None)])
                if upto == "s4":
                    B.wait(nc.sync, "V")
                    B.wait(nc.sync, "S")
                    return

                nT_s4 = B.n("T")
                _st4b.close()
                with contextlib.ExitStack() as _st5:
                    en5 = _st5.enter_context
                    tcs = en5(nc.sbuf_tensor("tcs", [128, 8 * 512], BF16))
                    xrc = en5(nc.sbuf_tensor("xrc", [128, 8 * 512], BF16))
                    xkc = en5(nc.sbuf_tensor("xkc", [128, 8 * 512], BF16))
                    rc = en5(nc.sbuf_tensor("rc", [128, 8 * 512], BF16))
                    krel = en5(nc.sbuf_tensor("krel", [128, 8 * 512], BF16))
                    kac = en5(nc.sbuf_tensor("kac", [128, 8 * 512], BF16))
                    gpre = en5(nc.sbuf_tensor("gpre", [128, 8 * 512], BF16))
                    gc = en5(nc.sbuf_tensor("gc", [128, 8 * 512], BF16))
                    ydc = en5(nc.sbuf_tensor("ydc", [128, 8 * 512], BF16))
                    xt2 = en5(nc.sbuf_tensor("xt2", [128, 2 * D], F32))
                    yt2 = en5(nc.sbuf_tensor("yt2", [128, 2 * D], F32))
                    pf = en5(nc.psum_tensor("pf", [128, 2 * 512], F32))
                    ptb = en5(nc.psum_tensor("ptb", [128, 2 * 1024], BF16))
                    for ch in range(NCH):
                        for jd in range(ND):
                            a1 = xnDT[:, jd * TP1 + 1 + ch * 512: jd * TP1 + 1 + (ch + 1) * 512]
                            a0 = xnDT[:, jd * TP1 + 0 + ch * 512: jd * TP1 + 0 + (ch + 1) * 512]
                            B.V(lambda: nc.vector.tensor_tensor(
                                tcs[:, jd * 512:(jd + 1) * 512], a1, a0, ALU.subtract),
                                waits=[("dbg", None)])
                            B.V(lambda: nc.vector.scalar_tensor_tensor(
                                xrc[:, jd * 512:(jd + 1) * 512],
                                tcs[:, jd * 512:(jd + 1) * 512], muR[:, jd:jd + 1], a0,
                                ALU.mult, ALU.add))
                            B.V(lambda: nc.vector.scalar_tensor_tensor(
                                xkc[:, jd * 512:(jd + 1) * 512],
                                tcs[:, jd * 512:(jd + 1) * 512], muK[:, jd:jd + 1], a0,
                                ALU.mult, ALU.add))
                        nc.tensor.wait_ge(B.sem("V"), B.n("V"))
                        for which, (src, ws) in ((0, (xrc, wr0s)), (1, (xkc, wkfs))):
                            for dd in range(ND):
                                buf = dd % 2
                                ps = pf[:, buf * 512:(buf + 1) * 512]
                                nc.tensor.wait_ge(B.sem("S"), B.n("S"))
                                nc.tensor.wait_ge(B.sem("V"), B.n("V"))
                                for jd in range(ND):
                                    tn = B.TT(lambda: nc.tensor.matmul(
                                        ps, ws[:, (jd * 8 + dd) * 128:(jd * 8 + dd) * 128 + 128],
                                        src[:, jd * 512:(jd + 1) * 512],
                                        start=(jd == 0), stop=(jd == 7)), tag=(jd == 7))
                                if which == 0:
                                    B.S(lambda: nc.scalar.activation(
                                        rc[:, dd * 512:(dd + 1) * 512], ps, AF.Sigmoid),
                                        waits=[("T", tn)])
                                else:
                                    vn = B.V(lambda: nc.vector.tensor_relu(
                                        krel[:, dd * 512:(dd + 1) * 512], ps),
                                        waits=[("T", tn)])
                                    B.S(lambda: nc.scalar.activation(
                                        kac[:, dd * 512:(dd + 1) * 512],
                                        krel[:, dd * 512:(dd + 1) * 512], AF.Square),
                                        waits=[("V", vn)])
                        for dd in range(ND):
                            buf = dd % 2
                            ps = pf[:, buf * 512:(buf + 1) * 512]
                            nc.tensor.wait_ge(B.sem("S"), B.n("S"))
                            nc.tensor.wait_ge(B.sem("V"), B.n("V"))
                            for jd in range(ND):
                                tn = B.TT(lambda: nc.tensor.matmul(
                                    ps, wvfs[:, (jd * 8 + dd) * 128:(jd * 8 + dd) * 128 + 128],
                                    kac[:, jd * 512:(jd + 1) * 512],
                                    start=(jd == 0), stop=(jd == 7)), tag=(jd == 7))
                            vn = B.V(lambda: nc.vector.tensor_tensor(
                                gpre[:, dd * 512:(dd + 1) * 512],
                                rc[:, dd * 512:(dd + 1) * 512], ps, ALU.mult),
                                waits=[("T", tn)])
                            sn = B.S(lambda: nc.scalar.activation(
                                gc[:, dd * 512:(dd + 1) * 512],
                                gpre[:, dd * 512:(dd + 1) * 512], AF.Silu),
                                waits=[("V", vn)])
                            B.V(lambda: nc.vector.tensor_tensor(
                                ydc[:, dd * 512:(dd + 1) * 512],
                                sDT[:, dd * T + ch * 512: dd * T + (ch + 1) * 512],
                                gc[:, dd * 512:(dd + 1) * 512], ALU.mult),
                                waits=[("S", sn)])
                        nc.tensor.wait_ge(B.sem("V"), B.n("V"))
                        for tb in range(4):
                            tt = ch * 4 + tb
                            tbuf = tt % 2
                            stysem = "styA" if tbuf == 0 else "styB"
                            ldsem = "ldx2A" if tbuf == 0 else "ldx2B"
                            B.dma(nc.sync, ldsem, xt2[:, tbuf * D:(tbuf + 1) * D],
                                  x_h.rearrange("(t p) d -> p t d", p=128)[:, tt, :],
                                  waits=[(stysem, None)])
                            B.wait(nc.vector, ldsem)
                            for jd in range(ND):
                                pbuf = jd % 2
                                tn = B.TT(lambda: nc.tensor.transpose(
                                    ptb[:, pbuf * 1024: pbuf * 1024 + 128],
                                    ydc[:, jd * 512 + tb * 128: jd * 512 + (tb + 1) * 128],
                                    ident[:]))
                                vn = B.V(lambda: nc.vector.tensor_tensor(
                                    yt2[:, tbuf * D + jd * 128: tbuf * D + (jd + 1) * 128],
                                    xt2[:, tbuf * D + jd * 128: tbuf * D + (jd + 1) * 128],
                                    ptb[:, pbuf * 1024: pbuf * 1024 + 128], ALU.add),
                                    waits=[("T", tn)])
                                nc.tensor.wait_ge(B.sem("V"), vn)
                            B.dma(nc.sync, stysem,
                                  y_h.rearrange("(t p) d -> p t d", p=128)[:, tt, :],
                                  yt2[:, tbuf * D:(tbuf + 1) * D], waits=[("V", None)])
                B.wait(nc.sync, "styA")
                B.wait(nc.sync, "styB")




def build_phase_a2(debug=()):
    nc = bass.Bass("TRN2", target_bir_lowering=False)
    B = Build(nc)

    y_h = nc.dram_tensor("y", [T, D], F32, kind="ExternalInput").ap()
    qkvw_h = nc.dram_tensor("qkvw_i", [D, 1536], F32, kind="ExternalInput").ap()
    qkvb_h = nc.dram_tensor("qkvb_i", [128, 12], F32, kind="ExternalInput").ap()
    projw_h = nc.dram_tensor("projw_i", [D, D], F32, kind="ExternalInput").ap()
    projb_h = nc.dram_tensor("projb_i", [128, 8], F32, kind="ExternalInput").ap()
    sink_h = nc.dram_tensor("sinkr", [1, 16], F32, kind="ExternalInput").ap()
    gatew_h = nc.dram_tensor("gatew_i", [D, 32], F32, kind="ExternalInput").ap()
    gateb_h = nc.dram_tensor("gateb_i", [32, 1], F32, kind="ExternalInput").ap()
    wup_h = nc.dram_tensor("wup_i", [32, D, 2048], F32, kind="ExternalInput").ap()
    wdn_h = nc.dram_tensor("wdn_i", [32, D, D], F32, kind="ExternalInput").ap()
    ffnw_h = nc.dram_tensor("ffnw_i", [128, 8], F32, kind="ExternalInput").ap()
    attnw_h = nc.dram_tensor("attnw_i", [128, 8], F32, kind="ExternalInput").ap()
    cosq_h = nc.dram_tensor("cosq_i", [128, T], F32, kind="ExternalInput").ap()
    sinq_h = nc.dram_tensor("sinq_i", [128, T], F32, kind="ExternalInput").ap()
    ident_h = nc.dram_tensor("identity_i", [128, 128], F32, kind="ExternalInput").ap()
    mwin_h = nc.dram_tensor("maskwin_i", [128, 256], F32, kind="ExternalInput").ap()
    oh32_h = nc.dram_tensor("oh32_i", [32, 32 * 128], F32, kind="ExternalInput").ap()
    onesrow_h = nc.dram_tensor("onesrow_i", [1, 128], F32, kind="ExternalInput").ap()

    g2_h = nc.dram_tensor("g2", [128, 8 * T], F32, kind="ExternalOutput").ap()
    xmid_h = nc.dram_tensor("xmid", [128, 8 * T], F32, kind="Internal").ap()

    def dbg_tensor(name, shape):
        return nc.dram_tensor("dbg_" + name, shape, F32, kind="ExternalOutput").ap()

    with contextlib.ExitStack() as _st:
        en = _st.enter_context
        ident = en(nc.sbuf_tensor("ident", [128, 128], BF16))
        onesrow = en(nc.sbuf_tensor("onesrow", [1, 128], BF16))
        h2 = en(nc.sbuf_tensor("h2", [128, 8 * T], BF16))
        cTb = en(nc.sbuf_tensor("cTb", [32, T], BF16))
        small = en(nc.sbuf_tensor("small", [128, 32], F32))
        ffnw = en(nc.sbuf_tensor("ffnw", [128, 8], F32))
        attnw = en(nc.sbuf_tensor("attnw", [128, 8], F32))
        block = en(nc.Block())

        @block.gpsimd
        def _(gpsimd):
            _emit_phase_a2(
                (nc, B, debug, y_h, qkvw_h, qkvb_h, projw_h, projb_h, sink_h,
                 gatew_h, gateb_h, wup_h, wdn_h, ffnw_h, cosq_h, sinq_h,
                 ident_h, mwin_h, oh32_h, onesrow_h, g2_h,
                 xmid_h, dbg_tensor, ident, onesrow, h2, cTb, small, ffnw,
                 attnw, attnw_h))
    return nc


def _emit_ln_dt(nc, B, src_h, dst, small_eps, ident, dt_out=BF16):
    with contextlib.ExitStack() as st6:
        e6 = st6.enter_context
        lnscr = e6(nc.sbuf_tensor("ln_scr", [128, D], F32))
        st8 = e6(nc.sbuf_tensor("ln_st8", [128, 8], F32))
        xn_t = e6(nc.sbuf_tensor("ln_xnt", [128, 2 * D], dt_out))
        xld = e6(nc.sbuf_tensor("ln_xld", [128, 2 * D], F32))
        ptr = e6(nc.psum_tensor("ln_ptr", [128, 2 * 1024], dt_out))
        copyV = {}
        trT = {}
        applyS = {}
        B.dma(nc.sync, "lnldA", xld[:, 0:D],
              src_h.rearrange("(t p) d -> p t d", p=128)[:, 0, :])
        base_a = B.n("lnldA") - 16
        base_b = B.n("lnldB")
        for it in range(NT):
            buf = it % 2
            if it + 1 < NT:
                B.dma(nc.sync, "lnldA" if (it + 1) % 2 == 0 else "lnldB",
                      xld[:, (1 - buf) * D:(2 - buf) * D],
                      src_h.rearrange("(t p) d -> p t d", p=128)[:, it + 1, :],
                      waits=[("S", applyS.get(1 - buf, 0))])
            xt = xld[:, buf * D:(buf + 1) * D]
            ldsem = "lnldA" if buf == 0 else "lnldB"
            base = base_a if buf == 0 else base_b
            B.wait(nc.vector, ldsem, base + 16 * (it // 2 + 1))
            B.wait(nc.scalar, ldsem, base + 16 * (it // 2 + 1))
            B.V(lambda: nc.vector.tensor_tensor(lnscr[:], xt, xt, ALU.mult))
            B.V(lambda: nc.vector.tensor_reduce(st8[:, 7:8], lnscr[:], AX.X, ALU.add))
            B.V(lambda: nc.vector.tensor_scalar(st8[:, 1:2], st8[:, 7:8], 1.0 / D, None, ALU.mult))
            B.V(lambda: nc.vector.tensor_reduce(st8[:, 0:1], xt, AX.X, ALU.add))
            B.V(lambda: nc.vector.tensor_scalar(st8[:, 2:3], st8[:, 0:1], 1.0 / D, None, ALU.mult))
            B.V(lambda: nc.vector.tensor_tensor(st8[:, 3:4], st8[:, 2:3], st8[:, 2:3], ALU.mult))
            vn = B.V(lambda: nc.vector.tensor_tensor(st8[:, 4:5], st8[:, 1:2], st8[:, 3:4], ALU.subtract))
            B.S(lambda: nc.scalar.activation(st8[:, 5:6], st8[:, 4:5], AF.Ln,
                                             bias=small_eps), waits=[("V", vn)])
            B.S(lambda: nc.scalar.activation(st8[:, 6:7], st8[:, 5:6], AF.Exp, scale=-0.5))
            sn = B.n("S")
            B.V(lambda: nc.vector.tensor_tensor(st8[:, 7:8], st8[:, 2:3], st8[:, 6:7], ALU.mult),
                waits=[("S", sn)])
            vn = B.V(lambda: nc.vector.tensor_scalar(st8[:, 3:4], st8[:, 7:8], -1.0, None, ALU.mult))
            sn = B.S(lambda: nc.scalar.activation(
                xn_t[:, buf * D:(buf + 1) * D], xt, AF.Identity,
                bias=st8[:, 3:4], scale=st8[:, 6:7]),
                waits=[("V", vn), ("T", trT.get(buf, 0))])
            applyS[buf] = sn
            nc.tensor.wait_ge(B.sem("S"), sn)
            if buf in copyV:
                nc.tensor.wait_ge(B.sem("V"), copyV[buf])
            for jd in range(ND):
                tn = B.TT(lambda: nc.tensor.transpose(
                    ptr[:, buf * 1024 + jd * 128: buf * 1024 + (jd + 1) * 128],
                    xn_t[:, buf * D + jd * 128: buf * D + (jd + 1) * 128],
                    ident[:]))
            trT[buf] = tn
            for jd in range(ND):
                vn = B.V(lambda: nc.vector.tensor_copy(
                    dst[:, jd * T + it * 128: jd * T + (it + 1) * 128],
                    ptr[:, buf * 1024 + jd * 128: buf * 1024 + (jd + 1) * 128]),
                    waits=[("T", tn)])
            copyV[buf] = vn


def _emit_phase_a2(E):
    (nc, B, debug, y_h, qkvw_h, qkvb_h, projw_h, projb_h, sink_h,
     gatew_h, gateb_h, wup_h, wdn_h, ffnw_h, cosq_h, sinq_h,
     ident_h, mwin_h, oh32_h, onesrow_h, g2_h,
     xmid_h, dbg_tensor, ident, onesrow, h2, cTb, small, ffnw,
     attnw, attnw_h) = E

    B.dma(nc.gpsimd, "ldc", ident[:], ident_h)
    B.dma(nc.gpsimd, "ldc", onesrow[:], onesrow_h)
    B.dma(nc.sync, "ldcf", ffnw[:], ffnw_h)
    B.dma(nc.sync, "ldcf", attnw[:], attnw_h)
    B.wait(nc.tensor, "ldc")
    B.wait(nc.vector, "ldcf")
    B.V(lambda: nc.vector.memset(small[:, 8:9], 1e-5))

    st69 = contextlib.ExitStack()
    xn2 = st69.enter_context(nc.sbuf_tensor("xn2", [128, 8 * T], BF16))
    _emit_ln_dt(nc, B, y_h, xn2, small[:, 8:9], ident)

    if "xn2" in debug:
        B.dma(nc.gpsimd, "dbg", dbg_tensor("xn2", [128, 8 * T]), xn2[:],
              waits=[("V", None)])

    with contextlib.ExitStack() as st7:
        e7 = st7.enter_context
        qp2 = e7(nc.sbuf_tensor("qp2", [128, 8 * T], BF16))
        kp2 = e7(nc.sbuf_tensor("kp2", [128, 2 * T], BF16))
        vp2 = e7(nc.sbuf_tensor("vp2", [128, 2 * T], BF16))
        vT2 = e7(nc.sbuf_tensor("vT2", [128, NT * 256], BF16))
        z2 = e7(nc.sbuf_tensor("z2", [128, 8 * T], BF16))
        sinkb = e7(nc.sbuf_tensor("sinkb", [128, 16], F32))
        mwin = e7(nc.sbuf_tensor("mwin", [128, 256], BF16))
        pmm = e7(nc.psum_tensor("pmm", [128, 2 * 512], F32))
        ptv = e7(nc.psum_tensor("ptv", [128, 2 * 1024], BF16))
        st7i = e7(contextlib.ExitStack())
        e7i = st7i.enter_context
        qkvs = e7i(nc.sbuf_tensor("qkvs", [128, 8 * 12 * 128], BF16))
        qkvb = e7i(nc.sbuf_tensor("qkvb", [128, 12], F32))
        cosq = e7i(nc.sbuf_tensor("cosq", [128, T], BF16))
        sinq = e7i(nc.sbuf_tensor("sinq", [128, T], BF16))
        ropes = e7i(nc.sbuf_tensor("ropes", [128, 3 * 512], BF16))
        rawc = e7i(nc.sbuf_tensor("rawc", [128, 2 * 512], BF16))
        hatc = e7i(nc.sbuf_tensor("hatc", [128, 8 * 512], BF16))
        sqx = e7i(nc.sbuf_tensor("sqx", [128, 2 * 512], BF16))
        nrow = e7i(nc.sbuf_tensor("nrow", [1, 512], F32))
        nrb = e7i(nc.sbuf_tensor("nrb", [1, 512], BF16))
        nones = e7i(nc.sbuf_tensor("nones", [128, 1], BF16))
        pnm = e7i(nc.psum_tensor("pnm", [128, 512], F32))
        pnr = e7i(nc.psum_tensor("pnr", [128, 512], F32))
        B.dma(nc.gpsimd, "ldw",
              qkvs[:].rearrange("p (k f c) -> p k f c", k=8, f=12),
              qkvw_h.rearrange("(k p) (f c) -> p k f c", p=128, c=128))
        B.dma(nc.gpsimd, "ldw", mwin[:], mwin_h)
        B.dma(nc.sync, "ldcf", qkvb[:], qkvb_h)
        B.dma(nc.gpsimd, "ldc", cosq[:], cosq_h)
        B.dma(nc.gpsimd, "ldc", sinq[:], sinq_h)
        with nc.sbuf_tensor("sinkrow", [1, 16], BF16) as sinkrow:
            B.dma(nc.gpsimd, "ldc", sinkrow[:], sink_h)
            tn = B.TT(lambda: nc.tensor.matmul(
                pmm[:, 0:16], onesrow[:], sinkrow[0:1, :], start=True, stop=True),
                waits=[("ldc", None), ("V", None), ("S", None)])
            B.V(lambda: nc.vector.tensor_copy(sinkb[:], pmm[:, 0:16]), waits=[("T", tn)])

        B.wait(nc.tensor, "ldw")
        B.wait(nc.tensor, "ldcf")
        B.wait(nc.tensor, "V")
        B.V(lambda: nc.vector.memset(nones[:], 1.0))
        B.V(lambda: nc.vector.memset(small[0:1, 10:11], 9.765625e-4))
        evV = {}
        hprevT = {}
        for ch in range(NCH):
            hbuf = 0
            sqT = {}
            for kd in range(ND):
                sb_ = kd % 2
                vq = B.V(lambda: nc.vector.tensor_tensor(
                    sqx[:, sb_ * 512:(sb_ + 1) * 512],
                    xn2[:, kd * T + ch * 512: kd * T + (ch + 1) * 512],
                    xn2[:, kd * T + ch * 512: kd * T + (ch + 1) * 512], ALU.mult),
                    waits=[("T", sqT.get(sb_, 0))])
                tms = B.TT(lambda: nc.tensor.matmul(
                    pnm[0:1, :], nones[:], sqx[:, sb_ * 512:(sb_ + 1) * 512],
                    start=(kd == 0), stop=(kd == ND - 1)),
                    waits=[("V", vq)], tag=True)
                sqT[sb_] = tms
            sr = B.S(lambda: nc.scalar.activation(
                nrow[0:1, :], pnm[0:1, :], AF.Ln,
                bias=small[0:1, 8:9], scale=small[0:1, 10:11]), waits=[("T", tms)])
            sr = B.S(lambda: nc.scalar.activation(
                nrb[0:1, :], nrow[0:1, :], AF.Exp, scale=-0.5))
            trb = B.TT(lambda: nc.tensor.matmul(
                pnr[:], onesrow[:], nrb[0:1, :], start=True, stop=True),
                waits=[("S", sr)])
            for kd in range(ND):
                hv = B.V(lambda: nc.vector.scalar_tensor_tensor(
                    hatc[:, (hbuf * 8 + kd) * 512:(hbuf * 8 + kd + 1) * 512],
                    xn2[:, kd * T + ch * 512: kd * T + (ch + 1) * 512],
                    attnw[:, kd:kd + 1], pnr[:], ALU.mult, ALU.mult),
                    waits=[("T", trb), ("T", hprevT.get(hbuf, 0))])
            nc.tensor.wait_ge(B.sem("V"), hv)
            for fi in range(12):
                isq = fi < 8
                isk = fi in (8, 9)
                buf = (ch * 12 + fi) % 2
                nc.tensor.wait_ge(B.sem("V"), evV.get(buf, 0))
                ps = pmm[:, buf * 512:(buf + 1) * 512]
                for kd in range(ND):
                    tn = B.TT(lambda: nc.tensor.matmul(
                        ps, qkvs[:, (kd * 12 + fi) * 128:(kd * 12 + fi) * 128 + 128],
                        hatc[:, (hbuf * 8 + kd) * 512:(hbuf * 8 + kd + 1) * 512],
                        start=(kd == 0), stop=(kd == ND - 1)), tag=(kd == ND - 1))
                if not (isq or isk):
                    evV[buf] = B.V(lambda: nc.vector.tensor_scalar(
                        vp2[:, (fi - 10) * T + ch * 512:(fi - 10) * T + (ch + 1) * 512],
                        ps, qkvb[:, fi:fi + 1], None, ALU.add), waits=[("T", tn)])
                    continue
                B.V(lambda: nc.vector.tensor_scalar(
                    rawc[:, buf * 512:(buf + 1) * 512], ps, qkvb[:, fi:fi + 1],
                    None, ALU.add), waits=[("T", tn)])
                dst = qp2 if isq else kp2
                dbase = fi * T if isq else (fi - 8) * T
                for b0 in (0, 64):
                    sb0 = slice(b0, b0 + 32)
                    cs = cosq[sb0, ch * 512:(ch + 1) * 512]
                    sn_ = sinq[sb0, ch * 512:(ch + 1) * 512]
                    a1 = rawc[sb0, buf * 512:(buf + 1) * 512]
                    B.V(lambda: nc.vector.tensor_copy(
                        ropes[sb0, 512:1024],
                        rawc[b0 + 32:b0 + 64, buf * 512:(buf + 1) * 512]))
                    B.V(lambda: nc.vector.tensor_tensor(ropes[sb0, 0:512], a1, cs, ALU.mult))
                    B.V(lambda: nc.vector.tensor_tensor(
                        ropes[sb0, 1024:1536], ropes[sb0, 512:1024], sn_, ALU.mult))
                    B.V(lambda: nc.vector.tensor_tensor(
                        dst[sb0, dbase + ch * 512: dbase + (ch + 1) * 512],
                        ropes[sb0, 0:512], ropes[sb0, 1024:1536], ALU.subtract))
                    B.V(lambda: nc.vector.tensor_tensor(ropes[sb0, 0:512], a1, sn_, ALU.mult))
                    B.V(lambda: nc.vector.tensor_tensor(
                        ropes[sb0, 1024:1536], ropes[sb0, 512:1024], cs, ALU.mult))
                    B.V(lambda: nc.vector.tensor_tensor(
                        ropes[sb0, 512:1024],
                        ropes[sb0, 0:512], ropes[sb0, 1024:1536], ALU.add))
                    evV[buf] = B.V(lambda: nc.vector.tensor_copy(
                        dst[b0 + 32:b0 + 64, dbase + ch * 512: dbase + (ch + 1) * 512],
                        ropes[sb0, 512:1024]))
            hprevT[hbuf] = B.n("T")
        tcv = {}
        nc.tensor.wait_ge(B.sem("V"), B.n("V"))
        for pi in range(2):
            for tc in range(NT):
                pbuf = (pi * NT + tc) % 2
                nc.tensor.wait_ge(B.sem("V"), tcv.get(pbuf, 0))
                tn = B.TT(lambda: nc.tensor.transpose(
                    ptv[:, pbuf * 1024: pbuf * 1024 + 128],
                    vp2[:, pi * T + tc * 128: pi * T + (tc + 1) * 128],
                    ident[:]))
                tcv[pbuf] = B.V(lambda: nc.vector.tensor_copy(
                    vT2[:, tc * 256 + pi * 128: tc * 256 + (pi + 1) * 128],
                    ptv[:, pbuf * 1024: pbuf * 1024 + 128]), waits=[("T", tn)])

        if "qkv2" in debug:
            B.dma(nc.gpsimd, "dbg", dbg_tensor("q2", [128, 8 * T]), qp2[:], waits=[("V", None)])
            B.dma(nc.gpsimd, "dbg", dbg_tensor("k2", [128, 2 * T]), kp2[:], waits=[("V", None)])
            B.dma(nc.gpsimd, "dbg", dbg_tensor("v2", [128, 2 * T]), vp2[:], waits=[("V", None)])
        nc.tensor.wait_ge(B.sem("V"), B.n("V"))
        st7i.close()

        with contextlib.ExitStack() as st8b:
            e8 = st8b.enter_context
            kp2d = e8(nc.sbuf_tensor("kp2d", [128, 2 * T], BF16))
            psc = e8(nc.psum_tensor("psc", [128, 2 * 512], F32))
            pav = e8(nc.psum_tensor("pav", [128, 512], F32))
            esc = e8(nc.sbuf_tensor("esc", [128, 2 * 256], BF16))
            attb = e8(nc.sbuf_tensor("attb", [128, 256], BF16))
            attT = e8(nc.sbuf_tensor("attT", [128, 256], BF16))
            arow = e8(nc.sbuf_tensor("arow", [128, 8], F32))
            for tile_ in range(2):
                for p_ in range(2):
                    B.V(lambda: nc.vector.tensor_copy(
                        kp2d[64 * (1 - p_):64 * (1 - p_) + 64, tile_ * T:(tile_ + 1) * T],
                        kp2[64 * p_:64 * p_ + 64, tile_ * T:(tile_ + 1) * T]))
            nc.tensor.wait_ge(B.sem("V"), B.n("V"))
            expS = {}
            pscV = {}
            escV = {}
            attb_gT = 0
            attT_gT = 0
            zevV = 0
            for h in range(16):
                kvh = h // 4
                ktile, kbase = kvh // 2, 64 * (kvh % 2)
                qtile, qbase = h // 2, 64 * (h % 2)
                for qt in range(NT):
                    sb = qt % 2
                    wid = 256 if qt > 0 else 128
                    k0 = (qt - 1) * 128 if qt > 0 else 0
                    nc.tensor.wait_ge(B.sem("S"), expS.get(sb, 0))
                    nc.tensor.wait_ge(B.sem("V"), pscV.get(sb, 0))
                    ps = psc[:, sb * 512: sb * 512 + wid]
                    ksrc = kp2 if kbase == qbase else kp2d
                    nc.tensor.matmul(
                        ps, qp2[qbase:qbase + 64, qtile * T + qt * 128: qtile * T + (qt + 1) * 128],
                        ksrc[qbase:qbase + 64, ktile * T + k0: ktile * T + k0 + wid],
                        start=True, stop=False, tile_position=(qbase, 0))
                    i1 = nc.tensor.matmul(
                        ps, ident[:], mwin[:, 0:wid] if qt > 0 else mwin[:, 128:256],
                        start=False, stop=True)
                    tn = B.inc("T", i1)
                    vm = B.V(lambda: nc.vector.tensor_reduce(
                        arow[:, 0:1], ps, AX.X, ALU.max), waits=[("T", tn)])
                    vm = B.V(lambda: nc.vector.tensor_scalar(
                        arow[:, 1:2], arow[:, 0:1], -0.125, None, ALU.mult))
                    pscV[sb] = vm
                    se = B.S(lambda: nc.scalar.activation(
                        esc[:, sb * 256: sb * 256 + wid], ps, AF.Exp,
                        bias=arow[:, 1:2], scale=0.125),
                        waits=[("V", max(vm, escV.get(sb, 0)))])
                    expS[sb] = se
                    B.V(lambda: nc.vector.tensor_reduce(
                        arow[:, 2:3], esc[:, sb * 256: sb * 256 + wid], AX.X, ALU.add),
                        waits=[("S", se)])
                    ss = B.S(lambda: nc.scalar.activation(
                        arow[:, 3:4], arow[:, 1:2], AF.Exp,
                        bias=sinkb[:, h:h + 1]))
                    B.V(lambda: nc.vector.tensor_tensor(
                        arow[:, 4:5], arow[:, 2:3], arow[:, 3:4], ALU.add),
                        waits=[("S", ss)])
                    B.V(lambda: nc.vector.reciprocal(arow[:, 5:6], arow[:, 4:5]))
                    va = B.V(lambda: nc.vector.tensor_scalar(
                        attb[:, 0:wid], esc[:, sb * 256: sb * 256 + wid],
                        arow[:, 5:6], None, ALU.mult), waits=[("T", attb_gT)])
                    escV[sb] = va
                    nc.tensor.wait_ge(B.sem("V"), va)
                    nblk = wid // 128
                    for kb in range(nblk):
                        tn2 = B.TT(lambda: nc.tensor.transpose(
                            ptv[:, (kb % 2) * 1024 + 512:(kb % 2) * 1024 + 640],
                            attb[:, kb * 128:(kb + 1) * 128], ident[:]),
                            waits=[("V", attT_gT)])
                        vv = B.V(lambda: nc.vector.tensor_copy(
                            attT[:, kb * 128:(kb + 1) * 128],
                            ptv[:, (kb % 2) * 1024 + 512:(kb % 2) * 1024 + 640]),
                            waits=[("T", tn2)])
                    attb_gT = tn2
                    nc.tensor.wait_ge(B.sem("V"), vv)
                    for kb in range(nblk):
                        kc = qt - 1 + kb if qt > 0 else 0
                        tn3 = B.TT(lambda: nc.tensor.matmul(
                            pav[0:64, 0:128],
                            vT2[:, kc * 256 + kvh * 64: kc * 256 + (kvh + 1) * 64],
                            attT[:, kb * 128:(kb + 1) * 128],
                            start=(kb == 0), stop=(kb == nblk - 1)),
                            waits=([("V", zevV)] if kb == 0 else ()),
                            tag=(kb == nblk - 1))
                    attT_gT = tn3
                    zevV = B.V(lambda: nc.vector.tensor_copy(
                        z2[qbase:qbase + 64, qtile * T + qt * 128: qtile * T + (qt + 1) * 128],
                        pav[0:64, 0:128]), waits=[("T", tn3)])

        if "z2" in debug:
            B.dma(nc.gpsimd, "dbg", dbg_tensor("z2d", [128, 8 * T]), z2[:], waits=[("V", None)])

        with contextlib.ExitStack() as st9:
            e9 = st9.enter_context
            projs = e9(nc.sbuf_tensor("projs", [128, 64 * 128], BF16))
            projb = e9(nc.sbuf_tensor("projb", [128, 8], F32))
            B.dma(nc.gpsimd, "ldw2",
                  projs[:].rearrange("p (k f c) -> p k f c", k=8, f=8),
                  projw_h.rearrange("(k p) (f c) -> p k f c", p=128, c=128),
                  waits=[("T", B.n("T")), ("S", B.n("S")), ("V", B.n("V"))])
            B.dma(nc.sync, "ldcf", projb[:], projb_h,
                  waits=[("T", B.n("T")), ("S", B.n("S")), ("V", B.n("V"))])
            B.wait(nc.tensor, "ldw2")
            B.wait(nc.tensor, "ldcf")
            xmc = e9(nc.sbuf_tensor("xmc", [128, 8 * 512], F32))
            sqc = e9(nc.sbuf_tensor("sqc", [128, 8 * 512], BF16))
            rrow = e9(nc.sbuf_tensor("rrow", [1, 512], F32))
            rrb = e9(nc.sbuf_tensor("rrb", [1, 512], BF16))
            onescol = e9(nc.sbuf_tensor("onescol", [128, 1], BF16))
            pms = e9(nc.psum_tensor("pms", [128, 512], F32))
            prb = e9(nc.psum_tensor("prb", [128, 512], F32))
            B.V(lambda: nc.vector.memset(onescol[:], 1.0))
            B.V(lambda: nc.vector.memset(small[0:1, 9:10], 9.765625e-4))
            prev_dma = 0
            prev_h2 = 0
            for ch in range(NCH):
                for dj in range(ND):
                    buf = dj % 2
                    ps = pmm[:, buf * 512:(buf + 1) * 512]
                    nc.tensor.wait_ge(B.sem("V"), B.n("V"))
                    for fk in range(ND):
                        tn = B.TT(lambda: nc.tensor.matmul(
                            ps, projs[:, (fk * 8 + dj) * 128:(fk * 8 + dj) * 128 + 128],
                            z2[:, fk * T + ch * 512: fk * T + (ch + 1) * 512],
                            start=(fk == 0), stop=(fk == ND - 1)), tag=(fk == ND - 1))
                    B.V(lambda: nc.vector.scalar_tensor_tensor(
                        xmc[:, dj * 512:(dj + 1) * 512], ps, projb[:, dj:dj + 1],
                        xn2[:, dj * T + ch * 512: dj * T + (ch + 1) * 512],
                        ALU.add, ALU.add), waits=[("T", tn), ("xms", prev_dma)])
                    vv = B.V(lambda: nc.vector.tensor_tensor(
                        sqc[:, dj * 512:(dj + 1) * 512],
                        xmc[:, dj * 512:(dj + 1) * 512],
                        xmc[:, dj * 512:(dj + 1) * 512], ALU.mult))
                prev_dma = B.dma(nc.sync, "xms",
                                 xmid_h[:].rearrange("p (a c) -> p a c", a=8)[:, :, ch * 512:(ch + 1) * 512],
                                 xmc[:].rearrange("p (a c) -> p a c", a=8),
                                 waits=[("V", vv)])
                nc.tensor.wait_ge(B.sem("V"), vv)
                for kd in range(ND):
                    tms = B.TT(lambda: nc.tensor.matmul(
                        pms[0:1, :], onescol[:],
                        sqc[:, kd * 512:(kd + 1) * 512],
                        start=(kd == 0), stop=(kd == ND - 1)), tag=(kd == ND - 1))
                sr = B.S(lambda: nc.scalar.activation(
                    rrow[0:1, :], pms[0:1, :], AF.Ln,
                    bias=small[0:1, 8:9], scale=small[0:1, 9:10]), waits=[("T", tms)])
                sr = B.S(lambda: nc.scalar.activation(
                    rrb[0:1, :], rrow[0:1, :], AF.Exp, scale=-0.5))
                trb = B.TT(lambda: nc.tensor.matmul(
                    prb[:], onesrow[:], rrb[0:1, :], start=True, stop=True),
                    waits=[("S", sr)])
                for dj in range(ND):
                    prev_h2 = B.V(lambda: nc.vector.scalar_tensor_tensor(
                        h2[:, dj * T + ch * 512: dj * T + (ch + 1) * 512],
                        xmc[:, dj * 512:(dj + 1) * 512], ffnw[:, dj:dj + 1],
                        prb[:], ALU.mult, ALU.mult), waits=[("T", trb)])
                nc.tensor.wait_ge(B.sem("V"), prev_h2)
                nc.tensor.wait_ge(B.sem("S"), B.n("S"))

    st69.close()
    if "h2" in debug:
        B.dma(nc.gpsimd, "dbg", dbg_tensor("h2d", [128, 8 * T]), h2[:], waits=[("V", None)])

    with contextlib.ExitStack() as s11:
        e11 = s11.enter_context
        gatews = e11(nc.sbuf_tensor("gatews", [128, 8 * 32], BF16))
        gateb = e11(nc.sbuf_tensor("gateb", [32, 1], F32))
        identf = e11(nc.sbuf_tensor("identf", [128, 128], F32))
        glog = e11(nc.sbuf_tensor("glog", [32, T], F32))
        glT = e11(nc.sbuf_tensor("glT", [128, NT * 32], F32))
        gw1 = e11(nc.sbuf_tensor("gw1", [128, 4 * 32], F32))
        gcol = e11(nc.sbuf_tensor("gcol", [128, 8], F32))
        pgl = e11(nc.psum_tensor("pgl", [128, 2 * 512], F32))
        gw_ = [("T", B.n("T")), ("V", B.n("V")), ("S", B.n("S")), ("dbg", B.n("dbg"))]
        B.dma(nc.gpsimd, "ldg", gatews[:].rearrange("p (k c) -> p k c", k=8),
              gatew_h.rearrange("(k p) c -> p k c", p=128), waits=gw_)
        B.dma(nc.sync, "ldgf", gateb[:], gateb_h, waits=gw_)
        B.dma(nc.sync, "ldgf", identf[:], ident_h)
        B.wait(nc.tensor, "ldg")
        B.wait(nc.tensor, "ldgf")
        evG = {}
        for ch in range(NCH):
            buf = ch % 2
            ps = pgl[0:32, buf * 512:(buf + 1) * 512]
            nc.tensor.wait_ge(B.sem("V"), evG.get(buf, 0))
            for kd in range(ND):
                tn = B.TT(lambda: nc.tensor.matmul(
                    ps, gatews[:, kd * 32:(kd + 1) * 32],
                    h2[:, kd * T + ch * 512: kd * T + (ch + 1) * 512],
                    start=(kd == 0), stop=(kd == ND - 1)), tag=(kd == ND - 1))
            evG[buf] = B.V(lambda: nc.vector.tensor_scalar(
                glog[:, ch * 512:(ch + 1) * 512], ps, gateb[:, 0:1], None, ALU.add),
                waits=[("T", tn)])
        trG = {}
        for tt in range(NT):
            pb = tt % 2
            nc.tensor.wait_ge(B.sem("V"), B.n("V"))
            tn = B.TT(lambda: nc.tensor.transpose(
                pgl[:, pb * 512: pb * 512 + 32].bitcast(F32) if False else pgl[:, pb * 512: pb * 512 + 32],
                glog[:, tt * 128:(tt + 1) * 128], identf[0:32, 0:32]),
                waits=[("V", trG.get(pb, 0))])
            vv = B.V(lambda: nc.vector.tensor_copy(
                glT[:, tt * 32:(tt + 1) * 32], pgl[:, pb * 512: pb * 512 + 32]),
                waits=[("T", tn)])
            trG[pb] = vv
        for tt in range(NT):
            g0 = glT[:, tt * 32:(tt + 1) * 32]
            B.V(lambda: nc.vector.tensor_reduce(gcol[:, 0:1], g0, AX.X, ALU.max))
            cur = g0
            for rnd in range(3):
                nxt = gw1[:, ((rnd + 1) % 2) * 32:((rnd + 1) % 2) * 32 + 32]
                B.V(lambda: nc.vector.tensor_scalar(
                    gw1[:, 64:96], cur, gcol[:, rnd:rnd + 1], None, ALU.is_ge))
                B.V(lambda: nc.vector.tensor_scalar(
                    gw1[:, 96:128], gw1[:, 64:96], 1e30, None, ALU.mult))
                B.V(lambda: nc.vector.tensor_tensor(nxt, cur, gw1[:, 96:128], ALU.subtract))
                B.V(lambda: nc.vector.tensor_reduce(
                    gcol[:, rnd + 1:rnd + 2], nxt, AX.X, ALU.max))
                cur = nxt
            B.V(lambda: nc.vector.tensor_scalar(
                gw1[:, 64:96], g0, gcol[:, 3:4], None, ALU.is_ge))
            vv = B.V(lambda: nc.vector.tensor_scalar(
                gcol[:, 4:5], gcol[:, 0:1], -1.0, None, ALU.mult))
            sv = B.S(lambda: nc.scalar.activation(
                gw1[:, 96:128], g0, AF.Exp, bias=gcol[:, 4:5]), waits=[("V", vv)])
            B.V(lambda: nc.vector.tensor_tensor(
                gw1[:, 0:32], gw1[:, 64:96], gw1[:, 96:128], ALU.mult),
                waits=[("S", sv)])
            B.V(lambda: nc.vector.tensor_reduce(gcol[:, 5:6], gw1[:, 0:32], AX.X, ALU.add))
            B.V(lambda: nc.vector.reciprocal(gcol[:, 6:7], gcol[:, 5:6]))
            vv = B.V(lambda: nc.vector.tensor_scalar(
                gw1[:, 32:64].bitcast(BF16)[:, 0:32] if False else gw1[:, 32:64],
                gw1[:, 0:32], gcol[:, 6:7], None, ALU.mult))
            pb = tt % 2
            nc.tensor.wait_ge(B.sem("V"), vv)
            tn = B.TT(lambda: nc.tensor.transpose(
                pgl[0:32, pb * 512 + 64: pb * 512 + 64 + 128],
                gw1[:, 32:64], identf[:]))
            vv = B.V(lambda: nc.vector.tensor_copy(
                cTb[0:32, tt * 128:(tt + 1) * 128], pgl[0:32, pb * 512 + 64: pb * 512 + 64 + 128]),
                waits=[("T", tn)])
            trG[pb] = vv

    if "comb" in debug:
        B.dma(nc.gpsimd, "dbg", dbg_tensor("comb", [32, T]), cTb[:], waits=[("V", None)])

    with contextlib.ExitStack() as s12:
        e12 = s12.enter_context
        y2a = e12(nc.sbuf_tensor("y2a", [128, 8 * 1024], F32))
        y2s = e12(nc.sbuf_tensor("y2s", [128, 512], F32))
        acts = e12(nc.sbuf_tensor("acts", [128, 16 * 512], BF16))
        ascr = e12(nc.sbuf_tensor("ascr", [128, 2 * 512], BF16))
        wupP = e12(nc.sbuf_tensor("wupP", [128, 2 * 8 * 1024], BF16))
        wdnP = e12(nc.sbuf_tensor("wdnP", [128, 2 * 8 * 1024], BF16))
        cbs = e12(nc.sbuf_tensor("cbs", [128, 2 * 512], BF16))
        oh32 = e12(nc.sbuf_tensor("oh32", [32, 32 * 128], BF16))
        xml = e12(nc.sbuf_tensor("xml", [128, 8 * 512], F32))
        g2c = e12(nc.sbuf_tensor("g2c", [128, 8 * 512], F32))
        pup = e12(nc.psum_tensor("pup", [128, 4 * 512], F32))
        pdn = e12(nc.psum_tensor("pdn", [128, 2 * 512], F32))
        pcb = e12(nc.psum_tensor("pcb", [128, 512], F32))
        B.dma(nc.gpsimd, "ldg", oh32[:], oh32_h,
              waits=[("T", B.n("T")), ("V", B.n("V")), ("S", B.n("S")),
                     ("dbg", B.n("dbg"))])
        B.wait(nc.tensor, "ldg")
        wup_r = wup_h.rearrange("e (k p) c -> e p k c", p=128)
        wdn_r = wdn_h.rearrange("e (k p) c -> e p k c", p=128)
        for th in range(2):
            tb0 = th * 2
            B.V(lambda: nc.vector.memset(y2a[:], 0.0))
            for e in range(32):
                ebuf = e % 2
                guardT = B.n("T")
                B.dma(nc.gpsimd, "lwu%d" % ebuf,
                      wupP[:, ebuf * 8192: ebuf * 8192 + 4096].rearrange("p (k c) -> p k c", k=8),
                      wup_r[e][:, :, 0:512], waits=[("T", guardT)])
                B.dma(nc.gpsimd, "lwg%d" % ebuf,
                      wupP[:, ebuf * 8192 + 4096:(ebuf + 1) * 8192].rearrange("p (k c) -> p k c", k=8),
                      wup_r[e][:, :, 1024:1536], waits=[("T", guardT)])
                B.dma(nc.gpsimd, "lwd%d" % ebuf,
                      wdnP[:, ebuf * 8192:(ebuf + 1) * 8192].rearrange("p (k c) -> p k c", k=8),
                      wdn_r[e], waits=[("T", guardT)])
                for fp in range(2):
                    if fp == 1:
                        guardT = B.n("T")
                        B.dma(nc.gpsimd, "lwu%d" % ebuf,
                              wupP[:, ebuf * 8192: ebuf * 8192 + 4096].rearrange("p (k c) -> p k c", k=8),
                              wup_r[e][:, :, 512:1024], waits=[("T", guardT)])
                        B.dma(nc.gpsimd, "lwg%d" % ebuf,
                              wupP[:, ebuf * 8192 + 4096:(ebuf + 1) * 8192].rearrange("p (k c) -> p k c", k=8),
                              wup_r[e][:, :, 1536:2048], waits=[("T", guardT)])
                    B.wait(nc.tensor, "lwu%d" % ebuf)
                    B.wait(nc.tensor, "lwg%d" % ebuf)
                    for ch2 in range(2):
                        ch = tb0 + ch2
                        if fp == 0:
                            tn = B.TT(lambda: nc.tensor.matmul(
                                pcb[:], oh32[:, e * 128:(e + 1) * 128],
                                cTb[0:32, ch * 512:(ch + 1) * 512], start=True, stop=True),
                                waits=[("V", B.n("V"))])
                            B.V(lambda: nc.vector.tensor_copy(
                                cbs[:, ch2 * 512:(ch2 + 1) * 512], pcb[:]), waits=[("T", tn)])
                        for i in range(4):
                            ft = fp * 4 + i
                            psA = pup[:, (i % 2) * 512:(i % 2) * 512 + 512]
                            psB = pup[:, 1024 + (i % 2) * 512: 1024 + (i % 2) * 512 + 512]
                            nc.tensor.wait_ge(B.sem("V"), B.n("V"))
                            nc.tensor.wait_ge(B.sem("S"), B.n("S"))
                            for kd in range(ND):
                                nc.tensor.matmul(
                                    psA, wupP[:, ebuf * 8192 + kd * 512 + i * 128:
                                              ebuf * 8192 + kd * 512 + i * 128 + 128],
                                    h2[:, kd * T + ch * 512: kd * T + (ch + 1) * 512],
                                    start=(kd == 0), stop=(kd == ND - 1))
                                tn = B.TT(lambda: nc.tensor.matmul(
                                    psB, wupP[:, ebuf * 8192 + 4096 + kd * 512 + i * 128:
                                              ebuf * 8192 + 4096 + kd * 512 + i * 128 + 128],
                                    h2[:, kd * T + ch * 512: kd * T + (ch + 1) * 512],
                                    start=(kd == 0), stop=(kd == ND - 1)), tag=(kd == ND - 1))
                            sg = B.S(lambda: nc.scalar.activation(
                                ascr[:, 0:512], psB, AF.Silu), waits=[("T", tn)])
                            uc = B.S(lambda: nc.scalar.copy(ascr[:, 512:1024], psA))
                            B.V(lambda: nc.vector.tensor_tensor(
                                acts[:, (ch2 * 8 + ft) * 512:(ch2 * 8 + ft + 1) * 512],
                                ascr[:, 0:512], ascr[:, 512:1024],
                                ALU.mult), waits=[("S", uc)])
                B.wait(nc.tensor, "lwd%d" % ebuf)
                pdnV = {}
                for ch2 in range(2):
                    for ft in range(8):
                        B.V(lambda: nc.vector.tensor_tensor(
                            ascr[:, 0:512],
                            acts[:, (ch2 * 8 + ft) * 512:(ch2 * 8 + ft + 1) * 512],
                            cbs[:, ch2 * 512:(ch2 + 1) * 512], ALU.mult))
                        vv = B.V(lambda: nc.vector.tensor_copy(
                            acts[:, (ch2 * 8 + ft) * 512:(ch2 * 8 + ft + 1) * 512],
                            ascr[:, 0:512]))
                    nc.tensor.wait_ge(B.sem("V"), B.n("V"))
                    for dj in range(ND):
                        dsl = dj % 2
                        nc.tensor.wait_ge(B.sem("V"), pdnV.get(dsl, 0))
                        for ft in range(8):
                            tn2 = B.TT(lambda: nc.tensor.matmul(
                                pdn[:, dsl * 512: dsl * 512 + 512],
                                wdnP[:, ebuf * 8192 + ft * 1024 + dj * 128:
                                     ebuf * 8192 + ft * 1024 + dj * 128 + 128],
                                acts[:, (ch2 * 8 + ft) * 512:(ch2 * 8 + ft + 1) * 512],
                                start=(ft == 0), stop=(ft == 7)), tag=(ft == 7))
                        B.V(lambda: nc.vector.tensor_tensor(
                            y2s[:], y2a[:, dj * 1024 + ch2 * 512: dj * 1024 + (ch2 + 1) * 512],
                            pdn[:, dsl * 512: dsl * 512 + 512], ALU.add),
                            waits=[("T", tn2)])
                        pdnV[dsl] = B.V(lambda: nc.vector.tensor_copy(
                            y2a[:, dj * 1024 + ch2 * 512: dj * 1024 + (ch2 + 1) * 512], y2s[:]))
                nc.tensor.wait_ge(B.sem("V"), B.n("V"))
            for ch2 in range(2):
                ch = tb0 + ch2
                dv = B.dma(nc.sync, "xml", xml[:].rearrange("p (a c) -> p a c", a=8),
                           xmid_h[:].rearrange("p (a c) -> p a c", a=8)[:, :, ch * 512:(ch + 1) * 512],
                           waits=[("V", B.n("V")), ("g2s", None), ("xms", None),
                                  ("S", B.n("S"))])
                B.wait(nc.vector, "xml", dv)
                for dj in range(ND):
                    vv = B.V(lambda: nc.vector.tensor_tensor(
                        g2c[:, dj * 512:(dj + 1) * 512],
                        xml[:, dj * 512:(dj + 1) * 512],
                        y2a[:, dj * 1024 + ch2 * 512: dj * 1024 + (ch2 + 1) * 512],
                        ALU.add))
                    sv = B.S(lambda: nc.scalar.activation(
                        xml[:, dj * 512:(dj + 1) * 512],
                        g2c[:, dj * 512:(dj + 1) * 512], AF.Silu), waits=[("V", vv)])
                B.dma(nc.sync, "g2s",
                      g2_h.rearrange("p (a t) -> p a t", a=8)[:, :, ch * 512:(ch + 1) * 512],
                      xml[:].rearrange("p (a c) -> p a c", a=8),
                      waits=[("S", B.n("S"))])
        B.wait(nc.sync, "g2s")
        B.wait(nc.sync, "xms")




def build_phase_b(debug=()):
    nc = bass.Bass("TRN2", target_bir_lowering=False)
    B = Build(nc)

    y_h = nc.dram_tensor("yb", [T, D], F32, kind="ExternalInput").ap()
    g2_h = nc.dram_tensor("g2b", [T, D], F32, kind="ExternalInput").ap()
    wq_h = nc.dram_tensor("bWq", [2, D, D], F32, kind="ExternalInput").ap()
    wk_h = nc.dram_tensor("bWk", [2, D, D], F32, kind="ExternalInput").ap()
    wv_h = nc.dram_tensor("bWv", [2, D, D], F32, kind="ExternalInput").ap()
    wb_h = nc.dram_tensor("bWb", [2, D, 8], F32, kind="ExternalInput").ap()
    wo_h = nc.dram_tensor("bWo", [2, D, D], F32, kind="ExternalInput").ap()
    wg_h = nc.dram_tensor("bWg", [2, D, 5632], F32, kind="ExternalInput").ap()
    wd_h = nc.dram_tensor("bWd", [2, 2816, D], F32, kind="ExternalInput").ap()
    n1w_h = nc.dram_tensor("n1w", [128, 16], F32, kind="ExternalInput").ap()
    n2w_h = nc.dram_tensor("n2w", [128, 16], F32, kind="ExternalInput").ap()
    onw_h = nc.dram_tensor("onw", [2, 128], F32, kind="ExternalInput").ap()
    fnw_h = nc.dram_tensor("fnw", [128, 8], F32, kind="ExternalInput").ap()
    ident_h = nc.dram_tensor("identity_i", [128, 128], F32, kind="ExternalInput").ap()
    onesrow_h = nc.dram_tensor("onesrow_i", [1, 128], F32, kind="ExternalInput").ap()
    mU_h = nc.dram_tensor("maskU_i", [128, 128], F32, kind="ExternalInput").ap()
    mP_h = nc.dram_tensor("maskP_i", [128, 128], F32, kind="ExternalInput").ap()
    oh8_h = nc.dram_tensor("oh8_i", [8, 8 * 128], F32, kind="ExternalInput").ap()
    ohc8_h = nc.dram_tensor("ohc8_i", [128, 64], F32, kind="ExternalInput").ap()

    out_h = nc.dram_tensor("outb", [T, D], F32, kind="ExternalOutput").ap()

    def dbg_tensor(name, shape):
        return nc.dram_tensor("dbg_" + name, shape, F32, kind="ExternalOutput").ap()

    with contextlib.ExitStack() as _st:
        en = _st.enter_context
        xR = en(nc.sbuf_tensor("xR", [128, 8 * T], F32))
        ident = en(nc.sbuf_tensor("identB", [128, 128], BF16))
        identf = en(nc.sbuf_tensor("identfB", [128, 128], F32))
        onesrow = en(nc.sbuf_tensor("onesrowB", [1, 128], BF16))
        mU = en(nc.sbuf_tensor("mUS", [128, 128], BF16))
        mP = en(nc.sbuf_tensor("mPS", [128, 128], BF16))
        oh8 = en(nc.sbuf_tensor("oh8S", [8, 8 * 128], BF16))
        ohc8 = en(nc.sbuf_tensor("ohc8S", [128, 64], BF16))
        n1w = en(nc.sbuf_tensor("n1wS", [128, 16], F32))
        n2w = en(nc.sbuf_tensor("n2wS", [128, 16], F32))
        fnw = en(nc.sbuf_tensor("fnwS", [128, 8], F32))
        onwb = en(nc.sbuf_tensor("onwbS", [128, 2 * 128], BF16))
        smallB = en(nc.sbuf_tensor("smallBS", [128, 16], F32))
        block = en(nc.Block())

        @block.gpsimd
        def _(gpsimd):
            _emit_phase_b(
                (nc, B, debug, y_h, g2_h, wq_h, wk_h, wv_h, wb_h, wo_h, wg_h,
                 wd_h, n1w_h, n2w_h, onw_h, fnw_h, ident_h, onesrow_h, mU_h,
                 mP_h, oh8_h, ohc8_h, out_h, dbg_tensor, xR, ident, identf,
                 onesrow, mU, mP, oh8, ohc8, n1w, n2w, fnw, onwb, smallB))
    return nc


def _rms_chunk_b(nc, B, xR, ch, dst_fn, wcol, eps_ap, scale_ap, sqx, nrow, nrb,
                 nones, onesrow, pE):
    sqT = {}
    tms = 0
    for kd in range(ND):
        sb_ = kd % 2
        vq = B.V(lambda: nc.vector.tensor_tensor(
            sqx[:, sb_ * 512:(sb_ + 1) * 512],
            xR[:, kd * T + ch * 512: kd * T + (ch + 1) * 512],
            xR[:, kd * T + ch * 512: kd * T + (ch + 1) * 512], ALU.mult),
            waits=[("T", sqT.get(sb_, 0))])
        tms = B.TT(lambda: nc.tensor.matmul(
            pE[0:1, :], nones[:], sqx[:, sb_ * 512:(sb_ + 1) * 512],
            start=(kd == 0), stop=(kd == ND - 1)),
            waits=[("V", vq)], tag=True)
        sqT[sb_] = tms
    sr = B.S(lambda: nc.scalar.activation(
        nrow[0:1, :], pE[0:1, :], AF.Ln, bias=eps_ap, scale=scale_ap),
        waits=[("T", tms)])
    sr = B.S(lambda: nc.scalar.activation(
        nrb[0:1, :], nrow[0:1, :], AF.Exp, scale=-0.5))
    trb = B.TT(lambda: nc.tensor.matmul(
        pE[:], onesrow[:], nrb[0:1, :], start=True, stop=True),
        waits=[("S", sr)])
    hv = 0
    for kd in range(ND):
        hv = B.V(lambda: nc.vector.scalar_tensor_tensor(
            dst_fn(kd),
            xR[:, kd * T + ch * 512: kd * T + (ch + 1) * 512],
            wcol[:, kd:kd + 1], pE[:], ALU.mult, ALU.mult),
            waits=[("T", trb)])
    nc.tensor.wait_ge(B.sem("V"), hv)
    return hv


def _emit_phase_b(E):
    (nc, B, debug, y_h, g2_h, wq_h, wk_h, wv_h, wb_h, wo_h, wg_h,
     wd_h, n1w_h, n2w_h, onw_h, fnw_h, ident_h, onesrow_h, mU_h,
     mP_h, oh8_h, ohc8_h, out_h, dbg_tensor, xR, ident, identf,
     onesrow, mU, mP, oh8, ohc8, n1w, n2w, fnw, onwb, smallB) = E

    for t_, s_ in ((ident, ident_h), (onesrow, onesrow_h), (mU, mU_h),
                   (mP, mP_h), (oh8, oh8_h), (ohc8, ohc8_h)):
        B.dma(nc.gpsimd, "ldc", t_[:], s_)
    for t_, s_ in ((identf, ident_h), (n1w, n1w_h), (n2w, n2w_h), (fnw, fnw_h)):
        B.dma(nc.sync, "ldcf", t_[:], s_)
    B.wait(nc.tensor, "ldc")
    B.wait(nc.tensor, "ldcf")
    B.wait(nc.vector, "ldcf")
    B.V(lambda: nc.vector.memset(smallB[:, 0:1], 1e-5))
    B.V(lambda: nc.vector.memset(smallB[:, 1:2], 1e-6))
    B.V(lambda: nc.vector.memset(smallB[:, 2:3], 9.765625e-4))
    B.V(lambda: nc.vector.memset(smallB[:, 3:4], 7.8125e-3))

    _emit_ln_dt(nc, B, y_h, xR, smallB[:, 0:1], identf, F32)

    with contextlib.ExitStack() as stb:
        eb = stb.enter_context
        qF = eb(nc.sbuf_tensor("qF", [128, 8 * T], BF16))
        kF = eb(nc.sbuf_tensor("kF", [128, 8 * T], BF16))
        vF = eb(nc.sbuf_tensor("vF", [128, 8 * T], BF16))
        ws = eb(nc.sbuf_tensor("wsB", [128, 64 * 128], BF16))
        wbs = eb(nc.sbuf_tensor("wbsB", [128, 8 * 8], BF16))
        hc = eb(nc.sbuf_tensor("hcB", [128, 8 * 512], BF16))
        sqx = eb(nc.sbuf_tensor("sqxB", [128, 2 * 512], BF16))
        bsc = eb(nc.sbuf_tensor("bscB", [8, 512], BF16))
        nrow = eb(nc.sbuf_tensor("nrowB", [1, 512], F32))
        nrb = eb(nc.sbuf_tensor("nrbB", [1, 512], BF16))
        nones = eb(nc.sbuf_tensor("nonesB", [128, 1], BF16))
        betaT = eb(nc.sbuf_tensor("betaT", [128, NT * 8], F32))
        rsq = eb(nc.sbuf_tensor("rsqB", [8, 2 * T], BF16))
        qn = eb(nc.sbuf_tensor("qnB", [128, 128], BF16))
        kn = eb(nc.sbuf_tensor("knB", [128, 128], BF16))
        mbeta = eb(nc.sbuf_tensor("mbetaB", [128, 1024], BF16))
        u0 = hc[:, 0:1024]
        dch = hc[:, 1024:2048]
        t1s = hc[:, 2048:3072]
        aU = hc[:, 3072:4096]
        pmS = mbeta
        t2s = dch
        osb = u0
        orr = eb(nc.sbuf_tensor("orrB", [128, 8], F32))
        orb = eb(nc.sbuf_tensor("orbB", [128, 8], F32))
        xscr = eb(nc.sbuf_tensor("xscrB", [128, 512], F32))
        pA = eb(nc.psum_tensor("pA", [128, 1024], F32))
        pB = eb(nc.psum_tensor("pB", [128, 1024], F32))
        pC = eb(nc.psum_tensor("pC", [128, 1024], F32))
        pD = eb(nc.psum_tensor("pD", [128, 1024], BF16))
        pE = eb(nc.psum_tensor("pE", [128, 512], F32))
        B.V(lambda: nc.vector.memset(nones[:], 1.0))

        onrow = sqx[0:1, 0:256]
        B.dma(nc.gpsimd, "ldc", onrow.rearrange("o (l c) -> o l c", l=2),
              onw_h.rearrange("(o l) c -> o l c", o=1))
        for l in range(2):
            tn = B.TT(lambda: nc.tensor.matmul(
                pE[:, 0:128], onesrow[:], onrow[0:1, l * 128:(l + 1) * 128],
                start=True, stop=True),
                waits=[("ldc", None), ("V", B.n("V"))])
            B.V(lambda: nc.vector.tensor_copy(
                onwb[:, l * 128:(l + 1) * 128], pE[:, 0:128]), waits=[("T", tn)])

        for l in range(2):
            for which, (w_h, dstF) in enumerate(((wq_h, qF), (wk_h, kF), (wv_h, vF))):
                B.dma(nc.gpsimd, "lwB",
                      ws[:].rearrange("p (kk f c) -> p kk f c", kk=8, c=128),
                      w_h[l].rearrange("(kk p) (f c) -> p kk f c", p=128, c=128),
                      waits=[("T", B.n("T")), ("V", B.n("V")), ("S", B.n("S"))])
                B.wait(nc.tensor, "lwB")
                evV = {}
                for ch in range(NCH):
                    _rms_chunk_b(nc, B, xR, ch,
                                 lambda kd: hc[:, kd * 512:(kd + 1) * 512],
                                 n1w[:, l * 8:l * 8 + 8],
                                 smallB[0:1, 1:2], smallB[0:1, 2:3],
                                 sqx, nrow, nrb, nones, onesrow, pE)
                    for fi in range(ND):
                        buf = (ch * ND + fi) % 2
                        nc.tensor.wait_ge(B.sem("S"), evV.get(buf, 0))
                        ps = pA[:, buf * 512:(buf + 1) * 512]
                        for kd in range(ND):
                            tn = B.TT(lambda: nc.tensor.matmul(
                                ps, ws[:, (kd * 8 + fi) * 128:(kd * 8 + fi) * 128 + 128],
                                hc[:, kd * 512:(kd + 1) * 512],
                                start=(kd == 0), stop=(kd == ND - 1)), tag=(kd == ND - 1))
                        evV[buf] = B.S(lambda: nc.scalar.activation(
                            dstF[:, fi * T + ch * 512: fi * T + (ch + 1) * 512],
                            ps, AF.Silu), waits=[("T", tn)])
                nc.tensor.wait_ge(B.sem("S"), B.n("S"))
            B.dma(nc.gpsimd, "lwB", wbs[:].rearrange("p (kk c) -> p kk c", kk=8),
                  wb_h[l].rearrange("(kk p) c -> p kk c", p=128),
                  waits=[("T", B.n("T")), ("S", B.n("S"))])
            B.wait(nc.tensor, "lwB")
            for ch in range(NCH):
                _rms_chunk_b(nc, B, xR, ch,
                             lambda kd: hc[:, kd * 512:(kd + 1) * 512],
                             n1w[:, l * 8:l * 8 + 8],
                             smallB[0:1, 1:2], smallB[0:1, 2:3],
                             sqx, nrow, nrb, nones, onesrow, pE)
                for kd in range(ND):
                    tn = B.TT(lambda: nc.tensor.matmul(
                        pA[0:8, 0:512], wbs[:, kd * 8:(kd + 1) * 8],
                        hc[:, kd * 512:(kd + 1) * 512],
                        start=(kd == 0), stop=(kd == ND - 1)),
                        waits=([("V", B.n("V"))] if kd == 0 else ()),
                        tag=(kd == ND - 1))
                sv = B.S(lambda: nc.scalar.activation(
                    bsc[0:8, :], pA[0:8, 0:512], AF.Sigmoid), waits=[("T", tn)])
                for tb in range(4):
                    tt = ch * 4 + tb
                    tn = B.TT(lambda: nc.tensor.transpose(
                        pD[0:128, 0:8], bsc[0:8, tb * 128:(tb + 1) * 128],
                        ident[0:8, 0:8]),
                        waits=[("S", sv), ("V", B.n("V"))])
                    B.V(lambda: nc.vector.tensor_scalar(
                        betaT[:, tt * 8:(tt + 1) * 8], pD[0:128, 0:8],
                        2.0, None, ALU.mult), waits=[("T", tn)])
            nc.tensor.wait_ge(B.sem("V"), B.n("V"))

            if l == 0 and "bqkv" in debug:
                B.dma(nc.gpsimd, "dbg", dbg_tensor("bq", [128, 8 * T]), qF[:],
                      waits=[("S", None), ("V", None)])
                B.dma(nc.gpsimd, "dbg", dbg_tensor("bk", [128, 8 * T]), kF[:], waits=[("S", None)])
                B.dma(nc.gpsimd, "dbg", dbg_tensor("bv", [128, 8 * T]), vF[:], waits=[("S", None)])
                B.dma(nc.gpsimd, "dbg", dbg_tensor("bbt", [128, NT * 8]), betaT[:],
                      waits=[("V", None)])
            sqg = {}
            for w2, srcF in ((0, qF), (1, kF)):
                for ch in range(NCH):
                    for h in range(8):
                        sb_ = h % 2
                        vq = B.V(lambda: nc.vector.tensor_tensor(
                            sqx[:, sb_ * 512:(sb_ + 1) * 512],
                            srcF[:, h * T + ch * 512: h * T + (ch + 1) * 512],
                            srcF[:, h * T + ch * 512: h * T + (ch + 1) * 512],
                            ALU.mult), waits=[("T", sqg.get(sb_, 0))])
                        tn = B.TT(lambda: nc.tensor.matmul(
                            pE[0:8, :], ohc8[:, h * 8:(h + 1) * 8],
                            sqx[:, sb_ * 512:(sb_ + 1) * 512],
                            start=(h == 0), stop=(h == 7)),
                            waits=[("V", vq)], tag=True)
                        sqg[sb_] = tn
                    sv = B.S(lambda: nc.scalar.activation(
                        bsc[0:8, :], pE[0:8, :], AF.Ln, bias=smallB[0:8, 1:2]),
                        waits=[("T", tn)])
                    sv = B.S(lambda: nc.scalar.activation(
                        rsq[0:8, w2 * T + ch * 512: w2 * T + (ch + 1) * 512],
                        bsc[0:8, :], AF.Exp, scale=-0.5))
                    nc.tensor.wait_ge(B.sem("S"), sv)

            B.dma(nc.gpsimd, "lwB",
                  ws[:].rearrange("p (kk f c) -> p kk f c", kk=8, c=128),
                  wo_h[l].rearrange("(kk p) (f c) -> p kk f c", p=128, c=128),
                  waits=[("T", B.n("T")), ("S", B.n("S"))])
            B.wait(nc.tensor, "lwB")

            def app(src, dst_sbuf, bview, scr, extra=None, op=None):
                for h in range(8):
                    tn = B.TT(lambda: nc.tensor.matmul(
                        pA[:, h * 128:(h + 1) * 128], aU[:, h * 128:(h + 1) * 128],
                        src[:, h * 128:(h + 1) * 128], start=True, stop=True),
                        waits=([("V", B.n("V"))] if h == 0 else ()),
                        tag=(h == 7))
                if op is None:
                    return B.V(lambda: nc.vector.tensor_tensor(
                        dst_sbuf[:].rearrange("p (a c) -> p a c", a=8),
                        pA[:].rearrange("p (a c) -> p a c", a=8), bview, ALU.mult),
                        waits=[("T", tn)])
                B.V(lambda: nc.vector.tensor_tensor(
                    scr[:].rearrange("p (a c) -> p a c", a=8),
                    pA[:].rearrange("p (a c) -> p a c", a=8), bview, ALU.mult),
                    waits=[("T", tn)])
                return B.V(lambda: nc.vector.tensor_tensor(
                    dst_sbuf[:], extra[:], scr[:], op))

            for tt in range(NT):
                tsl = slice(tt * 128, (tt + 1) * 128)
                for h in range(8):
                    tn = B.TT(lambda: nc.tensor.transpose(
                        pD[:, h * 128:(h + 1) * 128],
                        vF[:, h * T + tt * 128: h * T + (tt + 1) * 128], ident[:]),
                        waits=([("V", B.n("V"))] if h == 0 else ()), tag=(h == 7))
                bview = betaT[:, tt * 8:(tt + 1) * 8].unsqueeze(2).broadcast_to([128, 8, 128])
                vu0 = B.V(lambda: nc.vector.tensor_tensor(
                    u0[:].rearrange("p (a c) -> p a c", a=8),
                    pD[:].rearrange("p (a c) -> p a c", a=8), bview, ALU.mult),
                    waits=[("T", tn), ("dbg", None)])
                for h in range(8):
                    tr1 = B.TT(lambda: nc.tensor.matmul(
                        pE[:, 0:128], oh8[:, h * 128:(h + 1) * 128],
                        rsq[0:8, 0 * T + tt * 128: 0 * T + (tt + 1) * 128],
                        start=True, stop=True),
                        waits=[("V", B.n("V"))], tag=True)
                    tr2 = B.TT(lambda: nc.tensor.matmul(
                        pE[:, 128:256], oh8[:, h * 128:(h + 1) * 128],
                        rsq[0:8, 1 * T + tt * 128: 1 * T + (tt + 1) * 128],
                        start=True, stop=True), tag=True)
                    vq1 = B.V(lambda: nc.vector.tensor_tensor(
                        qn[:], qF[:, h * T + tt * 128: h * T + (tt + 1) * 128],
                        pE[:, 0:128], ALU.mult), waits=[("T", tr2)])
                    vq2 = B.V(lambda: nc.vector.tensor_tensor(
                        kn[:], kF[:, h * T + tt * 128: h * T + (tt + 1) * 128],
                        pE[:, 128:256], ALU.mult))
                    tn = B.TT(lambda: nc.tensor.matmul(
                        pA[:, h * 128:(h + 1) * 128], kn[:], kn[:],
                        start=True, stop=True), waits=[("V", vq2)], tag=False)
                    tn = B.TT(lambda: nc.tensor.matmul(
                        pB[:, h * 128:(h + 1) * 128], kn[:], qn[:],
                        start=True, stop=True), tag=(h == 7))
                mview = mU[:].unsqueeze(1).broadcast_to([128, 8, 128])
                vau = B.V(lambda: nc.vector.tensor_tensor(
                    aU[:].rearrange("p (a c) -> p a c", a=8),
                    pA[:].rearrange("p (a c) -> p a c", a=8), mview, ALU.mult),
                    waits=[("T", tn)])
                nc.tensor.wait_ge(B.sem("V"), vau)
                app(u0, dch, bview, mbeta)
                app(dch, t1s, bview, mbeta)
                app(t1s, dch, bview, mbeta)
                app(dch, t1s, bview, mbeta, extra=u0, op=ALU.add)
                app(t1s, dch, bview, mbeta)
                app(dch, dch, bview, mbeta, extra=t1s, op=ALU.add)
                vsol = app(dch, t1s, bview, mbeta, extra=dch, op=ALU.subtract)
                mpview = mP[:].unsqueeze(1).broadcast_to([128, 8, 128])
                vpm = B.V(lambda: nc.vector.tensor_tensor(
                    pmS[:].rearrange("p (a c) -> p a c", a=8),
                    pB[:].rearrange("p (a c) -> p a c", a=8), mpview, ALU.mult))
                nc.tensor.wait_ge(B.sem("V"), vpm)
                if l == 0 and tt == 0 and "btile" in debug:
                    B.dma(nc.gpsimd, "dbg", dbg_tensor("bu0", [128, 1024]), u0[:],
                          waits=[("V", None), ("S", None)])
                    B.dma(nc.gpsimd, "dbg", dbg_tensor("bau", [128, 1024]), aU[:])
                    B.dma(nc.gpsimd, "dbg", dbg_tensor("bpm", [128, 1024]), pmS[:])
                    B.dma(nc.gpsimd, "dbg", dbg_tensor("bU", [128, 1024]), t1s[:])
                    nc.tensor.wait_ge(B.sem("dbg"), B.n("dbg"))
                for h in range(8):
                    tn = B.TT(lambda: nc.tensor.matmul(
                        pB[:, h * 128:(h + 1) * 128], pmS[:, h * 128:(h + 1) * 128],
                        t1s[:, h * 128:(h + 1) * 128], start=True, stop=True),
                        tag=(h == 7))
                vo = B.V(lambda: nc.vector.tensor_copy(osb[:], pB[:]), waits=[("T", tn)])
                vm = B.V(lambda: nc.vector.tensor_tensor(
                    dch[:], osb[:], osb[:], ALU.mult))
                vm = B.V(lambda: nc.vector.tensor_reduce(
                    orr[:, 0:8], dch[:].rearrange("p (a c) -> p a c", a=8),
                    AX.X, ALU.add))
                so = B.S(lambda: nc.scalar.activation(
                    orb[:, 0:8], orr[:, 0:8], AF.Ln, bias=smallB[:, 1:2],
                    scale=smallB[:, 3:4]), waits=[("V", vm)])
                so = B.S(lambda: nc.scalar.activation(
                    orr[:, 0:8], orb[:, 0:8], AF.Exp, scale=-0.5))
                oview = orr[:, 0:8].unsqueeze(2).broadcast_to([128, 8, 128])
                vn1 = B.V(lambda: nc.vector.tensor_tensor(
                    dch[:].rearrange("p (a c) -> p a c", a=8),
                    osb[:].rearrange("p (a c) -> p a c", a=8), oview, ALU.mult),
                    waits=[("S", so)])
                wview = onwb[:, l * 128:(l + 1) * 128].unsqueeze(1).broadcast_to([128, 8, 128])
                vn2 = B.V(lambda: nc.vector.tensor_tensor(
                    t1s[:].rearrange("p (a c) -> p a c", a=8),
                    dch[:].rearrange("p (a c) -> p a c", a=8), wview, ALU.mult))
                nc.tensor.wait_ge(B.sem("V"), vn2)
                for h in range(8):
                    tn = B.TT(lambda: nc.tensor.transpose(
                        pD[:, h * 128:(h + 1) * 128],
                        t1s[:, h * 128:(h + 1) * 128], ident[:]), tag=(h == 7))
                vt = B.V(lambda: nc.vector.tensor_copy(dch[:], pD[:]), waits=[("T", tn)])
                nc.tensor.wait_ge(B.sem("V"), vt)
                for dj in range(ND):
                    for h in range(8):
                        tn = B.TT(lambda: nc.tensor.matmul(
                            pC[:, dj * 128:(dj + 1) * 128],
                            ws[:, (h * 8 + dj) * 128:(h * 8 + dj) * 128 + 128],
                            dch[:, h * 128:(h + 1) * 128],
                            start=(h == 0), stop=(h == 7)), tag=(h == 7 and dj == ND - 1))
                for hf in range(2):
                    xview = xR[:].rearrange("p (a t) -> p a t", a=8)\
                        [:, hf * 4:(hf + 1) * 4, tt * 128:(tt + 1) * 128]
                    vx = B.V(lambda: nc.vector.tensor_tensor(
                        xscr[:].rearrange("p (a c) -> p a c", a=4), xview,
                        pC[:, hf * 512:(hf + 1) * 512].rearrange("p (a c) -> p a c", a=4),
                        ALU.add), waits=([("T", tn)] if hf == 0 else ()))
                    vx = B.V(lambda: nc.vector.tensor_copy(
                        xview, xscr[:].rearrange("p (a c) -> p a c", a=4)))
                nc.tensor.wait_ge(B.sem("V"), vx)

            if l == 0 and "bdelta" in debug:
                B.dma(nc.gpsimd, "dbg", dbg_tensor("brsq", [8, 2 * T]), rsq[:],
                      waits=[("S", None), ("V", None), ("T", None)])
                B.dma(nc.gpsimd, "dbg", dbg_tensor("bx1", [128, 8 * T]), xR[:],
                      waits=[("V", None)])
            _emit_b_glu(nc, B, l, wg_h, wd_h, xR, qF, kF, vF, n2w, smallB,
                        nones, onesrow, sqx, nrow, nrb, xscr, pA, pB, pE)

        _emit_b_final(nc, B, y_h, g2_h, out_h, xR, fnw, smallB, nones, onesrow,
                      sqx, nrow, nrb, hc, ident, pA, pD, pE, xscr,
                      qF[:, 0:8192].bitcast(F32), kF[:, 0:8192].bitcast(F32),
                      vF[:, 0:8192].bitcast(F32))


def _emit_b_glu(nc, B, l, wg_h, wd_h, xR, h2buf, actbuf, wpbuf, n2w, smallB,
                nones, onesrow, sqx, nrow, nrb, xscr, pA, pB, pE):
    nc.vector.wait_ge(B.sem("T"), B.n("T"))
    nc.scalar.wait_ge(B.sem("T"), B.n("T"))
    B.wait(nc.vector, "dbg")
    for ch in range(NCH):
        _rms_chunk_b(nc, B, xR, ch,
                     lambda kd: h2buf[:, kd * T + ch * 512: kd * T + (ch + 1) * 512],
                     n2w[:, l * 8:l * 8 + 8],
                     smallB[0:1, 1:2], smallB[0:1, 2:3],
                     sqx, nrow, nrb, nones, onesrow, pE)
    wg_r = wg_h[l].rearrange("(kk p) c -> p kk c", p=128)
    wd_r = wd_h[l].rearrange("(kk p) (dj c) -> kk p dj c", p=128, c=128)
    for f0, nf in ((0, 6), (6, 6), (12, 5), (17, 5)):
        parG = {}
        upT = {}
        for gi in range(nf):
            ft = f0 + gi
            par = gi % 2
            gslot = wpbuf[:, 8192 + par * 2048: 8192 + par * 2048 + 1024]
            yslot = wpbuf[:, 8192 + par * 2048 + 1024: 8192 + (par + 1) * 2048]
            wdslot = wpbuf[:, gi * 1024:(gi + 1) * 1024]
            gw = [("T", max(parG.get(par, 0), B.n("T") if gi < 2 else 0)),
                  ("V", B.n("V")), ("S", B.n("S"))]
            B.dma(nc.gpsimd, "lgg%d" % par, gslot.rearrange("p (kk c) -> p kk c", kk=8),
                  wg_r[:, :, ft * 128:(ft + 1) * 128], waits=gw)
            B.dma(nc.gpsimd, "lgy%d" % par, yslot.rearrange("p (kk c) -> p kk c", kk=8),
                  wg_r[:, :, 2816 + ft * 128: 2816 + (ft + 1) * 128])
            B.dma(nc.gpsimd, "lgd%d" % par, wdslot.rearrange("p (dj c) -> p dj c", dj=8),
                  wd_r[ft])
            B.wait(nc.tensor, "lgg%d" % par)
            B.wait(nc.tensor, "lgy%d" % par)
            for ch in range(NCH):
                for kd in range(ND):
                    nc.tensor.matmul(
                        pA[:, 0:512], gslot[:, kd * 128:(kd + 1) * 128],
                        h2buf[:, kd * T + ch * 512: kd * T + (ch + 1) * 512],
                        start=(kd == 0), stop=(kd == ND - 1))
                    tn = B.TT(lambda: nc.tensor.matmul(
                        pB[:, 0:512], yslot[:, kd * 128:(kd + 1) * 128],
                        h2buf[:, kd * T + ch * 512: kd * T + (ch + 1) * 512],
                        start=(kd == 0), stop=(kd == ND - 1)), tag=(kd == ND - 1))
                sg = B.S(lambda: nc.scalar.activation(
                    sqx[:, 0:512], pA[:, 0:512], AF.Silu), waits=[("T", tn)])
                va = B.V(lambda: nc.vector.tensor_tensor(
                    actbuf[:, (gi * 4 + ch) * 512:(gi * 4 + ch + 1) * 512],
                    sqx[:, 0:512], pB[:, 0:512], ALU.mult), waits=[("S", sg)])
                nc.tensor.wait_ge(B.sem("S"), sg)
                nc.tensor.wait_ge(B.sem("V"), va)
            parG[par] = B.n("T")
        B.wait(nc.tensor, "lgd0")
        B.wait(nc.tensor, "lgd1")
        pdV = {}
        for ch in range(NCH):
            for dj in range(ND):
                psl = dj % 2
                nc.tensor.wait_ge(B.sem("V"), pdV.get(psl, 0))
                for gi in range(nf):
                    tn = B.TT(lambda: nc.tensor.matmul(
                        pA[:, psl * 512:(psl + 1) * 512],
                        wpbuf[:, gi * 1024 + dj * 128: gi * 1024 + dj * 128 + 128],
                        actbuf[:, (gi * 4 + ch) * 512:(gi * 4 + ch + 1) * 512],
                        start=(gi == 0), stop=(gi == nf - 1)), tag=(gi == nf - 1))
                B.V(lambda: nc.vector.tensor_tensor(
                    xscr[:, 0:512], xR[:, dj * T + ch * 512: dj * T + (ch + 1) * 512],
                    pA[:, psl * 512:(psl + 1) * 512], ALU.add), waits=[("T", tn)])
                pdV[psl] = B.V(lambda: nc.vector.tensor_copy(
                    xR[:, dj * T + ch * 512: dj * T + (ch + 1) * 512], xscr[:, 0:512]))
        nc.tensor.wait_ge(B.sem("V"), B.n("V"))


def _emit_b_final(nc, B, y_h, g2_h, out_h, xR, fnw, smallB, nones, onesrow,
                  sqx, nrow, nrb, hc, ident, pA, pD, pE, xscr, ytl, gtl, otl):
    if True:
        oV = {}
        pDg = 0
        g0 = [("T", None), ("V", None), ("S", None)]
        for ch in range(NCH):
            _rms_chunk_b(nc, B, xR, ch,
                         lambda kd: hc[:, kd * 512:(kd + 1) * 512],
                         fnw[:, 0:8],
                         smallB[0:1, 1:2], smallB[0:1, 2:3],
                         sqx, nrow, nrb, nones, onesrow, pE)
            for tb in range(4):
                tt = ch * 4 + tb
                par = tt % 2
                sy = "ly%d" % par
                B.dma(nc.sync, sy, ytl[:, par * D:(par + 1) * D],
                      y_h.rearrange("(t p) d -> p t d", p=128)[:, tt, :],
                      waits=[("ot%d" % par, None)] + g0)
                B.dma(nc.sync, "lg%d" % par, gtl[:, par * D:(par + 1) * D],
                      g2_h.rearrange("(t p) d -> p t d", p=128)[:, tt, :],
                      waits=[("V", oV.get(par, 0) or 0), ("ot%d" % par, None)] + g0)
                for dj in range(ND):
                    tn = B.TT(lambda: nc.tensor.transpose(
                        pD[:, dj * 128:(dj + 1) * 128],
                        hc[:, dj * 512 + tb * 128: dj * 512 + (tb + 1) * 128],
                        ident[:]),
                        waits=([("V", max(oV.get(par, 0), pDg))] if dj == 0 else ()),
                        tag=(dj == ND - 1))
                B.wait(nc.vector, sy)
                B.wait(nc.vector, "lg%d" % par)
                pDg = B.V(lambda: nc.vector.tensor_tensor(
                    otl[:, par * D:(par + 1) * D].rearrange("p (a c) -> p a c", a=8),
                    gtl[:, par * D:(par + 1) * D].rearrange("p (a c) -> p a c", a=8),
                    pD[:].rearrange("p (a c) -> p a c", a=8), ALU.mult),
                    waits=[("T", tn)])
                oV[par] = B.V(lambda: nc.vector.tensor_add(
                    otl[:, par * D:(par + 1) * D], ytl[:, par * D:(par + 1) * D],
                    otl[:, par * D:(par + 1) * D])) if False else None
                oV[par] = B.V(lambda: nc.vector.tensor_tensor(
                    gtl[:, par * D:(par + 1) * D],
                    ytl[:, par * D:(par + 1) * D],
                    otl[:, par * D:(par + 1) * D], ALU.add))
                B.dma(nc.sync, "ot%d" % par,
                      out_h.rearrange("(t p) d -> p t d", p=128)[:, tt, :],
                      gtl[:, par * D:(par + 1) * D], waits=[("V", oV[par])])
        B.wait(nc.sync, "ot0")
        B.wait(nc.sync, "ot1")



def _pack_col(v):
    return np.ascontiguousarray(v.reshape(8, 128).T)


def _consts_a():
    ident = np.eye(128, dtype=np.float32)
    onesrow = np.ones((1, 128), np.float32)
    m = np.zeros((4, 128, 512), np.float32)
    r = np.arange(128)[:, None]
    c = np.arange(512)[None, :]
    for mm in range(4):
        m[mm] = np.where(128 * mm + r > c, NEG, 0.0)
    ohc = np.zeros((128, 16, 16), np.float32)
    for h in range(16):
        rows = slice(0, 64) if h % 2 == 0 else slice(64, 128)
        ohc[rows, h, h] = 1.0
    oh16 = np.zeros((16, 16, 128), np.float32)
    for h in range(16):
        oh16[h, h, :] = 1.0
    return dict(identity_i=ident, onesrow_i=onesrow, maskL0T=m,
                ohc16_i=np.ascontiguousarray(ohc.reshape(128, 256)),
                oh16_i=np.ascontiguousarray(oh16.reshape(16, 2048)))


def _inputs_a(inp, c):
    t = np.ascontiguousarray
    pair = lambda w: t(w.transpose(0, 2, 1).reshape(8, 2, 1024, 64)
                       .transpose(0, 2, 1, 3).reshape(8, 1024, 128))
    d = {
        "x": t(inp["L0_attn_x"][c]),
        "wq0": pair(inp["L0_attn_W_Q"]),
        "wk0": pair(inp["L0_attn_W_K"]),
        "wv0": pair(inp["L0_attn_W_V"]),
        "wo0t": t(inp["L0_attn_W_O"].T),
        "wr0t": t(inp["L0_ffn_W_r"].T),
        "wk0ft": t(inp["L0_ffn_W_k"].T),
        "wv0ft": t(inp["L0_ffn_W_v"].T),
        "mur": _pack_col(inp["L0_ffn_mu_r"]),
        "muk": _pack_col(inp["L0_ffn_mu_k"]),
    }
    d.update(_consts_a())
    return d




def _consts_a2():
    DH = 64
    pos = np.arange(T, dtype=np.float64) / 32.0
    inv = 1.0 / (150000.0 ** (np.arange(0, DH, 2, dtype=np.float64) / DH))
    fr = inv[:, None] * pos[None, :]
    cos = np.ascontiguousarray(np.tile(np.cos(fr).astype(np.float32), (4, 1)))
    sin = np.ascontiguousarray(np.tile(np.sin(fr).astype(np.float32), (4, 1)))
    r = np.arange(128)[:, None]
    c = np.arange(128)[None, :]
    sub = np.where(c >= r, 0.0, NEG).astype(np.float32)
    diag = np.where(c <= r, 0.0, NEG).astype(np.float32)
    mwin = np.concatenate([sub, diag], axis=1)
    oh32 = np.zeros((32, 32, 128), np.float32)
    for e in range(32):
        oh32[e, e, :] = 1.0
    return dict(cosq_i=cos, sinq_i=sin, maskwin_i=mwin,
                oh32_i=np.ascontiguousarray(oh32.reshape(32, 4096)))


_PERM64 = np.concatenate([np.arange(0, 64, 2), np.arange(1, 64, 2)])


def _inputs_a2(inp, y_c):
    t = np.ascontiguousarray
    qkvw = inp["L1_ffn_qkv_w"].copy()
    qkvb = inp["L1_ffn_qkv_b"].copy()
    for h in range(16):
        qkvw[:, h * 64:(h + 1) * 64] = qkvw[:, h * 64 + _PERM64]
        qkvb[h * 64:(h + 1) * 64] = qkvb[h * 64 + _PERM64]
    for kv in range(4):
        b0 = 1024 + kv * 64
        qkvw[:, b0:b0 + 64] = qkvw[:, b0 + _PERM64]
        qkvb[b0:b0 + 64] = qkvb[b0 + _PERM64]
    d = {
        "y": t(y_c),
        "qkvw_i": t(qkvw),
        "qkvb_i": t(qkvb.reshape(12, 128).T),
        "projw_i": t(inp["L1_ffn_proj_w"]),
        "projb_i": _pack_col(inp["L1_ffn_proj_b"]),
        "sinkr": t(inp["L1_ffn_attn_sink"].reshape(1, 16)),
        "gatew_i": t(inp["L1_ffn_gate_w"]),
        "gateb_i": t(inp["L1_ffn_gate_b"].reshape(32, 1)),
        "wup_i": t(inp["L1_ffn_W_up"]),
        "wdn_i": t(inp["L1_ffn_W_down"]),
        "ffnw_i": _pack_col(inp["L1_ffn_ffn_norm_w"]),
        "attnw_i": _pack_col(inp["L1_ffn_attn_norm_w"]),
    }
    d.update(_consts_a2())
    c1 = _consts_a()
    d["identity_i"] = c1["identity_i"]
    d["onesrow_i"] = c1["onesrow_i"]
    return d



def _consts_b():
    ij = np.arange(128)
    sameg = (ij[:, None] // 8) == (ij[None, :] // 8)
    mU = (sameg & ((ij[:, None] % 8) < (ij[None, :] % 8))).astype(np.float32)
    mP = (sameg & ((ij[:, None] % 8) <= (ij[None, :] % 8))).astype(np.float32)
    oh8 = np.zeros((8, 8, 128), np.float32)
    for h in range(8):
        oh8[h, h, :] = 1.0
    ohc8 = np.zeros((128, 8, 8), np.float32)
    for h in range(8):
        ohc8[:, h, h] = 1.0
    return dict(maskU_i=mU, maskP_i=mP,
                oh8_i=np.ascontiguousarray(oh8.reshape(8, 1024)),
                ohc8_i=np.ascontiguousarray(ohc8.reshape(128, 64)))


def _inputs_b(inp, y_j, g2_j):
    t = np.ascontiguousarray
    d = {
        "yb": t(y_j.astype(np.float32)),
        "g2b": t(g2_j.astype(np.float32)),
        "bWq": t(inp["L1_attn_Wq"]),
        "bWk": t(inp["L1_attn_Wk"]),
        "bWv": t(inp["L1_attn_Wv"]),
        "bWb": t(inp["L1_attn_Wb"]),
        "bWo": t(inp["L1_attn_Wo"]),
        "bWg": t(inp["L1_attn_Wg"]),
        "bWd": t(inp["L1_attn_Wd"]),
        "n1w": t(np.stack([_pack_col(inp["L1_attn_norm1_w"][l]) for l in range(2)], 1)
                 .reshape(128, 16)),
        "n2w": t(np.stack([_pack_col(inp["L1_attn_norm2_w"][l]) for l in range(2)], 1)
                 .reshape(128, 16)),
        "onw": t(inp["L1_attn_onorm_w"]),
        "fnw": _pack_col(inp["L1_attn_final_norm_w"]),
    }
    d.update(_consts_b())
    c1 = _consts_a()
    d["identity_i"] = c1["identity_i"]
    d["onesrow_i"] = c1["onesrow_i"]
    return d

def _dt_rows(a):
    return a.reshape(128, 8, T).transpose(2, 1, 0).reshape(T, D)


def kernel(**inputs):
    inputs = {k: np.asarray(v) for k, v in inputs.items()}
    cores = list(range(8))

    nc_a = build_phase_a()
    res_a = run_bass_kernel_spmd(nc_a, [_inputs_a(inputs, c) for c in cores],
                                 core_ids=cores)
    y = np.stack([np.asarray(r["y"]) for r in res_a.results], 0)

    nc_a2 = build_phase_a2()
    res_a2 = run_bass_kernel_spmd(nc_a2, [_inputs_a2(inputs, y[c]) for c in cores],
                                  core_ids=cores)
    g2 = np.stack([_dt_rows(np.asarray(r["g2"])) for r in res_a2.results], 0)

    def resl(a, j):
        return np.ascontiguousarray(
            a[:, j * 256:(j + 1) * 256, :].transpose(1, 0, 2).reshape(T, D))

    nc_b = build_phase_b()
    res_b = run_bass_kernel_spmd(
        nc_b, [_inputs_b(inputs, resl(y, j), resl(g2, j)) for j in cores],
        core_ids=cores)
    out = np.empty((8, T, D), np.float32)
    for j in cores:
        o = np.asarray(res_b.results[j]["outb"]).reshape(256, 8, D).transpose(1, 0, 2)
        out[:, j * 256:(j + 1) * 256, :] = o
    return out


# version 05 (final)
# speedup vs baseline: 1.2226x; 1.2226x over previous
import contextlib
import numpy as np
import concourse.bass as bass
import concourse.mybir as mybir
from concourse.bass_utils import run_bass_kernel_spmd

F32 = mybir.dt.float32
BF16 = mybir.dt.bfloat16
AF = mybir.ActivationFunctionType
ALU = mybir.AluOpType
AX = mybir.AxisListType

T = 2048
D = 1024
NT = T // 128
ND = D // 128
NCH = T // 512
TP1 = T + 1
NEG = -1e30


class Build:

    def __init__(self, nc):
        self.nc = nc
        self.sems = {}
        self.cnt = {}

    def sem(self, name):
        if name not in self.sems:
            self.sems[name] = self.nc.semaphore(name).__enter__()
            self.cnt[name] = 0
        return self.sems[name]

    def n(self, name):
        self.sem(name)
        return self.cnt[name]

    def inc(self, name, inst, k=1):
        inst.then_inc(self.sem(name), k)
        self.cnt[name] += k
        return self.cnt[name]

    def wait(self, eng, name, val=None):
        self.sem(name)
        v = self.cnt[name] if val is None else val
        if v > 0:
            eng.wait_ge(self.sems[name], v)

    def _chain(self, name, eng, emit, waits):
        self.sem(name)
        for wname, wval in waits:
            self.wait(eng, wname, wval)
        if self.cnt[name] > 0:
            eng.wait_ge(self.sems[name], self.cnt[name])
        return self.inc(name, emit())

    def V(self, emit, waits=()):
        return self._chain("V", self.nc.vector, emit, waits)

    def S(self, emit, waits=()):
        return self._chain("S", self.nc.scalar, emit, waits)

    def G(self, emit, waits=()):
        return self._chain("G", self.nc.gpsimd, emit, waits)

    def TT(self, emit, waits=(), tag=True):
        for wname, wval in waits:
            self.wait(self.nc.tensor, wname, wval)
        inst = emit()
        if tag:
            return self.inc("T", inst)
        return self.n("T")

    def dma(self, queue, name, out, in_, waits=(), **kw):
        for wname, wval in waits:
            self.wait(queue, wname, wval)
        if queue is self.nc.gpsimd:
            self.wait(queue, name)
        inst = queue.dma_start(out=out, in_=in_, **kw)
        return self.inc(name, inst, 16)



def build_phase_a(debug=(), upto="y"):
    nc = bass.Bass("TRN2", target_bir_lowering=False)
    B = Build(nc)
    result = {}

    x_h = nc.dram_tensor("x", [T, D], F32, kind="ExternalInput").ap()
    wq0_h = nc.dram_tensor("wq0", [8, D, 128], F32, kind="ExternalInput").ap()
    wk0_h = nc.dram_tensor("wk0", [8, D, 128], F32, kind="ExternalInput").ap()
    wv0_h = nc.dram_tensor("wv0", [8, D, 128], F32, kind="ExternalInput").ap()
    wo0_h = nc.dram_tensor("wo0t", [D, D], F32, kind="ExternalInput").ap()
    wr0_h = nc.dram_tensor("wr0t", [D, D], F32, kind="ExternalInput").ap()
    wkf_h = nc.dram_tensor("wk0ft", [D, D], F32, kind="ExternalInput").ap()
    wvf_h = nc.dram_tensor("wv0ft", [D, D], F32, kind="ExternalInput").ap()
    mur_h = nc.dram_tensor("mur", [128, 8], F32, kind="ExternalInput").ap()
    muk_h = nc.dram_tensor("muk", [128, 8], F32, kind="ExternalInput").ap()
    ident_h = nc.dram_tensor("identity_i", [128, 128], F32, kind="ExternalInput").ap()
    mL0_h = nc.dram_tensor("maskL0T", [4, 128, 512], F32, kind="ExternalInput").ap()
    ohc16_h = nc.dram_tensor("ohc16_i", [128, 256], F32, kind="ExternalInput").ap()
    oh16_h = nc.dram_tensor("oh16_i", [16, 2048], F32, kind="ExternalInput").ap()
    onesrow_h = nc.dram_tensor("onesrow_i", [1, 128], F32, kind="ExternalInput").ap()

    y_h = nc.dram_tensor("y", [T, D], F32, kind="ExternalOutput").ap()
    zscr_h = nc.dram_tensor("zscr", [16, 64, T], BF16, kind="Internal").ap()

    def dbg_tensor(name, shape):
        return nc.dram_tensor("dbg_" + name, shape, F32, kind="ExternalOutput").ap()

    with contextlib.ExitStack() as _st:
        en = _st.enter_context
        xnDT = en(nc.sbuf_tensor("xnDT", [128, ND * TP1], BF16))
        ident = en(nc.sbuf_tensor("ident", [128, 128], BF16))
        onesrow = en(nc.sbuf_tensor("onesrow", [1, 128], BF16))
        negMb = en(nc.sbuf_tensor("negMb", [128, 16], F32))
        small = en(nc.sbuf_tensor("small", [128, 64], F32))
        smallb = en(nc.sbuf_tensor("smallb", [16, 4], BF16))
        block = en(nc.Block())
        E = (nc, B, debug, upto, x_h, wq0_h, wk0_h, wv0_h, wo0_h, wr0_h, wkf_h,
             wvf_h, mur_h, muk_h, ident_h, mL0_h, ohc16_h, oh16_h, onesrow_h,
             y_h, zscr_h, dbg_tensor, xnDT, ident, onesrow,
             negMb, small, smallb)
        @block.gpsimd
        def _(gpsimd):
            _emit_phase_a(E)
    return nc


def _emit_phase_a(E):
    (nc, B, debug, upto, x_h, wq0_h, wk0_h, wv0_h, wo0_h, wr0_h, wkf_h, wvf_h,
     mur_h, muk_h, ident_h, mL0_h, ohc16_h, oh16_h, onesrow_h, y_h, zscr_h,
     dbg_tensor, xnDT, ident, onesrow, negMb, small, smallb) = E
    if True:
        if True:
            B.dma(nc.gpsimd, "ldc", ident[:], ident_h)
            B.dma(nc.gpsimd, "ldc", onesrow[:], onesrow_h)

            with contextlib.ExitStack() as _st0:
                en0 = _st0.enter_context
                qP = en0(nc.sbuf_tensor("qP", [128, 8 * T], BF16))
                kP = en0(nc.sbuf_tensor("kP", [128, 8 * T], BF16))
                vTN = en0(nc.sbuf_tensor("vTN", [128, NT * 1040], BF16))
                wq0s = en0(nc.sbuf_tensor("wq0s", [128, 64 * 128], BF16))
                wk0s = en0(nc.sbuf_tensor("wk0s", [128, 64 * 128], BF16))
                wv0s = en0(nc.sbuf_tensor("wv0s", [128, 64 * 128], BF16))
                mskL0 = en0(nc.sbuf_tensor("mskL0", [128, 4 * 512], BF16))
                ohc16 = en0(nc.sbuf_tensor("ohc16", [128, 256], BF16))
                oh16 = en0(nc.sbuf_tensor("oh16", [16, 2048], BF16))
                for w_s, w_hh in ((wq0s, wq0_h), (wk0s, wk0_h), (wv0s, wv0_h)):
                    B.dma(nc.gpsimd, "ldw",
                          w_s[:].rearrange("p (a k c) -> p a k c", a=8, k=8),
                          w_hh.rearrange("a (k p) c -> p a k c", p=128))
                B.dma(nc.gpsimd, "ldc", mskL0[:].rearrange("p (m c) -> p m c", m=4),
                      mL0_h.rearrange("m p c -> p m c"))
                B.dma(nc.gpsimd, "ldc", ohc16[:], ohc16_h)
                B.dma(nc.gpsimd, "ldc", oh16[:], oh16_h)

                with contextlib.ExitStack() as _st1:
                    en1 = _st1.enter_context
                    lnscr = en1(nc.sbuf_tensor("lnscr", [128, D], F32))
                    st8 = en1(nc.sbuf_tensor("st8", [128, 8], F32))
                    xn_t = en1(nc.sbuf_tensor("xn_t", [128, 2 * D], BF16))
                    xld = en1(nc.sbuf_tensor("xld", [128, 2 * D], F32))
                    ptr = en1(nc.psum_tensor("ptr", [128, 2 * 1024], BF16))
                    B.wait(nc.tensor, "ldc")
                    B.V(lambda: nc.vector.memset(small[:, 22:23], 1e-5))
                    copyV = {}
                    trT = {}
                    applyS = {}
                    B.dma(nc.sync, "ldxA", xld[:, 0:D],
                          x_h.rearrange("(t p) d -> p t d", p=128)[:, 0, :])
                    for it in range(NT):
                        buf = it % 2
                        if it + 1 < NT:
                            B.dma(nc.sync, "ldxA" if (it + 1) % 2 == 0 else "ldxB",
                                  xld[:, (1 - buf) * D:(2 - buf) * D],
                                  x_h.rearrange("(t p) d -> p t d", p=128)[:, it + 1, :],
                                  waits=[("S", applyS.get(1 - buf, 0))])
                        xt = xld[:, buf * D:(buf + 1) * D]
                        ldsem = "ldxA" if buf == 0 else "ldxB"
                        B.wait(nc.vector, ldsem, 16 * (it // 2 + 1))
                        B.wait(nc.scalar, ldsem, 16 * (it // 2 + 1))
                        B.V(lambda: nc.vector.tensor_tensor(lnscr[:], xt, xt, ALU.mult))
                        B.V(lambda: nc.vector.tensor_reduce(st8[:, 7:8], lnscr[:], AX.X, ALU.add))
                        B.V(lambda: nc.vector.tensor_scalar(st8[:, 1:2], st8[:, 7:8], 1.0 / D, None, ALU.mult))
                        B.V(lambda: nc.vector.tensor_reduce(st8[:, 0:1], xt, AX.X, ALU.add))
                        B.V(lambda: nc.vector.tensor_scalar(st8[:, 2:3], st8[:, 0:1], 1.0 / D, None, ALU.mult))
                        B.V(lambda: nc.vector.tensor_tensor(st8[:, 3:4], st8[:, 2:3], st8[:, 2:3], ALU.mult))
                        vn = B.V(lambda: nc.vector.tensor_tensor(st8[:, 4:5], st8[:, 1:2], st8[:, 3:4], ALU.subtract))
                        B.S(lambda: nc.scalar.activation(st8[:, 5:6], st8[:, 4:5], AF.Ln,
                                                          bias=small[:, 22:23]),
                            waits=[("V", vn)])
                        B.S(lambda: nc.scalar.activation(st8[:, 6:7], st8[:, 5:6], AF.Exp, scale=-0.5))
                        sn = B.n("S")
                        B.V(lambda: nc.vector.tensor_tensor(st8[:, 7:8], st8[:, 2:3], st8[:, 6:7], ALU.mult),
                            waits=[("S", sn)])
                        vn = B.V(lambda: nc.vector.tensor_scalar(st8[:, 3:4], st8[:, 7:8], -1.0, None, ALU.mult))
                        sn = B.S(lambda: nc.scalar.activation(
                            xn_t[:, buf * D:(buf + 1) * D], xt, AF.Identity,
                            bias=st8[:, 3:4], scale=st8[:, 6:7]),
                            waits=[("V", vn), ("T", trT.get(buf, 0))])
                        applyS[buf] = sn
                        nc.tensor.wait_ge(B.sem("S"), sn)
                        if buf in copyV:
                            nc.tensor.wait_ge(B.sem("V"), copyV[buf])
                        for jd in range(ND):
                            tn = B.TT(lambda: nc.tensor.transpose(
                                ptr[:, buf * 1024 + jd * 128: buf * 1024 + (jd + 1) * 128],
                                xn_t[:, buf * D + jd * 128: buf * D + (jd + 1) * 128],
                                ident[:]))
                        trT[buf] = tn
                        for jd in range(ND):
                            vn = B.V(lambda: nc.vector.tensor_copy(
                                xnDT[:, jd * TP1 + 1 + it * 128: jd * TP1 + 1 + (it + 1) * 128],
                                ptr[:, buf * 1024 + jd * 128: buf * 1024 + (jd + 1) * 128]),
                                waits=[("T", tn)])
                        copyV[buf] = vn
                    for jd in range(ND):
                        B.V(lambda: nc.vector.memset(xnDT[:, jd * TP1: jd * TP1 + 1], 0.0))

                if "xn" in debug:
                    B.dma(nc.gpsimd, "dbg", dbg_tensor("xn", [128, ND * TP1]), xnDT[:],
                          waits=[("V", None)])
                if upto == "s1":
                    B.wait(nc.sync, "V")
                    B.wait(nc.sync, "S")
                    return

                with contextlib.ExitStack() as _st2:
                    en2 = _st2.enter_context
                    pmm = en2(nc.psum_tensor("pmm", [128, 2 * 512], F32))
                    ptrv = en2(nc.psum_tensor("ptrv", [128, 2 * 1024], BF16))
                    sqs = en2(nc.sbuf_tensor("sqs", [128, 2 * 512], BF16))
                    vscr = en2(nc.sbuf_tensor("vscr", [128, T], BF16))
                    B.wait(nc.tensor, "ldw")
                    B.wait(nc.tensor, "V")
                    B.wait(nc.tensor, "S")
                    B.V(lambda: nc.vector.memset(vTN[:], 1.0), waits=[("ldc", None)])
                    evV = {}
                    evS = {}
                    for pi in range(8):
                        for which, ws in ((0, wq0s), (1, wk0s), (2, wv0s)):
                            for ch in range(NCH):
                                buf = (pi * 12 + which * 4 + ch) % 2
                                nc.tensor.wait_ge(B.sem("V"), evV.get(buf, 0))
                                ps = pmm[:, buf * 512:(buf + 1) * 512]
                                for kd in range(ND):
                                    tn = B.TT(lambda: nc.tensor.matmul(
                                        ps,
                                        ws[:, (pi * 8 + kd) * 128:(pi * 8 + kd) * 128 + 128],
                                        xnDT[:, kd * TP1 + 1 + ch * 512: kd * TP1 + 1 + (ch + 1) * 512],
                                        start=(kd == 0), stop=(kd == ND - 1)),
                                        tag=(kd == ND - 1))
                                dst = (qP, kP, None)[which]
                                if which == 1:
                                    evV[buf] = B.V(lambda: nc.vector.tensor_scalar(
                                        dst[:, pi * T + ch * 512: pi * T + (ch + 1) * 512],
                                        ps, 0.125, None, ALU.mult), waits=[("T", tn)])
                                elif which == 0:
                                    evV[buf] = B.V(lambda: nc.vector.tensor_copy(
                                        dst[:, pi * T + ch * 512: pi * T + (ch + 1) * 512],
                                        ps), waits=[("T", tn)])
                                else:
                                    evV[buf] = B.V(lambda: nc.vector.tensor_copy(
                                        vscr[:, ch * 512:(ch + 1) * 512],
                                        ps), waits=[("T", tn)])
                        vprev = B.n("V")
                        nc.tensor.wait_ge(B.sem("V"), vprev)
                        tcopyV = {}
                        for tc in range(NT):
                            pbuf = tc % 2
                            nc.tensor.wait_ge(B.sem("V"), tcopyV.get(pbuf, 0))
                            tn = B.TT(lambda: nc.tensor.transpose(
                                ptrv[:, pbuf * 1024: pbuf * 1024 + 128],
                                vscr[:, tc * 128: (tc + 1) * 128],
                                ident[:]))
                            B.V(lambda: nc.vector.tensor_copy(
                                vTN[:, tc * 1040 + (2 * pi) * 65: tc * 1040 + (2 * pi) * 65 + 64],
                                ptrv[:, pbuf * 1024: pbuf * 1024 + 64]),
                                waits=[("T", tn)])
                            tcopyV[pbuf] = B.V(lambda: nc.vector.tensor_copy(
                                vTN[:, tc * 1040 + (2 * pi + 1) * 65: tc * 1040 + (2 * pi + 1) * 65 + 64],
                                ptrv[:, pbuf * 1024 + 64: pbuf * 1024 + 128]))

                    sq_guard = {}
                    for which, src in ((0, qP), (1, kP)):
                        for ch in range(NCH):
                            pbuf = ch % 2
                            ps = pmm[:, pbuf * 512:(pbuf + 1) * 512]
                            nc.tensor.wait_ge(B.sem("V"), sq_guard.get(("ps", pbuf), 0))
                            for pi in range(8):
                                sbuf_i = pi % 2
                                vn = B.V(lambda: nc.vector.tensor_tensor(
                                    sqs[:, sbuf_i * 512:(sbuf_i + 1) * 512],
                                    src[:, pi * T + ch * 512: pi * T + (ch + 1) * 512],
                                    src[:, pi * T + ch * 512: pi * T + (ch + 1) * 512],
                                    ALU.mult),
                                    waits=[("T", sq_guard.get(("sq", sbuf_i), 0))])
                                for hh in range(2):
                                    h = 2 * pi + hh
                                    tn = B.TT(lambda: nc.tensor.matmul(
                                        ps[0:16, :], ohc16[:, h * 16:(h + 1) * 16],
                                        sqs[:, sbuf_i * 512:(sbuf_i + 1) * 512],
                                        start=(pi == 0 and hh == 0),
                                        stop=(pi == 7 and hh == 1)),
                                        waits=([("V", vn)] if hh == 0 else ()),
                                        tag=(hh == 1))
                                sq_guard[("sq", sbuf_i)] = B.n("T")
                            sq_guard[("ps", pbuf)] = B.V(lambda: nc.vector.tensor_reduce(
                                small[0:16, which * 4 + ch: which * 4 + ch + 1],
                                ps[0:16, :], AX.X, ALU.max), waits=[("T", tn)])
                        B.V(lambda: nc.vector.tensor_reduce(
                            small[0:16, 16 + which: 17 + which],
                            small[0:16, which * 4: which * 4 + 4], AX.X, ALU.max))
                    B.V(lambda: nc.vector.tensor_tensor(
                        small[0:16, 18:19], small[0:16, 16:17], small[0:16, 17:18], ALU.mult))
                    vn = B.n("V")
                    sn = B.S(lambda: nc.scalar.activation(small[0:16, 20:21], small[0:16, 18:19], AF.Sqrt),
                             waits=[("V", vn)])
                    vn = B.V(lambda: nc.vector.tensor_scalar(
                        smallb[0:16, 0:1], small[0:16, 20:21], -1.0, None, ALU.mult),
                        waits=[("S", sn)])
                    repV = 0
                    for h in range(16):
                        pbuf = h % 2
                        nc.tensor.wait_ge(B.sem("V"), max(repV, vn))
                        tn = B.TT(lambda: nc.tensor.matmul(
                            pmm[:, pbuf * 512: pbuf * 512 + 1],
                            oh16[:, h * 128:(h + 1) * 128],
                            smallb[0:16, 0:1], start=True, stop=True))
                        repV = B.V(lambda: nc.vector.tensor_copy(
                            negMb[:, h:h + 1], pmm[:, pbuf * 512: pbuf * 512 + 1]),
                            waits=[("T", tn)])
                    nc.tensor.wait_ge(B.sem("V"), B.n("V"))

                if "qkv" in debug:
                    B.dma(nc.gpsimd, "dbg", dbg_tensor("q", [128, 8 * T]), qP[:], waits=[("V", None)])
                    B.dma(nc.gpsimd, "dbg", dbg_tensor("k", [128, 8 * T]), kP[:], waits=[("V", None)])
                    B.dma(nc.gpsimd, "dbg", dbg_tensor("v", [128, NT * 1040]), vTN[:], waits=[("V", None)])
                    B.dma(nc.gpsimd, "dbg", dbg_tensor("negm", [128, 16]), negMb[:], waits=[("V", None)])

                with contextlib.ExitStack() as _st3:
                    en3 = _st3.enter_context
                    psc = en3(nc.psum_tensor("psc", [128, 2 * 512], F32))
                    pscB = en3(nc.psum_tensor("pscB", [128, 2 * 512], F32))
                    pav = en3(nc.psum_tensor("pav", [128, 2 * 512], F32))
                    prb = en3(nc.psum_tensor("prb", [128, 512], F32))
                    etile = en3(nc.sbuf_tensor("etile", [128, 4 * 512], BF16))
                    rrow = en3(nc.sbuf_tensor("rrow", [1, 512], F32))
                    rrowb = en3(nc.sbuf_tensor("rrowb", [1, 512], BF16))
                    rb = en3(nc.sbuf_tensor("rb", [128, 512], F32))
                    zev = en3(nc.sbuf_tensor("zev", [128, 512], BF16))
                    B.wait(nc.tensor, "V")
                    expS = {}
                    avT = {}
                    zread = 0
                    zevG = 0
                    last_tr = 0
                    last_rb = 0
                    for pi in range(8):
                        hA, hB = 2 * pi, 2 * pi + 1
                        for qc in range(NCH):
                            nkc = 4 * (qc + 1)
                            for kc in range(nkc):
                                sb = kc % 2
                                last_mm = {}
                                for half, (base, psall) in ((0, (0, psc)), (1, (64, pscB))):
                                    nc.tensor.wait_ge(B.sem("S"), expS.get((half, sb), 0))
                                    ps = psall[:, sb * 512:(sb + 1) * 512]
                                    diag = kc >= 4 * qc
                                    i1 = nc.tensor.matmul(
                                        ps,
                                        kP[base:base + 64, pi * T + kc * 128: pi * T + (kc + 1) * 128],
                                        qP[base:base + 64, pi * T + qc * 512: pi * T + (qc + 1) * 512],
                                        start=True, stop=not diag,
                                        tile_position=(base, 0))
                                    if diag:
                                        m = kc - 4 * qc
                                        i1 = nc.tensor.matmul(
                                            ps, ident[:],
                                            mskL0[:, m * 512:(m + 1) * 512],
                                            start=False, stop=True)
                                    last_mm[half] = i1
                                tn = B.inc("T", last_mm[1])
                                for half, (psall, hh) in ((0, (psc, hA)), (1, (pscB, hB))):
                                    ps = psall[:, sb * 512:(sb + 1) * 512]
                                    expS[(half, sb)] = B.S(lambda: nc.scalar.activation(
                                        etile[:, (2 * sb + half) * 512:(2 * sb + half + 1) * 512],
                                        ps, AF.Exp, bias=negMb[:, hh:hh + 1]),
                                        waits=[("T", tn)])
                                for half, hh in ((0, hA), (1, hB)):
                                    if kc == 0:
                                        nc.tensor.wait_ge(B.sem("V"), zread)
                                        nc.tensor.wait_ge(B.sem("S"), zevG)
                                    avT[half] = B.TT(lambda: nc.tensor.matmul(
                                        pav[0:65, half * 512:(half + 1) * 512],
                                        vTN[:, kc * 1040 + hh * 65: kc * 1040 + (hh + 1) * 65],
                                        etile[:, (2 * sb + half) * 512:(2 * sb + half + 1) * 512],
                                        start=(kc == 0), stop=(kc == nkc - 1)),
                                        waits=[("S", expS[(half, sb)])],
                                        tag=(kc == nkc - 1))
                            for half, hh in ((0, hA), (1, hB)):
                                pa = pav[0:65, half * 512:(half + 1) * 512]
                                sn = B.S(lambda: nc.scalar.activation(
                                    rrow[0:1, :], pa[64:65, :], AF.Ln),
                                    waits=[("T", avT[half])])
                                zevG = sn
                                sn = B.S(lambda: nc.scalar.activation(
                                    rrowb[0:1, :], rrow[0:1, :], AF.Exp, scale=-1.0),
                                    waits=[("T", last_tr)])
                                tr = B.TT(lambda: nc.tensor.matmul(
                                    prb[:], onesrow[:], rrowb[0:1, :],
                                    start=True, stop=True),
                                    waits=[("S", sn), ("V", last_rb)])
                                last_tr = tr
                                vn = B.V(lambda: nc.vector.tensor_copy(rb[:], prb[:]),
                                         waits=[("T", tr)])
                                last_rb = vn
                                zread = B.V(lambda: nc.vector.tensor_tensor(
                                    zev[0:64, :], pa[0:64, :], rb[0:64, :], ALU.mult),
                                    waits=[("zst", None)])
                                B.dma(nc.gpsimd, "zst",
                                      zscr_h[hh, :, qc * 512:(qc + 1) * 512],
                                      zev[0:64, :], waits=[("V", zread)])

            with contextlib.ExitStack() as _st4:
                en4 = _st4.enter_context
                sDT = en4(nc.sbuf_tensor("sDT", [128, 8 * T], BF16))
                wr0s = en4(nc.sbuf_tensor("wr0s", [128, 64 * 128], BF16))
                wkfs = en4(nc.sbuf_tensor("wkfs", [128, 64 * 128], BF16))
                wvfs = en4(nc.sbuf_tensor("wvfs", [128, 64 * 128], BF16))
                muR = en4(nc.sbuf_tensor("muR", [128, 8], F32))
                muK = en4(nc.sbuf_tensor("muK", [128, 8], F32))
                _st4b = en4(contextlib.ExitStack())
                en4b = _st4b.enter_context
                zP = en4b(nc.sbuf_tensor("zP", [128, 8 * T], BF16))
                wo0s = en4b(nc.sbuf_tensor("wo0s", [128, 64 * 128], BF16))
                pm2 = en4b(nc.psum_tensor("pm2", [128, 2 * 512], F32))
                nT_s3 = B.n("T")
                first = True
                for w_s, w_hh in ((wo0s, wo0_h), (wr0s, wr0_h), (wkfs, wkf_h), (wvfs, wvf_h)):
                    B.dma(nc.gpsimd, "ldw2",
                          w_s[:].rearrange("p (a c) -> p a c", a=8),
                          w_hh.rearrange("(a p) c -> p a c", p=128),
                          waits=([("T", nT_s3)] if first else ()))
                    first = False
                B.dma(nc.sync, "ldw2b", muR[:], mur_h, waits=[("T", nT_s3)])
                B.dma(nc.sync, "ldw2b", muK[:], muk_h)
                B.dma(nc.sync, "zld",
                      zP[:].rearrange("p (a t) -> p a t", a=8),
                      zscr_h.rearrange("(a b) e t -> (b e) a t", b=2),
                      waits=[("zst", None), ("T", nT_s3)])
                if "z" in debug:
                    B.dma(nc.gpsimd, "dbg", dbg_tensor("z", [128, 8 * T]), zP[:],
                          waits=[("zld", None)])
                B.wait(nc.tensor, "ldw2")
                B.wait(nc.tensor, "ldw2b")
                B.wait(nc.tensor, "zld")
                B.wait(nc.tensor, "V")
                B.wait(nc.tensor, "S")
                evV = {}
                for dj in range(ND):
                    for ch in range(NCH):
                        buf = (dj * 4 + ch) % 2
                        nc.tensor.wait_ge(B.sem("V"), evV.get(buf, 0))
                        ps = pm2[:, buf * 512:(buf + 1) * 512]
                        for pi in range(8):
                            tn = B.TT(lambda: nc.tensor.matmul(
                                ps, wo0s[:, (pi * 8 + dj) * 128:(pi * 8 + dj) * 128 + 128],
                                zP[:, pi * T + ch * 512: pi * T + (ch + 1) * 512],
                                start=(pi == 0), stop=(pi == 7)), tag=(pi == 7))
                        evV[buf] = B.V(lambda: nc.vector.tensor_copy(
                            sDT[:, dj * T + ch * 512: dj * T + (ch + 1) * 512], ps),
                            waits=[("T", tn)])

                if "s" in debug:
                    B.dma(nc.gpsimd, "dbg", dbg_tensor("s", [128, 8 * T]), sDT[:],
                          waits=[("V", None)])
                if upto == "s4":
                    B.wait(nc.sync, "V")
                    B.wait(nc.sync, "S")
                    return

                nT_s4 = B.n("T")
                _st4b.close()
                with contextlib.ExitStack() as _st5:
                    en5 = _st5.enter_context
                    tcs = en5(nc.sbuf_tensor("tcs", [128, 8 * 512], BF16))
                    xrc = en5(nc.sbuf_tensor("xrc", [128, 8 * 512], BF16))
                    xkc = en5(nc.sbuf_tensor("xkc", [128, 8 * 512], BF16))
                    rc = en5(nc.sbuf_tensor("rc", [128, 8 * 512], BF16))
                    krel = en5(nc.sbuf_tensor("krel", [128, 8 * 512], BF16))
                    kac = en5(nc.sbuf_tensor("kac", [128, 8 * 512], BF16))
                    gpre = en5(nc.sbuf_tensor("gpre", [128, 8 * 512], BF16))
                    gc = en5(nc.sbuf_tensor("gc", [128, 8 * 512], BF16))
                    ydc = en5(nc.sbuf_tensor("ydc", [128, 8 * 512], BF16))
                    xt2 = en5(nc.sbuf_tensor("xt2", [128, 2 * D], F32))
                    yt2 = en5(nc.sbuf_tensor("yt2", [128, 2 * D], F32))
                    pf = en5(nc.psum_tensor("pf", [128, 2 * 512], F32))
                    ptb = en5(nc.psum_tensor("ptb", [128, 2 * 1024], BF16))
                    for ch in range(NCH):
                        for jd in range(ND):
                            a1 = xnDT[:, jd * TP1 + 1 + ch * 512: jd * TP1 + 1 + (ch + 1) * 512]
                            a0 = xnDT[:, jd * TP1 + 0 + ch * 512: jd * TP1 + 0 + (ch + 1) * 512]
                            B.V(lambda: nc.vector.tensor_tensor(
                                tcs[:, jd * 512:(jd + 1) * 512], a1, a0, ALU.subtract),
                                waits=[("dbg", None)])
                            B.V(lambda: nc.vector.scalar_tensor_tensor(
                                xrc[:, jd * 512:(jd + 1) * 512],
                                tcs[:, jd * 512:(jd + 1) * 512], muR[:, jd:jd + 1], a0,
                                ALU.mult, ALU.add))
                            B.V(lambda: nc.vector.scalar_tensor_tensor(
                                xkc[:, jd * 512:(jd + 1) * 512],
                                tcs[:, jd * 512:(jd + 1) * 512], muK[:, jd:jd + 1], a0,
                                ALU.mult, ALU.add))
                        nc.tensor.wait_ge(B.sem("V"), B.n("V"))
                        for which, (src, ws) in ((0, (xrc, wr0s)), (1, (xkc, wkfs))):
                            for dd in range(ND):
                                buf = dd % 2
                                ps = pf[:, buf * 512:(buf + 1) * 512]
                                nc.tensor.wait_ge(B.sem("S"), B.n("S"))
                                nc.tensor.wait_ge(B.sem("V"), B.n("V"))
                                for jd in range(ND):
                                    tn = B.TT(lambda: nc.tensor.matmul(
                                        ps, ws[:, (jd * 8 + dd) * 128:(jd * 8 + dd) * 128 + 128],
                                        src[:, jd * 512:(jd + 1) * 512],
                                        start=(jd == 0), stop=(jd == 7)), tag=(jd == 7))
                                if which == 0:
                                    B.S(lambda: nc.scalar.activation(
                                        rc[:, dd * 512:(dd + 1) * 512], ps, AF.Sigmoid),
                                        waits=[("T", tn)])
                                else:
                                    vn = B.V(lambda: nc.vector.tensor_relu(
                                        krel[:, dd * 512:(dd + 1) * 512], ps),
                                        waits=[("T", tn)])
                                    B.S(lambda: nc.scalar.activation(
                                        kac[:, dd * 512:(dd + 1) * 512],
                                        krel[:, dd * 512:(dd + 1) * 512], AF.Square),
                                        waits=[("V", vn)])
                        for dd in range(ND):
                            buf = dd % 2
                            ps = pf[:, buf * 512:(buf + 1) * 512]
                            nc.tensor.wait_ge(B.sem("S"), B.n("S"))
                            nc.tensor.wait_ge(B.sem("V"), B.n("V"))
                            for jd in range(ND):
                                tn = B.TT(lambda: nc.tensor.matmul(
                                    ps, wvfs[:, (jd * 8 + dd) * 128:(jd * 8 + dd) * 128 + 128],
                                    kac[:, jd * 512:(jd + 1) * 512],
                                    start=(jd == 0), stop=(jd == 7)), tag=(jd == 7))
                            vn = B.V(lambda: nc.vector.tensor_tensor(
                                gpre[:, dd * 512:(dd + 1) * 512],
                                rc[:, dd * 512:(dd + 1) * 512], ps, ALU.mult),
                                waits=[("T", tn)])
                            sn = B.S(lambda: nc.scalar.activation(
                                gc[:, dd * 512:(dd + 1) * 512],
                                gpre[:, dd * 512:(dd + 1) * 512], AF.Silu),
                                waits=[("V", vn)])
                            B.V(lambda: nc.vector.tensor_tensor(
                                ydc[:, dd * 512:(dd + 1) * 512],
                                sDT[:, dd * T + ch * 512: dd * T + (ch + 1) * 512],
                                gc[:, dd * 512:(dd + 1) * 512], ALU.mult),
                                waits=[("S", sn)])
                        nc.tensor.wait_ge(B.sem("V"), B.n("V"))
                        for tb in range(4):
                            tt = ch * 4 + tb
                            tbuf = tt % 2
                            stysem = "styA" if tbuf == 0 else "styB"
                            ldsem = "ldx2A" if tbuf == 0 else "ldx2B"
                            B.dma(nc.sync, ldsem, xt2[:, tbuf * D:(tbuf + 1) * D],
                                  x_h.rearrange("(t p) d -> p t d", p=128)[:, tt, :],
                                  waits=[(stysem, None)])
                            B.wait(nc.vector, ldsem)
                            for jd in range(ND):
                                pbuf = jd % 2
                                tn = B.TT(lambda: nc.tensor.transpose(
                                    ptb[:, pbuf * 1024: pbuf * 1024 + 128],
                                    ydc[:, jd * 512 + tb * 128: jd * 512 + (tb + 1) * 128],
                                    ident[:]))
                                vn = B.V(lambda: nc.vector.tensor_tensor(
                                    yt2[:, tbuf * D + jd * 128: tbuf * D + (jd + 1) * 128],
                                    xt2[:, tbuf * D + jd * 128: tbuf * D + (jd + 1) * 128],
                                    ptb[:, pbuf * 1024: pbuf * 1024 + 128], ALU.add),
                                    waits=[("T", tn)])
                                nc.tensor.wait_ge(B.sem("V"), vn)
                            B.dma(nc.sync, stysem,
                                  y_h.rearrange("(t p) d -> p t d", p=128)[:, tt, :],
                                  yt2[:, tbuf * D:(tbuf + 1) * D], waits=[("V", None)])
                B.wait(nc.sync, "styA")
                B.wait(nc.sync, "styB")




def build_phase_a2(debug=()):
    nc = bass.Bass("TRN2", target_bir_lowering=False)
    B = Build(nc)

    y_h = nc.dram_tensor("y", [T, D], F32, kind="ExternalInput").ap()
    qkvw_h = nc.dram_tensor("qkvw_i", [D, 1536], F32, kind="ExternalInput").ap()
    qkvb_h = nc.dram_tensor("qkvb_i", [128, 12], F32, kind="ExternalInput").ap()
    projw_h = nc.dram_tensor("projw_i", [D, D], F32, kind="ExternalInput").ap()
    projb_h = nc.dram_tensor("projb_i", [128, 8], F32, kind="ExternalInput").ap()
    sink_h = nc.dram_tensor("sinkr", [1, 16], F32, kind="ExternalInput").ap()
    gatew_h = nc.dram_tensor("gatew_i", [D, 32], F32, kind="ExternalInput").ap()
    gateb_h = nc.dram_tensor("gateb_i", [32, 1], F32, kind="ExternalInput").ap()
    wup_h = nc.dram_tensor("wup_i", [32, D, 2048], F32, kind="ExternalInput").ap()
    wdn_h = nc.dram_tensor("wdn_i", [32, D, D], F32, kind="ExternalInput").ap()
    ffnw_h = nc.dram_tensor("ffnw_i", [128, 8], F32, kind="ExternalInput").ap()
    attnw_h = nc.dram_tensor("attnw_i", [128, 8], F32, kind="ExternalInput").ap()
    cosq_h = nc.dram_tensor("cosq_i", [128, T], F32, kind="ExternalInput").ap()
    sinq_h = nc.dram_tensor("sinq_i", [128, T], F32, kind="ExternalInput").ap()
    ident_h = nc.dram_tensor("identity_i", [128, 128], F32, kind="ExternalInput").ap()
    mwin_h = nc.dram_tensor("maskwin_i", [128, 256], F32, kind="ExternalInput").ap()
    oh32_h = nc.dram_tensor("oh32_i", [32, 32 * 128], F32, kind="ExternalInput").ap()
    onesrow_h = nc.dram_tensor("onesrow_i", [1, 128], F32, kind="ExternalInput").ap()

    g2_h = nc.dram_tensor("g2", [128, 8 * T], F32, kind="ExternalOutput").ap()
    xmid_h = nc.dram_tensor("xmid", [128, 8 * T], F32, kind="Internal").ap()

    def dbg_tensor(name, shape):
        return nc.dram_tensor("dbg_" + name, shape, F32, kind="ExternalOutput").ap()

    with contextlib.ExitStack() as _st:
        en = _st.enter_context
        ident = en(nc.sbuf_tensor("ident", [128, 128], BF16))
        onesrow = en(nc.sbuf_tensor("onesrow", [1, 128], BF16))
        h2 = en(nc.sbuf_tensor("h2", [128, 8 * T], BF16))
        cTb = en(nc.sbuf_tensor("cTb", [32, T], BF16))
        small = en(nc.sbuf_tensor("small", [128, 32], F32))
        ffnw = en(nc.sbuf_tensor("ffnw", [128, 8], F32))
        attnw = en(nc.sbuf_tensor("attnw", [128, 8], F32))
        block = en(nc.Block())

        @block.gpsimd
        def _(gpsimd):
            _emit_phase_a2(
                (nc, B, debug, y_h, qkvw_h, qkvb_h, projw_h, projb_h, sink_h,
                 gatew_h, gateb_h, wup_h, wdn_h, ffnw_h, cosq_h, sinq_h,
                 ident_h, mwin_h, oh32_h, onesrow_h, g2_h,
                 xmid_h, dbg_tensor, ident, onesrow, h2, cTb, small, ffnw,
                 attnw, attnw_h))
    return nc


def _emit_ln_dt(nc, B, src_h, dst, small_eps, ident, dt_out=BF16):
    with contextlib.ExitStack() as st6:
        e6 = st6.enter_context
        lnscr = e6(nc.sbuf_tensor("ln_scr", [128, D], F32))
        st8 = e6(nc.sbuf_tensor("ln_st8", [128, 8], F32))
        xn_t = e6(nc.sbuf_tensor("ln_xnt", [128, 2 * D], dt_out))
        xld = e6(nc.sbuf_tensor("ln_xld", [128, 2 * D], F32))
        ptr = e6(nc.psum_tensor("ln_ptr", [128, 2 * 1024], dt_out))
        copyV = {}
        trT = {}
        applyS = {}
        B.dma(nc.sync, "lnldA", xld[:, 0:D],
              src_h.rearrange("(t p) d -> p t d", p=128)[:, 0, :])
        base_a = B.n("lnldA") - 16
        base_b = B.n("lnldB")
        for it in range(NT):
            buf = it % 2
            if it + 1 < NT:
                B.dma(nc.sync, "lnldA" if (it + 1) % 2 == 0 else "lnldB",
                      xld[:, (1 - buf) * D:(2 - buf) * D],
                      src_h.rearrange("(t p) d -> p t d", p=128)[:, it + 1, :],
                      waits=[("S", applyS.get(1 - buf, 0))])
            xt = xld[:, buf * D:(buf + 1) * D]
            ldsem = "lnldA" if buf == 0 else "lnldB"
            base = base_a if buf == 0 else base_b
            B.wait(nc.vector, ldsem, base + 16 * (it // 2 + 1))
            B.wait(nc.scalar, ldsem, base + 16 * (it // 2 + 1))
            B.V(lambda: nc.vector.tensor_tensor(lnscr[:], xt, xt, ALU.mult))
            B.V(lambda: nc.vector.tensor_reduce(st8[:, 7:8], lnscr[:], AX.X, ALU.add))
            B.V(lambda: nc.vector.tensor_scalar(st8[:, 1:2], st8[:, 7:8], 1.0 / D, None, ALU.mult))
            B.V(lambda: nc.vector.tensor_reduce(st8[:, 0:1], xt, AX.X, ALU.add))
            B.V(lambda: nc.vector.tensor_scalar(st8[:, 2:3], st8[:, 0:1], 1.0 / D, None, ALU.mult))
            B.V(lambda: nc.vector.tensor_tensor(st8[:, 3:4], st8[:, 2:3], st8[:, 2:3], ALU.mult))
            vn = B.V(lambda: nc.vector.tensor_tensor(st8[:, 4:5], st8[:, 1:2], st8[:, 3:4], ALU.subtract))
            B.S(lambda: nc.scalar.activation(st8[:, 5:6], st8[:, 4:5], AF.Ln,
                                             bias=small_eps), waits=[("V", vn)])
            B.S(lambda: nc.scalar.activation(st8[:, 6:7], st8[:, 5:6], AF.Exp, scale=-0.5))
            sn = B.n("S")
            B.V(lambda: nc.vector.tensor_tensor(st8[:, 7:8], st8[:, 2:3], st8[:, 6:7], ALU.mult),
                waits=[("S", sn)])
            vn = B.V(lambda: nc.vector.tensor_scalar(st8[:, 3:4], st8[:, 7:8], -1.0, None, ALU.mult))
            sn = B.S(lambda: nc.scalar.activation(
                xn_t[:, buf * D:(buf + 1) * D], xt, AF.Identity,
                bias=st8[:, 3:4], scale=st8[:, 6:7]),
                waits=[("V", vn), ("T", trT.get(buf, 0))])
            applyS[buf] = sn
            nc.tensor.wait_ge(B.sem("S"), sn)
            if buf in copyV:
                nc.tensor.wait_ge(B.sem("V"), copyV[buf])
            for jd in range(ND):
                tn = B.TT(lambda: nc.tensor.transpose(
                    ptr[:, buf * 1024 + jd * 128: buf * 1024 + (jd + 1) * 128],
                    xn_t[:, buf * D + jd * 128: buf * D + (jd + 1) * 128],
                    ident[:]))
            trT[buf] = tn
            for jd in range(ND):
                vn = B.V(lambda: nc.vector.tensor_copy(
                    dst[:, jd * T + it * 128: jd * T + (it + 1) * 128],
                    ptr[:, buf * 1024 + jd * 128: buf * 1024 + (jd + 1) * 128]),
                    waits=[("T", tn)])
            copyV[buf] = vn


def _emit_phase_a2(E):
    (nc, B, debug, y_h, qkvw_h, qkvb_h, projw_h, projb_h, sink_h,
     gatew_h, gateb_h, wup_h, wdn_h, ffnw_h, cosq_h, sinq_h,
     ident_h, mwin_h, oh32_h, onesrow_h, g2_h,
     xmid_h, dbg_tensor, ident, onesrow, h2, cTb, small, ffnw,
     attnw, attnw_h) = E

    B.dma(nc.gpsimd, "ldc", ident[:], ident_h)
    B.dma(nc.gpsimd, "ldc", onesrow[:], onesrow_h)
    B.dma(nc.sync, "ldcf", ffnw[:], ffnw_h)
    B.dma(nc.sync, "ldcf", attnw[:], attnw_h)
    B.wait(nc.tensor, "ldc")
    B.wait(nc.vector, "ldcf")
    B.V(lambda: nc.vector.memset(small[:, 8:9], 1e-5))

    st69 = contextlib.ExitStack()
    xn2 = st69.enter_context(nc.sbuf_tensor("xn2", [128, 8 * T], BF16))
    _emit_ln_dt(nc, B, y_h, xn2, small[:, 8:9], ident)

    if "xn2" in debug:
        B.dma(nc.gpsimd, "dbg", dbg_tensor("xn2", [128, 8 * T]), xn2[:],
              waits=[("V", None)])

    with contextlib.ExitStack() as st7:
        e7 = st7.enter_context
        qp2 = e7(nc.sbuf_tensor("qp2", [128, 8 * T], BF16))
        kp2 = e7(nc.sbuf_tensor("kp2", [128, 2 * T], BF16))
        vp2 = e7(nc.sbuf_tensor("vp2", [128, 2 * T], BF16))
        vT2 = e7(nc.sbuf_tensor("vT2", [128, NT * 256], BF16))
        z2 = e7(nc.sbuf_tensor("z2", [128, 8 * T], BF16))
        sinkb = e7(nc.sbuf_tensor("sinkb", [128, 16], F32))
        mwin = e7(nc.sbuf_tensor("mwin", [128, 256], BF16))
        pmm = e7(nc.psum_tensor("pmm", [128, 2 * 512], F32))
        ptv = e7(nc.psum_tensor("ptv", [128, 2 * 1024], BF16))
        st7i = e7(contextlib.ExitStack())
        e7i = st7i.enter_context
        qkvs = e7i(nc.sbuf_tensor("qkvs", [128, 8 * 12 * 128], BF16))
        qkvb = e7i(nc.sbuf_tensor("qkvb", [128, 12], F32))
        cosq = e7i(nc.sbuf_tensor("cosq", [128, T], BF16))
        sinq = e7i(nc.sbuf_tensor("sinq", [128, T], BF16))
        ropes = e7i(nc.sbuf_tensor("ropes", [128, 3 * 512], BF16))
        rawc = e7i(nc.sbuf_tensor("rawc", [128, 2 * 512], BF16))
        hatc = e7i(nc.sbuf_tensor("hatc", [128, 8 * 512], BF16))
        sqx = e7i(nc.sbuf_tensor("sqx", [128, 2 * 512], BF16))
        nrow = e7i(nc.sbuf_tensor("nrow", [1, 512], F32))
        nrb = e7i(nc.sbuf_tensor("nrb", [1, 512], BF16))
        nones = e7i(nc.sbuf_tensor("nones", [128, 1], BF16))
        pnm = e7i(nc.psum_tensor("pnm", [128, 512], F32))
        pnr = e7i(nc.psum_tensor("pnr", [128, 512], F32))
        B.dma(nc.gpsimd, "ldw",
              qkvs[:].rearrange("p (k f c) -> p k f c", k=8, f=12),
              qkvw_h.rearrange("(k p) (f c) -> p k f c", p=128, c=128))
        B.dma(nc.gpsimd, "ldw", mwin[:], mwin_h)
        B.dma(nc.sync, "ldcf", qkvb[:], qkvb_h)
        B.dma(nc.gpsimd, "ldc", cosq[:], cosq_h)
        B.dma(nc.gpsimd, "ldc", sinq[:], sinq_h)
        with nc.sbuf_tensor("sinkrow", [1, 16], BF16) as sinkrow:
            B.dma(nc.gpsimd, "ldc", sinkrow[:], sink_h)
            tn = B.TT(lambda: nc.tensor.matmul(
                pmm[:, 0:16], onesrow[:], sinkrow[0:1, :], start=True, stop=True),
                waits=[("ldc", None), ("V", None), ("S", None)])
            B.V(lambda: nc.vector.tensor_copy(sinkb[:], pmm[:, 0:16]), waits=[("T", tn)])

        B.wait(nc.tensor, "ldw")
        B.wait(nc.tensor, "ldcf")
        B.wait(nc.tensor, "V")
        B.V(lambda: nc.vector.memset(nones[:], 1.0))
        B.V(lambda: nc.vector.memset(small[0:1, 10:11], 9.765625e-4))
        evV = {}
        hprevT = {}
        for ch in range(NCH):
            hbuf = 0
            sqT = {}
            for kd in range(ND):
                sb_ = kd % 2
                vq = B.V(lambda: nc.vector.tensor_tensor(
                    sqx[:, sb_ * 512:(sb_ + 1) * 512],
                    xn2[:, kd * T + ch * 512: kd * T + (ch + 1) * 512],
                    xn2[:, kd * T + ch * 512: kd * T + (ch + 1) * 512], ALU.mult),
                    waits=[("T", sqT.get(sb_, 0))])
                tms = B.TT(lambda: nc.tensor.matmul(
                    pnm[0:1, :], nones[:], sqx[:, sb_ * 512:(sb_ + 1) * 512],
                    start=(kd == 0), stop=(kd == ND - 1)),
                    waits=[("V", vq)], tag=True)
                sqT[sb_] = tms
            sr = B.S(lambda: nc.scalar.activation(
                nrow[0:1, :], pnm[0:1, :], AF.Ln,
                bias=small[0:1, 8:9], scale=small[0:1, 10:11]), waits=[("T", tms)])
            sr = B.S(lambda: nc.scalar.activation(
                nrb[0:1, :], nrow[0:1, :], AF.Exp, scale=-0.5))
            trb = B.TT(lambda: nc.tensor.matmul(
                pnr[:], onesrow[:], nrb[0:1, :], start=True, stop=True),
                waits=[("S", sr)])
            for kd in range(ND):
                hv = B.V(lambda: nc.vector.scalar_tensor_tensor(
                    hatc[:, (hbuf * 8 + kd) * 512:(hbuf * 8 + kd + 1) * 512],
                    xn2[:, kd * T + ch * 512: kd * T + (ch + 1) * 512],
                    attnw[:, kd:kd + 1], pnr[:], ALU.mult, ALU.mult),
                    waits=[("T", trb), ("T", hprevT.get(hbuf, 0))])
            nc.tensor.wait_ge(B.sem("V"), hv)
            for fi in range(12):
                isq = fi < 8
                isk = fi in (8, 9)
                buf = (ch * 12 + fi) % 2
                nc.tensor.wait_ge(B.sem("V"), evV.get(buf, 0))
                ps = pmm[:, buf * 512:(buf + 1) * 512]
                for kd in range(ND):
                    tn = B.TT(lambda: nc.tensor.matmul(
                        ps, qkvs[:, (kd * 12 + fi) * 128:(kd * 12 + fi) * 128 + 128],
                        hatc[:, (hbuf * 8 + kd) * 512:(hbuf * 8 + kd + 1) * 512],
                        start=(kd == 0), stop=(kd == ND - 1)), tag=(kd == ND - 1))
                if not (isq or isk):
                    evV[buf] = B.V(lambda: nc.vector.tensor_scalar(
                        vp2[:, (fi - 10) * T + ch * 512:(fi - 10) * T + (ch + 1) * 512],
                        ps, qkvb[:, fi:fi + 1], None, ALU.add), waits=[("T", tn)])
                    continue
                B.V(lambda: nc.vector.tensor_scalar(
                    rawc[:, buf * 512:(buf + 1) * 512], ps, qkvb[:, fi:fi + 1],
                    None, ALU.add), waits=[("T", tn)])
                dst = qp2 if isq else kp2
                dbase = fi * T if isq else (fi - 8) * T
                for b0 in (0, 64):
                    sb0 = slice(b0, b0 + 32)
                    cs = cosq[sb0, ch * 512:(ch + 1) * 512]
                    sn_ = sinq[sb0, ch * 512:(ch + 1) * 512]
                    a1 = rawc[sb0, buf * 512:(buf + 1) * 512]
                    B.V(lambda: nc.vector.tensor_copy(
                        ropes[sb0, 512:1024],
                        rawc[b0 + 32:b0 + 64, buf * 512:(buf + 1) * 512]))
                    B.V(lambda: nc.vector.tensor_tensor(ropes[sb0, 0:512], a1, cs, ALU.mult))
                    B.V(lambda: nc.vector.tensor_tensor(
                        ropes[sb0, 1024:1536], ropes[sb0, 512:1024], sn_, ALU.mult))
                    B.V(lambda: nc.vector.tensor_tensor(
                        dst[sb0, dbase + ch * 512: dbase + (ch + 1) * 512],
                        ropes[sb0, 0:512], ropes[sb0, 1024:1536], ALU.subtract))
                    B.V(lambda: nc.vector.tensor_tensor(ropes[sb0, 0:512], a1, sn_, ALU.mult))
                    B.V(lambda: nc.vector.tensor_tensor(
                        ropes[sb0, 1024:1536], ropes[sb0, 512:1024], cs, ALU.mult))
                    B.V(lambda: nc.vector.tensor_tensor(
                        ropes[sb0, 512:1024],
                        ropes[sb0, 0:512], ropes[sb0, 1024:1536], ALU.add))
                    evV[buf] = B.V(lambda: nc.vector.tensor_copy(
                        dst[b0 + 32:b0 + 64, dbase + ch * 512: dbase + (ch + 1) * 512],
                        ropes[sb0, 512:1024]))
            hprevT[hbuf] = B.n("T")
        tcv = {}
        nc.tensor.wait_ge(B.sem("V"), B.n("V"))
        for pi in range(2):
            for tc in range(NT):
                pbuf = (pi * NT + tc) % 2
                nc.tensor.wait_ge(B.sem("V"), tcv.get(pbuf, 0))
                tn = B.TT(lambda: nc.tensor.transpose(
                    ptv[:, pbuf * 1024: pbuf * 1024 + 128],
                    vp2[:, pi * T + tc * 128: pi * T + (tc + 1) * 128],
                    ident[:]))
                tcv[pbuf] = B.V(lambda: nc.vector.tensor_copy(
                    vT2[:, tc * 256 + pi * 128: tc * 256 + (pi + 1) * 128],
                    ptv[:, pbuf * 1024: pbuf * 1024 + 128]), waits=[("T", tn)])

        if "qkv2" in debug:
            B.dma(nc.gpsimd, "dbg", dbg_tensor("q2", [128, 8 * T]), qp2[:], waits=[("V", None)])
            B.dma(nc.gpsimd, "dbg", dbg_tensor("k2", [128, 2 * T]), kp2[:], waits=[("V", None)])
            B.dma(nc.gpsimd, "dbg", dbg_tensor("v2", [128, 2 * T]), vp2[:], waits=[("V", None)])
        nc.tensor.wait_ge(B.sem("V"), B.n("V"))
        st7i.close()

        with contextlib.ExitStack() as st8b:
            e8 = st8b.enter_context
            kp2d = e8(nc.sbuf_tensor("kp2d", [128, 2 * T], BF16))
            psc = e8(nc.psum_tensor("psc", [128, 2 * 512], F32))
            pav = e8(nc.psum_tensor("pav", [128, 512], F32))
            esc = e8(nc.sbuf_tensor("esc", [128, 2 * 256], BF16))
            attb = e8(nc.sbuf_tensor("attb", [128, 256], BF16))
            attT = e8(nc.sbuf_tensor("attT", [128, 256], BF16))
            arow = e8(nc.sbuf_tensor("arow", [128, 8], F32))
            for tile_ in range(2):
                for p_ in range(2):
                    B.V(lambda: nc.vector.tensor_copy(
                        kp2d[64 * (1 - p_):64 * (1 - p_) + 64, tile_ * T:(tile_ + 1) * T],
                        kp2[64 * p_:64 * p_ + 64, tile_ * T:(tile_ + 1) * T]))
            nc.tensor.wait_ge(B.sem("V"), B.n("V"))
            expS = {}
            pscV = {}
            escV = {}
            attb_gT = 0
            attT_gT = 0
            zevV = 0
            for h in range(16):
                kvh = h // 4
                ktile, kbase = kvh // 2, 64 * (kvh % 2)
                qtile, qbase = h // 2, 64 * (h % 2)
                for qt in range(NT):
                    sb = qt % 2
                    wid = 256 if qt > 0 else 128
                    k0 = (qt - 1) * 128 if qt > 0 else 0
                    nc.tensor.wait_ge(B.sem("S"), expS.get(sb, 0))
                    nc.tensor.wait_ge(B.sem("V"), pscV.get(sb, 0))
                    ps = psc[:, sb * 512: sb * 512 + wid]
                    ksrc = kp2 if kbase == qbase else kp2d
                    nc.tensor.matmul(
                        ps, qp2[qbase:qbase + 64, qtile * T + qt * 128: qtile * T + (qt + 1) * 128],
                        ksrc[qbase:qbase + 64, ktile * T + k0: ktile * T + k0 + wid],
                        start=True, stop=False, tile_position=(qbase, 0))
                    i1 = nc.tensor.matmul(
                        ps, ident[:], mwin[:, 0:wid] if qt > 0 else mwin[:, 128:256],
                        start=False, stop=True)
                    tn = B.inc("T", i1)
                    vm = B.V(lambda: nc.vector.tensor_reduce(
                        arow[:, 0:1], ps, AX.X, ALU.max), waits=[("T", tn)])
                    vm = B.V(lambda: nc.vector.tensor_scalar(
                        arow[:, 1:2], arow[:, 0:1], -0.125, None, ALU.mult))
                    pscV[sb] = vm
                    se = B.S(lambda: nc.scalar.activation(
                        esc[:, sb * 256: sb * 256 + wid], ps, AF.Exp,
                        bias=arow[:, 1:2], scale=0.125),
                        waits=[("V", max(vm, escV.get(sb, 0)))])
                    expS[sb] = se
                    B.V(lambda: nc.vector.tensor_reduce(
                        arow[:, 2:3], esc[:, sb * 256: sb * 256 + wid], AX.X, ALU.add),
                        waits=[("S", se)])
                    ss = B.S(lambda: nc.scalar.activation(
                        arow[:, 3:4], arow[:, 1:2], AF.Exp,
                        bias=sinkb[:, h:h + 1]))
                    B.V(lambda: nc.vector.tensor_tensor(
                        arow[:, 4:5], arow[:, 2:3], arow[:, 3:4], ALU.add),
                        waits=[("S", ss)])
                    B.V(lambda: nc.vector.reciprocal(arow[:, 5:6], arow[:, 4:5]))
                    va = B.V(lambda: nc.vector.tensor_scalar(
                        attb[:, 0:wid], esc[:, sb * 256: sb * 256 + wid],
                        arow[:, 5:6], None, ALU.mult), waits=[("T", attb_gT)])
                    escV[sb] = va
                    nc.tensor.wait_ge(B.sem("V"), va)
                    nblk = wid // 128
                    for kb in range(nblk):
                        tn2 = B.TT(lambda: nc.tensor.transpose(
                            ptv[:, (kb % 2) * 1024 + 512:(kb % 2) * 1024 + 640],
                            attb[:, kb * 128:(kb + 1) * 128], ident[:]),
                            waits=[("V", attT_gT)])
                        vv = B.V(lambda: nc.vector.tensor_copy(
                            attT[:, kb * 128:(kb + 1) * 128],
                            ptv[:, (kb % 2) * 1024 + 512:(kb % 2) * 1024 + 640]),
                            waits=[("T", tn2)])
                    attb_gT = tn2
                    nc.tensor.wait_ge(B.sem("V"), vv)
                    for kb in range(nblk):
                        kc = qt - 1 + kb if qt > 0 else 0
                        tn3 = B.TT(lambda: nc.tensor.matmul(
                            pav[0:64, 0:128],
                            vT2[:, kc * 256 + kvh * 64: kc * 256 + (kvh + 1) * 64],
                            attT[:, kb * 128:(kb + 1) * 128],
                            start=(kb == 0), stop=(kb == nblk - 1)),
                            waits=([("V", zevV)] if kb == 0 else ()),
                            tag=(kb == nblk - 1))
                    attT_gT = tn3
                    zevV = B.V(lambda: nc.vector.tensor_copy(
                        z2[qbase:qbase + 64, qtile * T + qt * 128: qtile * T + (qt + 1) * 128],
                        pav[0:64, 0:128]), waits=[("T", tn3)])

        if "z2" in debug:
            B.dma(nc.gpsimd, "dbg", dbg_tensor("z2d", [128, 8 * T]), z2[:], waits=[("V", None)])

        with contextlib.ExitStack() as st9:
            e9 = st9.enter_context
            projs = e9(nc.sbuf_tensor("projs", [128, 64 * 128], BF16))
            projb = e9(nc.sbuf_tensor("projb", [128, 8], F32))
            B.dma(nc.gpsimd, "ldw2",
                  projs[:].rearrange("p (k f c) -> p k f c", k=8, f=8),
                  projw_h.rearrange("(k p) (f c) -> p k f c", p=128, c=128),
                  waits=[("T", B.n("T")), ("S", B.n("S")), ("V", B.n("V"))])
            B.dma(nc.sync, "ldcf", projb[:], projb_h,
                  waits=[("T", B.n("T")), ("S", B.n("S")), ("V", B.n("V"))])
            B.wait(nc.tensor, "ldw2")
            B.wait(nc.tensor, "ldcf")
            xmc = e9(nc.sbuf_tensor("xmc", [128, 8 * 512], F32))
            sqc = e9(nc.sbuf_tensor("sqc", [128, 8 * 512], BF16))
            rrow = e9(nc.sbuf_tensor("rrow", [1, 512], F32))
            rrb = e9(nc.sbuf_tensor("rrb", [1, 512], BF16))
            onescol = e9(nc.sbuf_tensor("onescol", [128, 1], BF16))
            pms = e9(nc.psum_tensor("pms", [128, 512], F32))
            prb = e9(nc.psum_tensor("prb", [128, 512], F32))
            B.V(lambda: nc.vector.memset(onescol[:], 1.0))
            B.V(lambda: nc.vector.memset(small[0:1, 9:10], 9.765625e-4))
            prev_dma = 0
            prev_h2 = 0
            for ch in range(NCH):
                for dj in range(ND):
                    buf = dj % 2
                    ps = pmm[:, buf * 512:(buf + 1) * 512]
                    nc.tensor.wait_ge(B.sem("V"), B.n("V"))
                    for fk in range(ND):
                        tn = B.TT(lambda: nc.tensor.matmul(
                            ps, projs[:, (fk * 8 + dj) * 128:(fk * 8 + dj) * 128 + 128],
                            z2[:, fk * T + ch * 512: fk * T + (ch + 1) * 512],
                            start=(fk == 0), stop=(fk == ND - 1)), tag=(fk == ND - 1))
                    B.V(lambda: nc.vector.scalar_tensor_tensor(
                        xmc[:, dj * 512:(dj + 1) * 512], ps, projb[:, dj:dj + 1],
                        xn2[:, dj * T + ch * 512: dj * T + (ch + 1) * 512],
                        ALU.add, ALU.add), waits=[("T", tn), ("xms", prev_dma)])
                    vv = B.V(lambda: nc.vector.tensor_tensor(
                        sqc[:, dj * 512:(dj + 1) * 512],
                        xmc[:, dj * 512:(dj + 1) * 512],
                        xmc[:, dj * 512:(dj + 1) * 512], ALU.mult))
                prev_dma = B.dma(nc.sync, "xms",
                                 xmid_h[:].rearrange("p (a c) -> p a c", a=8)[:, :, ch * 512:(ch + 1) * 512],
                                 xmc[:].rearrange("p (a c) -> p a c", a=8),
                                 waits=[("V", vv)])
                nc.tensor.wait_ge(B.sem("V"), vv)
                for kd in range(ND):
                    tms = B.TT(lambda: nc.tensor.matmul(
                        pms[0:1, :], onescol[:],
                        sqc[:, kd * 512:(kd + 1) * 512],
                        start=(kd == 0), stop=(kd == ND - 1)), tag=(kd == ND - 1))
                sr = B.S(lambda: nc.scalar.activation(
                    rrow[0:1, :], pms[0:1, :], AF.Ln,
                    bias=small[0:1, 8:9], scale=small[0:1, 9:10]), waits=[("T", tms)])
                sr = B.S(lambda: nc.scalar.activation(
                    rrb[0:1, :], rrow[0:1, :], AF.Exp, scale=-0.5))
                trb = B.TT(lambda: nc.tensor.matmul(
                    prb[:], onesrow[:], rrb[0:1, :], start=True, stop=True),
                    waits=[("S", sr)])
                for dj in range(ND):
                    prev_h2 = B.V(lambda: nc.vector.scalar_tensor_tensor(
                        h2[:, dj * T + ch * 512: dj * T + (ch + 1) * 512],
                        xmc[:, dj * 512:(dj + 1) * 512], ffnw[:, dj:dj + 1],
                        prb[:], ALU.mult, ALU.mult), waits=[("T", trb)])
                nc.tensor.wait_ge(B.sem("V"), prev_h2)
                nc.tensor.wait_ge(B.sem("S"), B.n("S"))

    st69.close()
    if "h2" in debug:
        B.dma(nc.gpsimd, "dbg", dbg_tensor("h2d", [128, 8 * T]), h2[:], waits=[("V", None)])

    with contextlib.ExitStack() as s11:
        e11 = s11.enter_context
        gatews = e11(nc.sbuf_tensor("gatews", [128, 8 * 32], BF16))
        gateb = e11(nc.sbuf_tensor("gateb", [32, 1], F32))
        identf = e11(nc.sbuf_tensor("identf", [128, 128], F32))
        glog = e11(nc.sbuf_tensor("glog", [32, T], F32))
        glT = e11(nc.sbuf_tensor("glT", [128, NT * 32], F32))
        gw1 = e11(nc.sbuf_tensor("gw1", [128, 4 * 32], F32))
        gcol = e11(nc.sbuf_tensor("gcol", [128, 8], F32))
        pgl = e11(nc.psum_tensor("pgl", [128, 2 * 512], F32))
        gw_ = [("T", B.n("T")), ("V", B.n("V")), ("S", B.n("S")), ("dbg", B.n("dbg"))]
        B.dma(nc.gpsimd, "ldg", gatews[:].rearrange("p (k c) -> p k c", k=8),
              gatew_h.rearrange("(k p) c -> p k c", p=128), waits=gw_)
        B.dma(nc.sync, "ldgf", gateb[:], gateb_h, waits=gw_)
        B.dma(nc.sync, "ldgf", identf[:], ident_h)
        B.wait(nc.tensor, "ldg")
        B.wait(nc.tensor, "ldgf")
        evG = {}
        for ch in range(NCH):
            buf = ch % 2
            ps = pgl[0:32, buf * 512:(buf + 1) * 512]
            nc.tensor.wait_ge(B.sem("V"), evG.get(buf, 0))
            for kd in range(ND):
                tn = B.TT(lambda: nc.tensor.matmul(
                    ps, gatews[:, kd * 32:(kd + 1) * 32],
                    h2[:, kd * T + ch * 512: kd * T + (ch + 1) * 512],
                    start=(kd == 0), stop=(kd == ND - 1)), tag=(kd == ND - 1))
            evG[buf] = B.V(lambda: nc.vector.tensor_scalar(
                glog[:, ch * 512:(ch + 1) * 512], ps, gateb[:, 0:1], None, ALU.add),
                waits=[("T", tn)])
        trG = {}
        for tt in range(NT):
            pb = tt % 2
            nc.tensor.wait_ge(B.sem("V"), B.n("V"))
            tn = B.TT(lambda: nc.tensor.transpose(
                pgl[:, pb * 512: pb * 512 + 32].bitcast(F32) if False else pgl[:, pb * 512: pb * 512 + 32],
                glog[:, tt * 128:(tt + 1) * 128], identf[0:32, 0:32]),
                waits=[("V", trG.get(pb, 0))])
            vv = B.V(lambda: nc.vector.tensor_copy(
                glT[:, tt * 32:(tt + 1) * 32], pgl[:, pb * 512: pb * 512 + 32]),
                waits=[("T", tn)])
            trG[pb] = vv
        for tt in range(NT):
            g0 = glT[:, tt * 32:(tt + 1) * 32]
            B.V(lambda: nc.vector.tensor_reduce(gcol[:, 0:1], g0, AX.X, ALU.max))
            cur = g0
            for rnd in range(3):
                nxt = gw1[:, ((rnd + 1) % 2) * 32:((rnd + 1) % 2) * 32 + 32]
                B.V(lambda: nc.vector.tensor_scalar(
                    gw1[:, 64:96], cur, gcol[:, rnd:rnd + 1], None, ALU.is_ge))
                B.V(lambda: nc.vector.tensor_scalar(
                    gw1[:, 96:128], gw1[:, 64:96], 1e30, None, ALU.mult))
                B.V(lambda: nc.vector.tensor_tensor(nxt, cur, gw1[:, 96:128], ALU.subtract))
                B.V(lambda: nc.vector.tensor_reduce(
                    gcol[:, rnd + 1:rnd + 2], nxt, AX.X, ALU.max))
                cur = nxt
            B.V(lambda: nc.vector.tensor_scalar(
                gw1[:, 64:96], g0, gcol[:, 3:4], None, ALU.is_ge))
            vv = B.V(lambda: nc.vector.tensor_scalar(
                gcol[:, 4:5], gcol[:, 0:1], -1.0, None, ALU.mult))
            sv = B.S(lambda: nc.scalar.activation(
                gw1[:, 96:128], g0, AF.Exp, bias=gcol[:, 4:5]), waits=[("V", vv)])
            B.V(lambda: nc.vector.tensor_tensor(
                gw1[:, 0:32], gw1[:, 64:96], gw1[:, 96:128], ALU.mult),
                waits=[("S", sv)])
            B.V(lambda: nc.vector.tensor_reduce(gcol[:, 5:6], gw1[:, 0:32], AX.X, ALU.add))
            B.V(lambda: nc.vector.reciprocal(gcol[:, 6:7], gcol[:, 5:6]))
            vv = B.V(lambda: nc.vector.tensor_scalar(
                gw1[:, 32:64].bitcast(BF16)[:, 0:32] if False else gw1[:, 32:64],
                gw1[:, 0:32], gcol[:, 6:7], None, ALU.mult))
            pb = tt % 2
            nc.tensor.wait_ge(B.sem("V"), vv)
            tn = B.TT(lambda: nc.tensor.transpose(
                pgl[0:32, pb * 512 + 64: pb * 512 + 64 + 128],
                gw1[:, 32:64], identf[:]))
            vv = B.V(lambda: nc.vector.tensor_copy(
                cTb[0:32, tt * 128:(tt + 1) * 128], pgl[0:32, pb * 512 + 64: pb * 512 + 64 + 128]),
                waits=[("T", tn)])
            trG[pb] = vv

    if "comb" in debug:
        B.dma(nc.gpsimd, "dbg", dbg_tensor("comb", [32, T]), cTb[:], waits=[("V", None)])

    with contextlib.ExitStack() as s12:
        e12 = s12.enter_context
        y2a = e12(nc.sbuf_tensor("y2a", [128, 8 * 1024], F32))
        y2s = e12(nc.sbuf_tensor("y2s", [128, 512], F32))
        acts = e12(nc.sbuf_tensor("acts", [128, 16 * 512], BF16))
        ascr = e12(nc.sbuf_tensor("ascr", [128, 2 * 512], BF16))
        wupP = e12(nc.sbuf_tensor("wupP", [128, 2 * 8 * 1024], BF16))
        wdnP = e12(nc.sbuf_tensor("wdnP", [128, 2 * 8 * 1024], BF16))
        cbs = e12(nc.sbuf_tensor("cbs", [128, 2 * 512], BF16))
        oh32 = e12(nc.sbuf_tensor("oh32", [32, 32 * 128], BF16))
        xml = e12(nc.sbuf_tensor("xml", [128, 8 * 512], F32))
        g2c = e12(nc.sbuf_tensor("g2c", [128, 8 * 512], F32))
        pup = e12(nc.psum_tensor("pup", [128, 4 * 512], F32))
        pdn = e12(nc.psum_tensor("pdn", [128, 2 * 512], F32))
        pcb = e12(nc.psum_tensor("pcb", [128, 512], F32))
        B.dma(nc.gpsimd, "ldg", oh32[:], oh32_h,
              waits=[("T", B.n("T")), ("V", B.n("V")), ("S", B.n("S")),
                     ("dbg", B.n("dbg"))])
        B.wait(nc.tensor, "ldg")
        nc.tensor.wait_ge(B.sem("V"), B.n("V"))
        nc.tensor.wait_ge(B.sem("S"), B.n("S"))
        T0 = B.n("T")
        lastT = {0: None, 1: None}
        upS = {}
        pcbV = B.n("V")
        ascrV = B.n("V")
        wup_r = wup_h.rearrange("e (k p) c -> e p k c", p=128)
        wdn_r = wdn_h.rearrange("e (k p) c -> e p k c", p=128)
        for th in range(2):
            tb0 = th * 2
            B.V(lambda: nc.vector.memset(y2a[:], 0.0))
            for e in range(32):
                ebuf = e % 2
                gw = [("T", T0 if lastT[ebuf] is None else lastT[ebuf])]
                B.dma(nc.gpsimd, "lwu%d" % ebuf,
                      wupP[:, ebuf * 8192: ebuf * 8192 + 4096].rearrange("p (k c) -> p k c", k=8),
                      wup_r[e][:, :, 0:512], waits=gw)
                B.dma(nc.gpsimd, "lwg%d" % ebuf,
                      wupP[:, ebuf * 8192 + 4096:(ebuf + 1) * 8192].rearrange("p (k c) -> p k c", k=8),
                      wup_r[e][:, :, 1024:1536], waits=gw)
                B.dma(nc.gpsimd, "lwd%d" % ebuf,
                      wdnP[:, ebuf * 8192:(ebuf + 1) * 8192].rearrange("p (k c) -> p k c", k=8),
                      wdn_r[e], waits=gw)
                for fp in range(2):
                    if fp == 1:
                        guardT = B.n("T")
                        B.dma(nc.gpsimd, "lwu%d" % ebuf,
                              wupP[:, ebuf * 8192: ebuf * 8192 + 4096].rearrange("p (k c) -> p k c", k=8),
                              wup_r[e][:, :, 512:1024], waits=[("T", guardT)])
                        B.dma(nc.gpsimd, "lwg%d" % ebuf,
                              wupP[:, ebuf * 8192 + 4096:(ebuf + 1) * 8192].rearrange("p (k c) -> p k c", k=8),
                              wup_r[e][:, :, 1536:2048], waits=[("T", guardT)])
                    B.wait(nc.tensor, "lwu%d" % ebuf)
                    B.wait(nc.tensor, "lwg%d" % ebuf)
                    for ch2 in range(2):
                        ch = tb0 + ch2
                        if fp == 0:
                            tn = B.TT(lambda: nc.tensor.matmul(
                                pcb[:], oh32[:, e * 128:(e + 1) * 128],
                                cTb[0:32, ch * 512:(ch + 1) * 512], start=True, stop=True),
                                waits=[("V", pcbV)])
                            pcbV = B.V(lambda: nc.vector.tensor_copy(
                                cbs[:, ch2 * 512:(ch2 + 1) * 512], pcb[:]), waits=[("T", tn)])
                        for i in range(4):
                            ft = fp * 4 + i
                            psA = pup[:, (i % 2) * 512:(i % 2) * 512 + 512]
                            psB = pup[:, 1024 + (i % 2) * 512: 1024 + (i % 2) * 512 + 512]
                            ws = upS.get(i % 2, 0)
                            if ws:
                                nc.tensor.wait_ge(B.sem("S"), ws)
                            for kd in range(ND):
                                nc.tensor.matmul(
                                    psA, wupP[:, ebuf * 8192 + kd * 512 + i * 128:
                                              ebuf * 8192 + kd * 512 + i * 128 + 128],
                                    h2[:, kd * T + ch * 512: kd * T + (ch + 1) * 512],
                                    start=(kd == 0), stop=(kd == ND - 1))
                                tn = B.TT(lambda: nc.tensor.matmul(
                                    psB, wupP[:, ebuf * 8192 + 4096 + kd * 512 + i * 128:
                                              ebuf * 8192 + 4096 + kd * 512 + i * 128 + 128],
                                    h2[:, kd * T + ch * 512: kd * T + (ch + 1) * 512],
                                    start=(kd == 0), stop=(kd == ND - 1)), tag=(kd == ND - 1))
                            sg = B.S(lambda: nc.scalar.activation(
                                ascr[:, 0:512], psB, AF.Silu),
                                waits=[("T", tn), ("V", ascrV)])
                            uc = B.S(lambda: nc.scalar.copy(ascr[:, 512:1024], psA))
                            upS[i % 2] = uc
                            ascrV = B.V(lambda: nc.vector.tensor_tensor(
                                acts[:, (ch2 * 8 + ft) * 512:(ch2 * 8 + ft + 1) * 512],
                                ascr[:, 0:512], ascr[:, 512:1024],
                                ALU.mult), waits=[("S", uc)])
                B.wait(nc.tensor, "lwd%d" % ebuf)
                pdnV = {}
                for ch2 in range(2):
                    for ft in range(8):
                        B.V(lambda: nc.vector.tensor_tensor(
                            ascr[:, 0:512],
                            acts[:, (ch2 * 8 + ft) * 512:(ch2 * 8 + ft + 1) * 512],
                            cbs[:, ch2 * 512:(ch2 + 1) * 512], ALU.mult))
                        vv = B.V(lambda: nc.vector.tensor_copy(
                            acts[:, (ch2 * 8 + ft) * 512:(ch2 * 8 + ft + 1) * 512],
                            ascr[:, 0:512]))
                        ascrV = vv
                    nc.tensor.wait_ge(B.sem("V"), B.n("V"))
                    for dj in range(ND):
                        dsl = dj % 2
                        nc.tensor.wait_ge(B.sem("V"), pdnV.get(dsl, 0))
                        for ft in range(8):
                            tn2 = B.TT(lambda: nc.tensor.matmul(
                                pdn[:, dsl * 512: dsl * 512 + 512],
                                wdnP[:, ebuf * 8192 + ft * 1024 + dj * 128:
                                     ebuf * 8192 + ft * 1024 + dj * 128 + 128],
                                acts[:, (ch2 * 8 + ft) * 512:(ch2 * 8 + ft + 1) * 512],
                                start=(ft == 0), stop=(ft == 7)), tag=(ft == 7))
                        B.V(lambda: nc.vector.tensor_tensor(
                            y2s[:], y2a[:, dj * 1024 + ch2 * 512: dj * 1024 + (ch2 + 1) * 512],
                            pdn[:, dsl * 512: dsl * 512 + 512], ALU.add),
                            waits=[("T", tn2)])
                        pdnV[dsl] = B.V(lambda: nc.vector.tensor_copy(
                            y2a[:, dj * 1024 + ch2 * 512: dj * 1024 + (ch2 + 1) * 512], y2s[:]))
                nc.tensor.wait_ge(B.sem("V"), B.n("V"))
                lastT[ebuf] = B.n("T")
            for ch2 in range(2):
                ch = tb0 + ch2
                dv = B.dma(nc.sync, "xml", xml[:].rearrange("p (a c) -> p a c", a=8),
                           xmid_h[:].rearrange("p (a c) -> p a c", a=8)[:, :, ch * 512:(ch + 1) * 512],
                           waits=[("V", B.n("V")), ("g2s", None), ("xms", None),
                                  ("S", B.n("S"))])
                B.wait(nc.vector, "xml", dv)
                for dj in range(ND):
                    vv = B.V(lambda: nc.vector.tensor_tensor(
                        g2c[:, dj * 512:(dj + 1) * 512],
                        xml[:, dj * 512:(dj + 1) * 512],
                        y2a[:, dj * 1024 + ch2 * 512: dj * 1024 + (ch2 + 1) * 512],
                        ALU.add))
                    sv = B.S(lambda: nc.scalar.activation(
                        xml[:, dj * 512:(dj + 1) * 512],
                        g2c[:, dj * 512:(dj + 1) * 512], AF.Silu), waits=[("V", vv)])
                B.dma(nc.sync, "g2s",
                      g2_h.rearrange("p (a t) -> p a t", a=8)[:, :, ch * 512:(ch + 1) * 512],
                      xml[:].rearrange("p (a c) -> p a c", a=8),
                      waits=[("S", B.n("S"))])
        B.wait(nc.sync, "g2s")
        B.wait(nc.sync, "xms")




def build_phase_b(debug=()):
    nc = bass.Bass("TRN2", target_bir_lowering=False)
    B = Build(nc)

    y_h = nc.dram_tensor("yb", [T, D], F32, kind="ExternalInput").ap()
    g2_h = nc.dram_tensor("g2b", [T, D], F32, kind="ExternalInput").ap()
    wq_h = nc.dram_tensor("bWq", [2, D, D], F32, kind="ExternalInput").ap()
    wk_h = nc.dram_tensor("bWk", [2, D, D], F32, kind="ExternalInput").ap()
    wv_h = nc.dram_tensor("bWv", [2, D, D], F32, kind="ExternalInput").ap()
    wb_h = nc.dram_tensor("bWb", [2, D, 8], F32, kind="ExternalInput").ap()
    wo_h = nc.dram_tensor("bWo", [2, D, D], F32, kind="ExternalInput").ap()
    wg_h = nc.dram_tensor("bWg", [2, D, 5632], F32, kind="ExternalInput").ap()
    wd_h = nc.dram_tensor("bWd", [2, 2816, D], F32, kind="ExternalInput").ap()
    n1w_h = nc.dram_tensor("n1w", [128, 16], F32, kind="ExternalInput").ap()
    n2w_h = nc.dram_tensor("n2w", [128, 16], F32, kind="ExternalInput").ap()
    onw_h = nc.dram_tensor("onw", [2, 128], F32, kind="ExternalInput").ap()
    fnw_h = nc.dram_tensor("fnw", [128, 8], F32, kind="ExternalInput").ap()
    ident_h = nc.dram_tensor("identity_i", [128, 128], F32, kind="ExternalInput").ap()
    onesrow_h = nc.dram_tensor("onesrow_i", [1, 128], F32, kind="ExternalInput").ap()
    mU_h = nc.dram_tensor("maskU_i", [128, 128], F32, kind="ExternalInput").ap()
    mP_h = nc.dram_tensor("maskP_i", [128, 128], F32, kind="ExternalInput").ap()
    oh8_h = nc.dram_tensor("oh8_i", [8, 8 * 128], F32, kind="ExternalInput").ap()
    ohc8_h = nc.dram_tensor("ohc8_i", [128, 64], F32, kind="ExternalInput").ap()

    out_h = nc.dram_tensor("outb", [T, D], F32, kind="ExternalOutput").ap()

    def dbg_tensor(name, shape):
        return nc.dram_tensor("dbg_" + name, shape, F32, kind="ExternalOutput").ap()

    with contextlib.ExitStack() as _st:
        en = _st.enter_context
        xR = en(nc.sbuf_tensor("xR", [128, 8 * T], F32))
        ident = en(nc.sbuf_tensor("identB", [128, 128], BF16))
        identf = en(nc.sbuf_tensor("identfB", [128, 128], F32))
        onesrow = en(nc.sbuf_tensor("onesrowB", [1, 128], BF16))
        mU = en(nc.sbuf_tensor("mUS", [128, 128], BF16))
        mP = en(nc.sbuf_tensor("mPS", [128, 128], BF16))
        oh8 = en(nc.sbuf_tensor("oh8S", [8, 8 * 128], BF16))
        ohc8 = en(nc.sbuf_tensor("ohc8S", [128, 64], BF16))
        n1w = en(nc.sbuf_tensor("n1wS", [128, 16], F32))
        n2w = en(nc.sbuf_tensor("n2wS", [128, 16], F32))
        fnw = en(nc.sbuf_tensor("fnwS", [128, 8], F32))
        onwb = en(nc.sbuf_tensor("onwbS", [128, 2 * 128], BF16))
        smallB = en(nc.sbuf_tensor("smallBS", [128, 16], F32))
        block = en(nc.Block())

        @block.gpsimd
        def _(gpsimd):
            _emit_phase_b(
                (nc, B, debug, y_h, g2_h, wq_h, wk_h, wv_h, wb_h, wo_h, wg_h,
                 wd_h, n1w_h, n2w_h, onw_h, fnw_h, ident_h, onesrow_h, mU_h,
                 mP_h, oh8_h, ohc8_h, out_h, dbg_tensor, xR, ident, identf,
                 onesrow, mU, mP, oh8, ohc8, n1w, n2w, fnw, onwb, smallB))
    return nc


def _rms_chunk_b(nc, B, xR, ch, dst_fn, wcol, eps_ap, scale_ap, sqx, nrow, nrb,
                 nones, onesrow, pE):
    sqT = {}
    tms = 0
    for kd in range(ND):
        sb_ = kd % 2
        vq = B.V(lambda: nc.vector.tensor_tensor(
            sqx[:, sb_ * 512:(sb_ + 1) * 512],
            xR[:, kd * T + ch * 512: kd * T + (ch + 1) * 512],
            xR[:, kd * T + ch * 512: kd * T + (ch + 1) * 512], ALU.mult),
            waits=[("T", sqT.get(sb_, 0))])
        tms = B.TT(lambda: nc.tensor.matmul(
            pE[0:1, :], nones[:], sqx[:, sb_ * 512:(sb_ + 1) * 512],
            start=(kd == 0), stop=(kd == ND - 1)),
            waits=[("V", vq)], tag=True)
        sqT[sb_] = tms
    sr = B.S(lambda: nc.scalar.activation(
        nrow[0:1, :], pE[0:1, :], AF.Ln, bias=eps_ap, scale=scale_ap),
        waits=[("T", tms)])
    sr = B.S(lambda: nc.scalar.activation(
        nrb[0:1, :], nrow[0:1, :], AF.Exp, scale=-0.5))
    trb = B.TT(lambda: nc.tensor.matmul(
        pE[:], onesrow[:], nrb[0:1, :], start=True, stop=True),
        waits=[("S", sr)])
    hv = 0
    for kd in range(ND):
        hv = B.V(lambda: nc.vector.scalar_tensor_tensor(
            dst_fn(kd),
            xR[:, kd * T + ch * 512: kd * T + (ch + 1) * 512],
            wcol[:, kd:kd + 1], pE[:], ALU.mult, ALU.mult),
            waits=[("T", trb)])
    nc.tensor.wait_ge(B.sem("V"), hv)
    return hv


def _emit_phase_b(E):
    (nc, B, debug, y_h, g2_h, wq_h, wk_h, wv_h, wb_h, wo_h, wg_h,
     wd_h, n1w_h, n2w_h, onw_h, fnw_h, ident_h, onesrow_h, mU_h,
     mP_h, oh8_h, ohc8_h, out_h, dbg_tensor, xR, ident, identf,
     onesrow, mU, mP, oh8, ohc8, n1w, n2w, fnw, onwb, smallB) = E

    for t_, s_ in ((ident, ident_h), (onesrow, onesrow_h), (mU, mU_h),
                   (mP, mP_h), (oh8, oh8_h), (ohc8, ohc8_h)):
        B.dma(nc.gpsimd, "ldc", t_[:], s_)
    for t_, s_ in ((identf, ident_h), (n1w, n1w_h), (n2w, n2w_h), (fnw, fnw_h)):
        B.dma(nc.sync, "ldcf", t_[:], s_)
    B.wait(nc.tensor, "ldc")
    B.wait(nc.tensor, "ldcf")
    B.wait(nc.vector, "ldcf")
    B.V(lambda: nc.vector.memset(smallB[:, 0:1], 1e-5))
    B.V(lambda: nc.vector.memset(smallB[:, 1:2], 1e-6))
    B.V(lambda: nc.vector.memset(smallB[:, 2:3], 9.765625e-4))
    B.V(lambda: nc.vector.memset(smallB[:, 3:4], 7.8125e-3))

    _emit_ln_dt(nc, B, y_h, xR, smallB[:, 0:1], identf, F32)

    with contextlib.ExitStack() as stb:
        eb = stb.enter_context
        qF = eb(nc.sbuf_tensor("qF", [128, 8 * T], BF16))
        kF = eb(nc.sbuf_tensor("kF", [128, 8 * T], BF16))
        vF = eb(nc.sbuf_tensor("vF", [128, 8 * T], BF16))
        ws = eb(nc.sbuf_tensor("wsB", [128, 64 * 128], BF16))
        wbs = eb(nc.sbuf_tensor("wbsB", [128, 8 * 8], BF16))
        hc = eb(nc.sbuf_tensor("hcB", [128, 8 * 512], BF16))
        sqx = eb(nc.sbuf_tensor("sqxB", [128, 2 * 512], BF16))
        bsc = eb(nc.sbuf_tensor("bscB", [8, 512], BF16))
        nrow = eb(nc.sbuf_tensor("nrowB", [1, 512], F32))
        nrb = eb(nc.sbuf_tensor("nrbB", [1, 512], BF16))
        nones = eb(nc.sbuf_tensor("nonesB", [128, 1], BF16))
        betaT = eb(nc.sbuf_tensor("betaT", [128, NT * 8], F32))
        rsq = eb(nc.sbuf_tensor("rsqB", [8, 2 * T], BF16))
        qn = eb(nc.sbuf_tensor("qnB", [128, 128], BF16))
        kn = eb(nc.sbuf_tensor("knB", [128, 128], BF16))
        mbeta = eb(nc.sbuf_tensor("mbetaB", [128, 1024], BF16))
        u0 = hc[:, 0:1024]
        dch = hc[:, 1024:2048]
        t1s = hc[:, 2048:3072]
        aU = hc[:, 3072:4096]
        pmS = mbeta
        t2s = dch
        osb = u0
        orr = eb(nc.sbuf_tensor("orrB", [128, 8], F32))
        orb = eb(nc.sbuf_tensor("orbB", [128, 8], F32))
        xscr = eb(nc.sbuf_tensor("xscrB", [128, 512], F32))
        pA = eb(nc.psum_tensor("pA", [128, 1024], F32))
        pB = eb(nc.psum_tensor("pB", [128, 1024], F32))
        pC = eb(nc.psum_tensor("pC", [128, 1024], F32))
        pD = eb(nc.psum_tensor("pD", [128, 1024], BF16))
        pE = eb(nc.psum_tensor("pE", [128, 512], F32))
        B.V(lambda: nc.vector.memset(nones[:], 1.0))

        onrow = sqx[0:1, 0:256]
        B.dma(nc.gpsimd, "ldc", onrow.rearrange("o (l c) -> o l c", l=2),
              onw_h.rearrange("(o l) c -> o l c", o=1))
        for l in range(2):
            tn = B.TT(lambda: nc.tensor.matmul(
                pE[:, 0:128], onesrow[:], onrow[0:1, l * 128:(l + 1) * 128],
                start=True, stop=True),
                waits=[("ldc", None), ("V", B.n("V"))])
            B.V(lambda: nc.vector.tensor_copy(
                onwb[:, l * 128:(l + 1) * 128], pE[:, 0:128]), waits=[("T", tn)])

        for l in range(2):
            for which, (w_h, dstF) in enumerate(((wq_h, qF), (wk_h, kF), (wv_h, vF))):
                B.dma(nc.gpsimd, "lwB",
                      ws[:].rearrange("p (kk f c) -> p kk f c", kk=8, c=128),
                      w_h[l].rearrange("(kk p) (f c) -> p kk f c", p=128, c=128),
                      waits=[("T", B.n("T")), ("V", B.n("V")), ("S", B.n("S"))])
                B.wait(nc.tensor, "lwB")
                evV = {}
                for ch in range(NCH):
                    _rms_chunk_b(nc, B, xR, ch,
                                 lambda kd: hc[:, kd * 512:(kd + 1) * 512],
                                 n1w[:, l * 8:l * 8 + 8],
                                 smallB[0:1, 1:2], smallB[0:1, 2:3],
                                 sqx, nrow, nrb, nones, onesrow, pE)
                    for fi in range(ND):
                        buf = (ch * ND + fi) % 2
                        nc.tensor.wait_ge(B.sem("S"), evV.get(buf, 0))
                        ps = pA[:, buf * 512:(buf + 1) * 512]
                        for kd in range(ND):
                            tn = B.TT(lambda: nc.tensor.matmul(
                                ps, ws[:, (kd * 8 + fi) * 128:(kd * 8 + fi) * 128 + 128],
                                hc[:, kd * 512:(kd + 1) * 512],
                                start=(kd == 0), stop=(kd == ND - 1)), tag=(kd == ND - 1))
                        evV[buf] = B.S(lambda: nc.scalar.activation(
                            dstF[:, fi * T + ch * 512: fi * T + (ch + 1) * 512],
                            ps, AF.Silu), waits=[("T", tn)])
                nc.tensor.wait_ge(B.sem("S"), B.n("S"))
            B.dma(nc.gpsimd, "lwB", wbs[:].rearrange("p (kk c) -> p kk c", kk=8),
                  wb_h[l].rearrange("(kk p) c -> p kk c", p=128),
                  waits=[("T", B.n("T")), ("S", B.n("S"))])
            B.wait(nc.tensor, "lwB")
            for ch in range(NCH):
                _rms_chunk_b(nc, B, xR, ch,
                             lambda kd: hc[:, kd * 512:(kd + 1) * 512],
                             n1w[:, l * 8:l * 8 + 8],
                             smallB[0:1, 1:2], smallB[0:1, 2:3],
                             sqx, nrow, nrb, nones, onesrow, pE)
                for kd in range(ND):
                    tn = B.TT(lambda: nc.tensor.matmul(
                        pA[0:8, 0:512], wbs[:, kd * 8:(kd + 1) * 8],
                        hc[:, kd * 512:(kd + 1) * 512],
                        start=(kd == 0), stop=(kd == ND - 1)),
                        waits=([("V", B.n("V"))] if kd == 0 else ()),
                        tag=(kd == ND - 1))
                sv = B.S(lambda: nc.scalar.activation(
                    bsc[0:8, :], pA[0:8, 0:512], AF.Sigmoid), waits=[("T", tn)])
                for tb in range(4):
                    tt = ch * 4 + tb
                    tn = B.TT(lambda: nc.tensor.transpose(
                        pD[0:128, 0:8], bsc[0:8, tb * 128:(tb + 1) * 128],
                        ident[0:8, 0:8]),
                        waits=[("S", sv), ("V", B.n("V"))])
                    B.V(lambda: nc.vector.tensor_scalar(
                        betaT[:, tt * 8:(tt + 1) * 8], pD[0:128, 0:8],
                        2.0, None, ALU.mult), waits=[("T", tn)])
            nc.tensor.wait_ge(B.sem("V"), B.n("V"))

            if l == 0 and "bqkv" in debug:
                B.dma(nc.gpsimd, "dbg", dbg_tensor("bq", [128, 8 * T]), qF[:],
                      waits=[("S", None), ("V", None)])
                B.dma(nc.gpsimd, "dbg", dbg_tensor("bk", [128, 8 * T]), kF[:], waits=[("S", None)])
                B.dma(nc.gpsimd, "dbg", dbg_tensor("bv", [128, 8 * T]), vF[:], waits=[("S", None)])
                B.dma(nc.gpsimd, "dbg", dbg_tensor("bbt", [128, NT * 8]), betaT[:],
                      waits=[("V", None)])
            sqg = {}
            for w2, srcF in ((0, qF), (1, kF)):
                for ch in range(NCH):
                    for h in range(8):
                        sb_ = h % 2
                        vq = B.V(lambda: nc.vector.tensor_tensor(
                            sqx[:, sb_ * 512:(sb_ + 1) * 512],
                            srcF[:, h * T + ch * 512: h * T + (ch + 1) * 512],
                            srcF[:, h * T + ch * 512: h * T + (ch + 1) * 512],
                            ALU.mult), waits=[("T", sqg.get(sb_, 0))])
                        tn = B.TT(lambda: nc.tensor.matmul(
                            pE[0:8, :], ohc8[:, h * 8:(h + 1) * 8],
                            sqx[:, sb_ * 512:(sb_ + 1) * 512],
                            start=(h == 0), stop=(h == 7)),
                            waits=[("V", vq)], tag=True)
                        sqg[sb_] = tn
                    sv = B.S(lambda: nc.scalar.activation(
                        bsc[0:8, :], pE[0:8, :], AF.Ln, bias=smallB[0:8, 1:2]),
                        waits=[("T", tn)])
                    sv = B.S(lambda: nc.scalar.activation(
                        rsq[0:8, w2 * T + ch * 512: w2 * T + (ch + 1) * 512],
                        bsc[0:8, :], AF.Exp, scale=-0.5))
                    nc.tensor.wait_ge(B.sem("S"), sv)

            B.dma(nc.gpsimd, "lwB",
                  ws[:].rearrange("p (kk f c) -> p kk f c", kk=8, c=128),
                  wo_h[l].rearrange("(kk p) (f c) -> p kk f c", p=128, c=128),
                  waits=[("T", B.n("T")), ("S", B.n("S"))])
            B.wait(nc.tensor, "lwB")

            def app(src, dst_sbuf, bview, scr, extra=None, op=None):
                for h in range(8):
                    tn = B.TT(lambda: nc.tensor.matmul(
                        pA[:, h * 128:(h + 1) * 128], aU[:, h * 128:(h + 1) * 128],
                        src[:, h * 128:(h + 1) * 128], start=True, stop=True),
                        waits=([("V", B.n("V"))] if h == 0 else ()),
                        tag=(h == 7))
                if op is None:
                    return B.V(lambda: nc.vector.tensor_tensor(
                        dst_sbuf[:].rearrange("p (a c) -> p a c", a=8),
                        pA[:].rearrange("p (a c) -> p a c", a=8), bview, ALU.mult),
                        waits=[("T", tn)])
                B.V(lambda: nc.vector.tensor_tensor(
                    scr[:].rearrange("p (a c) -> p a c", a=8),
                    pA[:].rearrange("p (a c) -> p a c", a=8), bview, ALU.mult),
                    waits=[("T", tn)])
                return B.V(lambda: nc.vector.tensor_tensor(
                    dst_sbuf[:], extra[:], scr[:], op))

            for tt in range(NT):
                tsl = slice(tt * 128, (tt + 1) * 128)
                for h in range(8):
                    tn = B.TT(lambda: nc.tensor.transpose(
                        pD[:, h * 128:(h + 1) * 128],
                        vF[:, h * T + tt * 128: h * T + (tt + 1) * 128], ident[:]),
                        waits=([("V", B.n("V"))] if h == 0 else ()), tag=(h == 7))
                bview = betaT[:, tt * 8:(tt + 1) * 8].unsqueeze(2).broadcast_to([128, 8, 128])
                vu0 = B.V(lambda: nc.vector.tensor_tensor(
                    u0[:].rearrange("p (a c) -> p a c", a=8),
                    pD[:].rearrange("p (a c) -> p a c", a=8), bview, ALU.mult),
                    waits=[("T", tn), ("dbg", None)])
                for h in range(8):
                    tr1 = B.TT(lambda: nc.tensor.matmul(
                        pE[:, 0:128], oh8[:, h * 128:(h + 1) * 128],
                        rsq[0:8, 0 * T + tt * 128: 0 * T + (tt + 1) * 128],
                        start=True, stop=True),
                        waits=[("V", B.n("V"))], tag=True)
                    tr2 = B.TT(lambda: nc.tensor.matmul(
                        pE[:, 128:256], oh8[:, h * 128:(h + 1) * 128],
                        rsq[0:8, 1 * T + tt * 128: 1 * T + (tt + 1) * 128],
                        start=True, stop=True), tag=True)
                    vq1 = B.V(lambda: nc.vector.tensor_tensor(
                        qn[:], qF[:, h * T + tt * 128: h * T + (tt + 1) * 128],
                        pE[:, 0:128], ALU.mult), waits=[("T", tr2)])
                    vq2 = B.V(lambda: nc.vector.tensor_tensor(
                        kn[:], kF[:, h * T + tt * 128: h * T + (tt + 1) * 128],
                        pE[:, 128:256], ALU.mult))
                    tn = B.TT(lambda: nc.tensor.matmul(
                        pA[:, h * 128:(h + 1) * 128], kn[:], kn[:],
                        start=True, stop=True), waits=[("V", vq2)], tag=False)
                    tn = B.TT(lambda: nc.tensor.matmul(
                        pB[:, h * 128:(h + 1) * 128], kn[:], qn[:],
                        start=True, stop=True), tag=(h == 7))
                mview = mU[:].unsqueeze(1).broadcast_to([128, 8, 128])
                vau = B.V(lambda: nc.vector.tensor_tensor(
                    aU[:].rearrange("p (a c) -> p a c", a=8),
                    pA[:].rearrange("p (a c) -> p a c", a=8), mview, ALU.mult),
                    waits=[("T", tn)])
                nc.tensor.wait_ge(B.sem("V"), vau)
                app(u0, dch, bview, mbeta)
                app(dch, t1s, bview, mbeta)
                app(t1s, dch, bview, mbeta)
                app(dch, t1s, bview, mbeta, extra=u0, op=ALU.add)
                app(t1s, dch, bview, mbeta)
                app(dch, dch, bview, mbeta, extra=t1s, op=ALU.add)
                vsol = app(dch, t1s, bview, mbeta, extra=dch, op=ALU.subtract)
                mpview = mP[:].unsqueeze(1).broadcast_to([128, 8, 128])
                vpm = B.V(lambda: nc.vector.tensor_tensor(
                    pmS[:].rearrange("p (a c) -> p a c", a=8),
                    pB[:].rearrange("p (a c) -> p a c", a=8), mpview, ALU.mult))
                nc.tensor.wait_ge(B.sem("V"), vpm)
                if l == 0 and tt == 0 and "btile" in debug:
                    B.dma(nc.gpsimd, "dbg", dbg_tensor("bu0", [128, 1024]), u0[:],
                          waits=[("V", None), ("S", None)])
                    B.dma(nc.gpsimd, "dbg", dbg_tensor("bau", [128, 1024]), aU[:])
                    B.dma(nc.gpsimd, "dbg", dbg_tensor("bpm", [128, 1024]), pmS[:])
                    B.dma(nc.gpsimd, "dbg", dbg_tensor("bU", [128, 1024]), t1s[:])
                    nc.tensor.wait_ge(B.sem("dbg"), B.n("dbg"))
                for h in range(8):
                    tn = B.TT(lambda: nc.tensor.matmul(
                        pB[:, h * 128:(h + 1) * 128], pmS[:, h * 128:(h + 1) * 128],
                        t1s[:, h * 128:(h + 1) * 128], start=True, stop=True),
                        tag=(h == 7))
                vo = B.V(lambda: nc.vector.tensor_copy(osb[:], pB[:]), waits=[("T", tn)])
                vm = B.V(lambda: nc.vector.tensor_tensor(
                    dch[:], osb[:], osb[:], ALU.mult))
                vm = B.V(lambda: nc.vector.tensor_reduce(
                    orr[:, 0:8], dch[:].rearrange("p (a c) -> p a c", a=8),
                    AX.X, ALU.add))
                so = B.S(lambda: nc.scalar.activation(
                    orb[:, 0:8], orr[:, 0:8], AF.Ln, bias=smallB[:, 1:2],
                    scale=smallB[:, 3:4]), waits=[("V", vm)])
                so = B.S(lambda: nc.scalar.activation(
                    orr[:, 0:8], orb[:, 0:8], AF.Exp, scale=-0.5))
                oview = orr[:, 0:8].unsqueeze(2).broadcast_to([128, 8, 128])
                vn1 = B.V(lambda: nc.vector.tensor_tensor(
                    dch[:].rearrange("p (a c) -> p a c", a=8),
                    osb[:].rearrange("p (a c) -> p a c", a=8), oview, ALU.mult),
                    waits=[("S", so)])
                wview = onwb[:, l * 128:(l + 1) * 128].unsqueeze(1).broadcast_to([128, 8, 128])
                vn2 = B.V(lambda: nc.vector.tensor_tensor(
                    t1s[:].rearrange("p (a c) -> p a c", a=8),
                    dch[:].rearrange("p (a c) -> p a c", a=8), wview, ALU.mult))
                nc.tensor.wait_ge(B.sem("V"), vn2)
                for h in range(8):
                    tn = B.TT(lambda: nc.tensor.transpose(
                        pD[:, h * 128:(h + 1) * 128],
                        t1s[:, h * 128:(h + 1) * 128], ident[:]), tag=(h == 7))
                vt = B.V(lambda: nc.vector.tensor_copy(dch[:], pD[:]), waits=[("T", tn)])
                nc.tensor.wait_ge(B.sem("V"), vt)
                for dj in range(ND):
                    for h in range(8):
                        tn = B.TT(lambda: nc.tensor.matmul(
                            pC[:, dj * 128:(dj + 1) * 128],
                            ws[:, (h * 8 + dj) * 128:(h * 8 + dj) * 128 + 128],
                            dch[:, h * 128:(h + 1) * 128],
                            start=(h == 0), stop=(h == 7)), tag=(h == 7 and dj == ND - 1))
                for hf in range(2):
                    xview = xR[:].rearrange("p (a t) -> p a t", a=8)\
                        [:, hf * 4:(hf + 1) * 4, tt * 128:(tt + 1) * 128]
                    vx = B.V(lambda: nc.vector.tensor_tensor(
                        xscr[:].rearrange("p (a c) -> p a c", a=4), xview,
                        pC[:, hf * 512:(hf + 1) * 512].rearrange("p (a c) -> p a c", a=4),
                        ALU.add), waits=([("T", tn)] if hf == 0 else ()))
                    vx = B.V(lambda: nc.vector.tensor_copy(
                        xview, xscr[:].rearrange("p (a c) -> p a c", a=4)))
                nc.tensor.wait_ge(B.sem("V"), vx)

            if l == 0 and "bdelta" in debug:
                B.dma(nc.gpsimd, "dbg", dbg_tensor("brsq", [8, 2 * T]), rsq[:],
                      waits=[("S", None), ("V", None), ("T", None)])
                B.dma(nc.gpsimd, "dbg", dbg_tensor("bx1", [128, 8 * T]), xR[:],
                      waits=[("V", None)])
            _emit_b_glu(nc, B, l, wg_h, wd_h, xR, qF, kF, vF, n2w, smallB,
                        nones, onesrow, sqx, nrow, nrb, xscr, pA, pB, pE)

        _emit_b_final(nc, B, y_h, g2_h, out_h, xR, fnw, smallB, nones, onesrow,
                      sqx, nrow, nrb, hc, ident, pA, pD, pE, xscr,
                      qF[:, 0:8192].bitcast(F32), kF[:, 0:8192].bitcast(F32),
                      vF[:, 0:8192].bitcast(F32))


def _emit_b_glu(nc, B, l, wg_h, wd_h, xR, h2buf, actbuf, wpbuf, n2w, smallB,
                nones, onesrow, sqx, nrow, nrb, xscr, pA, pB, pE):
    nc.vector.wait_ge(B.sem("T"), B.n("T"))
    nc.scalar.wait_ge(B.sem("T"), B.n("T"))
    B.wait(nc.vector, "dbg")
    for ch in range(NCH):
        _rms_chunk_b(nc, B, xR, ch,
                     lambda kd: h2buf[:, kd * T + ch * 512: kd * T + (ch + 1) * 512],
                     n2w[:, l * 8:l * 8 + 8],
                     smallB[0:1, 1:2], smallB[0:1, 2:3],
                     sqx, nrow, nrb, nones, onesrow, pE)
    wg_r = wg_h[l].rearrange("(kk p) c -> p kk c", p=128)
    wd_r = wd_h[l].rearrange("(kk p) (dj c) -> kk p dj c", p=128, c=128)
    for f0, nf in ((0, 6), (6, 6), (12, 5), (17, 5)):
        parG = {}
        upT = {}
        for gi in range(nf):
            ft = f0 + gi
            par = gi % 2
            gslot = wpbuf[:, 8192 + par * 2048: 8192 + par * 2048 + 1024]
            yslot = wpbuf[:, 8192 + par * 2048 + 1024: 8192 + (par + 1) * 2048]
            wdslot = wpbuf[:, gi * 1024:(gi + 1) * 1024]
            gw = [("T", max(parG.get(par, 0), B.n("T") if gi < 2 else 0)),
                  ("V", B.n("V")), ("S", B.n("S"))]
            B.dma(nc.gpsimd, "lgg%d" % par, gslot.rearrange("p (kk c) -> p kk c", kk=8),
                  wg_r[:, :, ft * 128:(ft + 1) * 128], waits=gw)
            B.dma(nc.gpsimd, "lgy%d" % par, yslot.rearrange("p (kk c) -> p kk c", kk=8),
                  wg_r[:, :, 2816 + ft * 128: 2816 + (ft + 1) * 128])
            B.dma(nc.gpsimd, "lgd%d" % par, wdslot.rearrange("p (dj c) -> p dj c", dj=8),
                  wd_r[ft])
            B.wait(nc.tensor, "lgg%d" % par)
            B.wait(nc.tensor, "lgy%d" % par)
            for ch in range(NCH):
                for kd in range(ND):
                    nc.tensor.matmul(
                        pA[:, 0:512], gslot[:, kd * 128:(kd + 1) * 128],
                        h2buf[:, kd * T + ch * 512: kd * T + (ch + 1) * 512],
                        start=(kd == 0), stop=(kd == ND - 1))
                    tn = B.TT(lambda: nc.tensor.matmul(
                        pB[:, 0:512], yslot[:, kd * 128:(kd + 1) * 128],
                        h2buf[:, kd * T + ch * 512: kd * T + (ch + 1) * 512],
                        start=(kd == 0), stop=(kd == ND - 1)), tag=(kd == ND - 1))
                sg = B.S(lambda: nc.scalar.activation(
                    sqx[:, 0:512], pA[:, 0:512], AF.Silu), waits=[("T", tn)])
                va = B.V(lambda: nc.vector.tensor_tensor(
                    actbuf[:, (gi * 4 + ch) * 512:(gi * 4 + ch + 1) * 512],
                    sqx[:, 0:512], pB[:, 0:512], ALU.mult), waits=[("S", sg)])
                nc.tensor.wait_ge(B.sem("S"), sg)
                nc.tensor.wait_ge(B.sem("V"), va)
            parG[par] = B.n("T")
        B.wait(nc.tensor, "lgd0")
        B.wait(nc.tensor, "lgd1")
        pdV = {}
        for ch in range(NCH):
            for dj in range(ND):
                psl = dj % 2
                nc.tensor.wait_ge(B.sem("V"), pdV.get(psl, 0))
                for gi in range(nf):
                    tn = B.TT(lambda: nc.tensor.matmul(
                        pA[:, psl * 512:(psl + 1) * 512],
                        wpbuf[:, gi * 1024 + dj * 128: gi * 1024 + dj * 128 + 128],
                        actbuf[:, (gi * 4 + ch) * 512:(gi * 4 + ch + 1) * 512],
                        start=(gi == 0), stop=(gi == nf - 1)), tag=(gi == nf - 1))
                B.V(lambda: nc.vector.tensor_tensor(
                    xscr[:, 0:512], xR[:, dj * T + ch * 512: dj * T + (ch + 1) * 512],
                    pA[:, psl * 512:(psl + 1) * 512], ALU.add), waits=[("T", tn)])
                pdV[psl] = B.V(lambda: nc.vector.tensor_copy(
                    xR[:, dj * T + ch * 512: dj * T + (ch + 1) * 512], xscr[:, 0:512]))
        nc.tensor.wait_ge(B.sem("V"), B.n("V"))


def _emit_b_final(nc, B, y_h, g2_h, out_h, xR, fnw, smallB, nones, onesrow,
                  sqx, nrow, nrb, hc, ident, pA, pD, pE, xscr, ytl, gtl, otl):
    if True:
        oV = {}
        pDg = 0
        g0 = [("T", None), ("V", None), ("S", None)]
        for ch in range(NCH):
            _rms_chunk_b(nc, B, xR, ch,
                         lambda kd: hc[:, kd * 512:(kd + 1) * 512],
                         fnw[:, 0:8],
                         smallB[0:1, 1:2], smallB[0:1, 2:3],
                         sqx, nrow, nrb, nones, onesrow, pE)
            for tb in range(4):
                tt = ch * 4 + tb
                par = tt % 2
                sy = "ly%d" % par
                B.dma(nc.sync, sy, ytl[:, par * D:(par + 1) * D],
                      y_h.rearrange("(t p) d -> p t d", p=128)[:, tt, :],
                      waits=[("ot%d" % par, None)] + g0)
                B.dma(nc.sync, "lg%d" % par, gtl[:, par * D:(par + 1) * D],
                      g2_h.rearrange("(t p) d -> p t d", p=128)[:, tt, :],
                      waits=[("V", oV.get(par, 0) or 0), ("ot%d" % par, None)] + g0)
                for dj in range(ND):
                    tn = B.TT(lambda: nc.tensor.transpose(
                        pD[:, dj * 128:(dj + 1) * 128],
                        hc[:, dj * 512 + tb * 128: dj * 512 + (tb + 1) * 128],
                        ident[:]),
                        waits=([("V", max(oV.get(par, 0), pDg))] if dj == 0 else ()),
                        tag=(dj == ND - 1))
                B.wait(nc.vector, sy)
                B.wait(nc.vector, "lg%d" % par)
                pDg = B.V(lambda: nc.vector.tensor_tensor(
                    otl[:, par * D:(par + 1) * D].rearrange("p (a c) -> p a c", a=8),
                    gtl[:, par * D:(par + 1) * D].rearrange("p (a c) -> p a c", a=8),
                    pD[:].rearrange("p (a c) -> p a c", a=8), ALU.mult),
                    waits=[("T", tn)])
                oV[par] = B.V(lambda: nc.vector.tensor_add(
                    otl[:, par * D:(par + 1) * D], ytl[:, par * D:(par + 1) * D],
                    otl[:, par * D:(par + 1) * D])) if False else None
                oV[par] = B.V(lambda: nc.vector.tensor_tensor(
                    gtl[:, par * D:(par + 1) * D],
                    ytl[:, par * D:(par + 1) * D],
                    otl[:, par * D:(par + 1) * D], ALU.add))
                B.dma(nc.sync, "ot%d" % par,
                      out_h.rearrange("(t p) d -> p t d", p=128)[:, tt, :],
                      gtl[:, par * D:(par + 1) * D], waits=[("V", oV[par])])
        B.wait(nc.sync, "ot0")
        B.wait(nc.sync, "ot1")



def _pack_col(v):
    return np.ascontiguousarray(v.reshape(8, 128).T)


def _consts_a():
    ident = np.eye(128, dtype=np.float32)
    onesrow = np.ones((1, 128), np.float32)
    m = np.zeros((4, 128, 512), np.float32)
    r = np.arange(128)[:, None]
    c = np.arange(512)[None, :]
    for mm in range(4):
        m[mm] = np.where(128 * mm + r > c, NEG, 0.0)
    ohc = np.zeros((128, 16, 16), np.float32)
    for h in range(16):
        rows = slice(0, 64) if h % 2 == 0 else slice(64, 128)
        ohc[rows, h, h] = 1.0
    oh16 = np.zeros((16, 16, 128), np.float32)
    for h in range(16):
        oh16[h, h, :] = 1.0
    return dict(identity_i=ident, onesrow_i=onesrow, maskL0T=m,
                ohc16_i=np.ascontiguousarray(ohc.reshape(128, 256)),
                oh16_i=np.ascontiguousarray(oh16.reshape(16, 2048)))


def _inputs_a(inp, c):
    t = np.ascontiguousarray
    pair = lambda w: t(w.transpose(0, 2, 1).reshape(8, 2, 1024, 64)
                       .transpose(0, 2, 1, 3).reshape(8, 1024, 128))
    d = {
        "x": t(inp["L0_attn_x"][c]),
        "wq0": pair(inp["L0_attn_W_Q"]),
        "wk0": pair(inp["L0_attn_W_K"]),
        "wv0": pair(inp["L0_attn_W_V"]),
        "wo0t": t(inp["L0_attn_W_O"].T),
        "wr0t": t(inp["L0_ffn_W_r"].T),
        "wk0ft": t(inp["L0_ffn_W_k"].T),
        "wv0ft": t(inp["L0_ffn_W_v"].T),
        "mur": _pack_col(inp["L0_ffn_mu_r"]),
        "muk": _pack_col(inp["L0_ffn_mu_k"]),
    }
    d.update(_consts_a())
    return d




def _consts_a2():
    DH = 64
    pos = np.arange(T, dtype=np.float64) / 32.0
    inv = 1.0 / (150000.0 ** (np.arange(0, DH, 2, dtype=np.float64) / DH))
    fr = inv[:, None] * pos[None, :]
    cos = np.ascontiguousarray(np.tile(np.cos(fr).astype(np.float32), (4, 1)))
    sin = np.ascontiguousarray(np.tile(np.sin(fr).astype(np.float32), (4, 1)))
    r = np.arange(128)[:, None]
    c = np.arange(128)[None, :]
    sub = np.where(c >= r, 0.0, NEG).astype(np.float32)
    diag = np.where(c <= r, 0.0, NEG).astype(np.float32)
    mwin = np.concatenate([sub, diag], axis=1)
    oh32 = np.zeros((32, 32, 128), np.float32)
    for e in range(32):
        oh32[e, e, :] = 1.0
    return dict(cosq_i=cos, sinq_i=sin, maskwin_i=mwin,
                oh32_i=np.ascontiguousarray(oh32.reshape(32, 4096)))


_PERM64 = np.concatenate([np.arange(0, 64, 2), np.arange(1, 64, 2)])


def _inputs_a2(inp, y_c):
    t = np.ascontiguousarray
    qkvw = inp["L1_ffn_qkv_w"].copy()
    qkvb = inp["L1_ffn_qkv_b"].copy()
    for h in range(16):
        qkvw[:, h * 64:(h + 1) * 64] = qkvw[:, h * 64 + _PERM64]
        qkvb[h * 64:(h + 1) * 64] = qkvb[h * 64 + _PERM64]
    for kv in range(4):
        b0 = 1024 + kv * 64
        qkvw[:, b0:b0 + 64] = qkvw[:, b0 + _PERM64]
        qkvb[b0:b0 + 64] = qkvb[b0 + _PERM64]
    d = {
        "y": t(y_c),
        "qkvw_i": t(qkvw),
        "qkvb_i": t(qkvb.reshape(12, 128).T),
        "projw_i": t(inp["L1_ffn_proj_w"]),
        "projb_i": _pack_col(inp["L1_ffn_proj_b"]),
        "sinkr": t(inp["L1_ffn_attn_sink"].reshape(1, 16)),
        "gatew_i": t(inp["L1_ffn_gate_w"]),
        "gateb_i": t(inp["L1_ffn_gate_b"].reshape(32, 1)),
        "wup_i": t(inp["L1_ffn_W_up"]),
        "wdn_i": t(inp["L1_ffn_W_down"]),
        "ffnw_i": _pack_col(inp["L1_ffn_ffn_norm_w"]),
        "attnw_i": _pack_col(inp["L1_ffn_attn_norm_w"]),
    }
    d.update(_consts_a2())
    c1 = _consts_a()
    d["identity_i"] = c1["identity_i"]
    d["onesrow_i"] = c1["onesrow_i"]
    return d



def _consts_b():
    ij = np.arange(128)
    sameg = (ij[:, None] // 8) == (ij[None, :] // 8)
    mU = (sameg & ((ij[:, None] % 8) < (ij[None, :] % 8))).astype(np.float32)
    mP = (sameg & ((ij[:, None] % 8) <= (ij[None, :] % 8))).astype(np.float32)
    oh8 = np.zeros((8, 8, 128), np.float32)
    for h in range(8):
        oh8[h, h, :] = 1.0
    ohc8 = np.zeros((128, 8, 8), np.float32)
    for h in range(8):
        ohc8[:, h, h] = 1.0
    return dict(maskU_i=mU, maskP_i=mP,
                oh8_i=np.ascontiguousarray(oh8.reshape(8, 1024)),
                ohc8_i=np.ascontiguousarray(ohc8.reshape(128, 64)))


def _inputs_b(inp, y_j, g2_j):
    t = np.ascontiguousarray
    d = {
        "yb": t(y_j.astype(np.float32)),
        "g2b": t(g2_j.astype(np.float32)),
        "bWq": t(inp["L1_attn_Wq"]),
        "bWk": t(inp["L1_attn_Wk"]),
        "bWv": t(inp["L1_attn_Wv"]),
        "bWb": t(inp["L1_attn_Wb"]),
        "bWo": t(inp["L1_attn_Wo"]),
        "bWg": t(inp["L1_attn_Wg"]),
        "bWd": t(inp["L1_attn_Wd"]),
        "n1w": t(np.stack([_pack_col(inp["L1_attn_norm1_w"][l]) for l in range(2)], 1)
                 .reshape(128, 16)),
        "n2w": t(np.stack([_pack_col(inp["L1_attn_norm2_w"][l]) for l in range(2)], 1)
                 .reshape(128, 16)),
        "onw": t(inp["L1_attn_onorm_w"]),
        "fnw": _pack_col(inp["L1_attn_final_norm_w"]),
    }
    d.update(_consts_b())
    c1 = _consts_a()
    d["identity_i"] = c1["identity_i"]
    d["onesrow_i"] = c1["onesrow_i"]
    return d

def _dt_rows(a):
    return a.reshape(128, 8, T).transpose(2, 1, 0).reshape(T, D)


def kernel(**inputs):
    inputs = {k: np.asarray(v) for k, v in inputs.items()}
    cores = list(range(8))

    nc_a = build_phase_a()
    res_a = run_bass_kernel_spmd(nc_a, [_inputs_a(inputs, c) for c in cores],
                                 core_ids=cores)
    y = np.stack([np.asarray(r["y"]) for r in res_a.results], 0)

    nc_a2 = build_phase_a2()
    res_a2 = run_bass_kernel_spmd(nc_a2, [_inputs_a2(inputs, y[c]) for c in cores],
                                  core_ids=cores)
    g2 = np.stack([_dt_rows(np.asarray(r["g2"])) for r in res_a2.results], 0)

    def resl(a, j):
        return np.ascontiguousarray(
            a[:, j * 256:(j + 1) * 256, :].transpose(1, 0, 2).reshape(T, D))

    nc_b = build_phase_b()
    res_b = run_bass_kernel_spmd(
        nc_b, [_inputs_b(inputs, resl(y, j), resl(g2, j)) for j in cores],
        core_ids=cores)
    out = np.empty((8, T, D), np.float32)
    for j in cores:
        o = np.asarray(res_b.results[j]["outb"]).reshape(256, 8, D).transpose(1, 0, 2)
        out[:, j * 256:(j + 1) * 256, :] = o
    return out


# version 06
# speedup vs baseline: 1.2551x; 1.0266x over previous
import contextlib
import numpy as np
import concourse.bass as bass
import concourse.mybir as mybir
from concourse.bass_utils import run_bass_kernel_spmd

F32 = mybir.dt.float32
BF16 = mybir.dt.bfloat16
AF = mybir.ActivationFunctionType
ALU = mybir.AluOpType
AX = mybir.AxisListType

T = 2048
D = 1024
NT = T // 128
ND = D // 128
NCH = T // 512
TP1 = T + 1
NEG = -1e30


class Build:

    def __init__(self, nc):
        self.nc = nc
        self.sems = {}
        self.cnt = {}

    def sem(self, name):
        if name not in self.sems:
            self.sems[name] = self.nc.semaphore(name).__enter__()
            self.cnt[name] = 0
        return self.sems[name]

    def n(self, name):
        self.sem(name)
        return self.cnt[name]

    def inc(self, name, inst, k=1):
        inst.then_inc(self.sem(name), k)
        self.cnt[name] += k
        return self.cnt[name]

    def wait(self, eng, name, val=None):
        self.sem(name)
        v = self.cnt[name] if val is None else val
        if v > 0:
            eng.wait_ge(self.sems[name], v)

    def _chain(self, name, eng, emit, waits):
        self.sem(name)
        for wname, wval in waits:
            self.wait(eng, wname, wval)
        if self.cnt[name] > 0:
            eng.wait_ge(self.sems[name], self.cnt[name])
        return self.inc(name, emit())

    def V(self, emit, waits=()):
        return self._chain("V", self.nc.vector, emit, waits)

    def S(self, emit, waits=()):
        return self._chain("S", self.nc.scalar, emit, waits)

    def G(self, emit, waits=()):
        return self._chain("G", self.nc.gpsimd, emit, waits)

    def TT(self, emit, waits=(), tag=True):
        for wname, wval in waits:
            self.wait(self.nc.tensor, wname, wval)
        inst = emit()
        if tag:
            return self.inc("T", inst)
        return self.n("T")

    def dma(self, queue, name, out, in_, waits=(), **kw):
        for wname, wval in waits:
            self.wait(queue, wname, wval)
        if queue is self.nc.gpsimd:
            self.wait(queue, name)
        inst = queue.dma_start(out=out, in_=in_, **kw)
        return self.inc(name, inst, 16)



def build_phase_a(debug=(), upto="y"):
    nc = bass.Bass("TRN2", target_bir_lowering=False)
    B = Build(nc)
    result = {}

    x_h = nc.dram_tensor("x", [T, D], F32, kind="ExternalInput").ap()
    wq0_h = nc.dram_tensor("wq0", [8, D, 128], F32, kind="ExternalInput").ap()
    wk0_h = nc.dram_tensor("wk0", [8, D, 128], F32, kind="ExternalInput").ap()
    wv0_h = nc.dram_tensor("wv0", [8, D, 128], F32, kind="ExternalInput").ap()
    wo0_h = nc.dram_tensor("wo0t", [D, D], F32, kind="ExternalInput").ap()
    wr0_h = nc.dram_tensor("wr0t", [D, D], F32, kind="ExternalInput").ap()
    wkf_h = nc.dram_tensor("wk0ft", [D, D], F32, kind="ExternalInput").ap()
    wvf_h = nc.dram_tensor("wv0ft", [D, D], F32, kind="ExternalInput").ap()
    mur_h = nc.dram_tensor("mur", [128, 8], F32, kind="ExternalInput").ap()
    muk_h = nc.dram_tensor("muk", [128, 8], F32, kind="ExternalInput").ap()
    ident_h = nc.dram_tensor("identity_i", [128, 128], F32, kind="ExternalInput").ap()
    mL0_h = nc.dram_tensor("maskL0T", [4, 128, 512], F32, kind="ExternalInput").ap()
    ohc16_h = nc.dram_tensor("ohc16_i", [128, 256], F32, kind="ExternalInput").ap()
    oh16_h = nc.dram_tensor("oh16_i", [16, 2048], F32, kind="ExternalInput").ap()
    onesrow_h = nc.dram_tensor("onesrow_i", [1, 128], F32, kind="ExternalInput").ap()

    y_h = nc.dram_tensor("y", [T, D], F32, kind="ExternalOutput").ap()
    zscr_h = nc.dram_tensor("zscr", [16, 64, T], BF16, kind="Internal").ap()

    def dbg_tensor(name, shape):
        return nc.dram_tensor("dbg_" + name, shape, F32, kind="ExternalOutput").ap()

    with contextlib.ExitStack() as _st:
        en = _st.enter_context
        xnDT = en(nc.sbuf_tensor("xnDT", [128, ND * TP1], BF16))
        ident = en(nc.sbuf_tensor("ident", [128, 128], BF16))
        onesrow = en(nc.sbuf_tensor("onesrow", [1, 128], BF16))
        negMb = en(nc.sbuf_tensor("negMb", [128, 16], F32))
        small = en(nc.sbuf_tensor("small", [128, 64], F32))
        smallb = en(nc.sbuf_tensor("smallb", [16, 4], BF16))
        block = en(nc.Block())
        E = (nc, B, debug, upto, x_h, wq0_h, wk0_h, wv0_h, wo0_h, wr0_h, wkf_h,
             wvf_h, mur_h, muk_h, ident_h, mL0_h, ohc16_h, oh16_h, onesrow_h,
             y_h, zscr_h, dbg_tensor, xnDT, ident, onesrow,
             negMb, small, smallb)
        @block.gpsimd
        def _(gpsimd):
            _emit_phase_a(E)
    return nc


def _emit_phase_a(E):
    (nc, B, debug, upto, x_h, wq0_h, wk0_h, wv0_h, wo0_h, wr0_h, wkf_h, wvf_h,
     mur_h, muk_h, ident_h, mL0_h, ohc16_h, oh16_h, onesrow_h, y_h, zscr_h,
     dbg_tensor, xnDT, ident, onesrow, negMb, small, smallb) = E
    if True:
        if True:
            B.dma(nc.gpsimd, "ldc", ident[:], ident_h)
            B.dma(nc.gpsimd, "ldc", onesrow[:], onesrow_h)

            with contextlib.ExitStack() as _st0:
                en0 = _st0.enter_context
                qP = en0(nc.sbuf_tensor("qP", [128, 8 * T], BF16))
                kP = en0(nc.sbuf_tensor("kP", [128, 8 * T], BF16))
                vTN = en0(nc.sbuf_tensor("vTN", [128, NT * 1040], BF16))
                wq0s = en0(nc.sbuf_tensor("wq0s", [128, 64 * 128], BF16))
                wk0s = en0(nc.sbuf_tensor("wk0s", [128, 64 * 128], BF16))
                wv0s = en0(nc.sbuf_tensor("wv0s", [128, 64 * 128], BF16))
                mskL0 = en0(nc.sbuf_tensor("mskL0", [128, 4 * 512], BF16))
                ohc16 = en0(nc.sbuf_tensor("ohc16", [128, 256], BF16))
                oh16 = en0(nc.sbuf_tensor("oh16", [16, 2048], BF16))
                for w_s, w_hh in ((wq0s, wq0_h), (wk0s, wk0_h), (wv0s, wv0_h)):
                    B.dma(nc.gpsimd, "ldw",
                          w_s[:].rearrange("p (a k c) -> p a k c", a=8, k=8),
                          w_hh.rearrange("a (k p) c -> p a k c", p=128))
                B.dma(nc.gpsimd, "ldc", mskL0[:].rearrange("p (m c) -> p m c", m=4),
                      mL0_h.rearrange("m p c -> p m c"))
                B.dma(nc.gpsimd, "ldc", ohc16[:], ohc16_h)
                B.dma(nc.gpsimd, "ldc", oh16[:], oh16_h)

                with contextlib.ExitStack() as _st1:
                    en1 = _st1.enter_context
                    lnscr = en1(nc.sbuf_tensor("lnscr", [128, D], F32))
                    st8 = en1(nc.sbuf_tensor("st8", [128, 8], F32))
                    xn_t = en1(nc.sbuf_tensor("xn_t", [128, 2 * D], BF16))
                    xld = en1(nc.sbuf_tensor("xld", [128, 2 * D], F32))
                    ptr = en1(nc.psum_tensor("ptr", [128, 2 * 1024], BF16))
                    B.wait(nc.tensor, "ldc")
                    B.V(lambda: nc.vector.memset(small[:, 22:23], 1e-5))
                    copyV = {}
                    trT = {}
                    applyS = {}
                    B.dma(nc.sync, "ldxA", xld[:, 0:D],
                          x_h.rearrange("(t p) d -> p t d", p=128)[:, 0, :])
                    for it in range(NT):
                        buf = it % 2
                        if it + 1 < NT:
                            B.dma(nc.sync, "ldxA" if (it + 1) % 2 == 0 else "ldxB",
                                  xld[:, (1 - buf) * D:(2 - buf) * D],
                                  x_h.rearrange("(t p) d -> p t d", p=128)[:, it + 1, :],
                                  waits=[("S", applyS.get(1 - buf, 0))])
                        xt = xld[:, buf * D:(buf + 1) * D]
                        ldsem = "ldxA" if buf == 0 else "ldxB"
                        B.wait(nc.vector, ldsem, 16 * (it // 2 + 1))
                        B.wait(nc.scalar, ldsem, 16 * (it // 2 + 1))
                        B.V(lambda: nc.vector.tensor_tensor(lnscr[:], xt, xt, ALU.mult))
                        B.V(lambda: nc.vector.tensor_reduce(st8[:, 7:8], lnscr[:], AX.X, ALU.add))
                        B.V(lambda: nc.vector.tensor_scalar(st8[:, 1:2], st8[:, 7:8], 1.0 / D, None, ALU.mult))
                        B.V(lambda: nc.vector.tensor_reduce(st8[:, 0:1], xt, AX.X, ALU.add))
                        B.V(lambda: nc.vector.tensor_scalar(st8[:, 2:3], st8[:, 0:1], 1.0 / D, None, ALU.mult))
                        B.V(lambda: nc.vector.tensor_tensor(st8[:, 3:4], st8[:, 2:3], st8[:, 2:3], ALU.mult))
                        vn = B.V(lambda: nc.vector.tensor_tensor(st8[:, 4:5], st8[:, 1:2], st8[:, 3:4], ALU.subtract))
                        B.S(lambda: nc.scalar.activation(st8[:, 5:6], st8[:, 4:5], AF.Ln,
                                                          bias=small[:, 22:23]),
                            waits=[("V", vn)])
                        B.S(lambda: nc.scalar.activation(st8[:, 6:7], st8[:, 5:6], AF.Exp, scale=-0.5))
                        sn = B.n("S")
                        B.V(lambda: nc.vector.tensor_tensor(st8[:, 7:8], st8[:, 2:3], st8[:, 6:7], ALU.mult),
                            waits=[("S", sn)])
                        vn = B.V(lambda: nc.vector.tensor_scalar(st8[:, 3:4], st8[:, 7:8], -1.0, None, ALU.mult))
                        sn = B.S(lambda: nc.scalar.activation(
                            xn_t[:, buf * D:(buf + 1) * D], xt, AF.Identity,
                            bias=st8[:, 3:4], scale=st8[:, 6:7]),
                            waits=[("V", vn), ("T", trT.get(buf, 0))])
                        applyS[buf] = sn
                        nc.tensor.wait_ge(B.sem("S"), sn)
                        if buf in copyV:
                            nc.tensor.wait_ge(B.sem("V"), copyV[buf])
                        for jd in range(ND):
                            tn = B.TT(lambda: nc.tensor.transpose(
                                ptr[:, buf * 1024 + jd * 128: buf * 1024 + (jd + 1) * 128],
                                xn_t[:, buf * D + jd * 128: buf * D + (jd + 1) * 128],
                                ident[:]))
                        trT[buf] = tn
                        for jd in range(ND):
                            vn = B.V(lambda: nc.vector.tensor_copy(
                                xnDT[:, jd * TP1 + 1 + it * 128: jd * TP1 + 1 + (it + 1) * 128],
                                ptr[:, buf * 1024 + jd * 128: buf * 1024 + (jd + 1) * 128]),
                                waits=[("T", tn)])
                        copyV[buf] = vn
                    for jd in range(ND):
                        B.V(lambda: nc.vector.memset(xnDT[:, jd * TP1: jd * TP1 + 1], 0.0))

                if "xn" in debug:
                    B.dma(nc.gpsimd, "dbg", dbg_tensor("xn", [128, ND * TP1]), xnDT[:],
                          waits=[("V", None)])
                if upto == "s1":
                    B.wait(nc.sync, "V")
                    B.wait(nc.sync, "S")
                    return

                with contextlib.ExitStack() as _st2:
                    en2 = _st2.enter_context
                    pmm = en2(nc.psum_tensor("pmm", [128, 2 * 512], F32))
                    ptrv = en2(nc.psum_tensor("ptrv", [128, 2 * 1024], BF16))
                    sqs = en2(nc.sbuf_tensor("sqs", [128, 2 * 512], BF16))
                    vscr = en2(nc.sbuf_tensor("vscr", [128, T], BF16))
                    B.wait(nc.tensor, "ldw")
                    B.wait(nc.tensor, "V")
                    B.wait(nc.tensor, "S")
                    B.V(lambda: nc.vector.memset(vTN[:], 1.0), waits=[("ldc", None)])
                    evV = {}
                    evS = {}
                    for pi in range(8):
                        for which, ws in ((0, wq0s), (1, wk0s), (2, wv0s)):
                            for ch in range(NCH):
                                buf = (pi * 12 + which * 4 + ch) % 2
                                nc.tensor.wait_ge(B.sem("V"), evV.get(buf, 0))
                                ps = pmm[:, buf * 512:(buf + 1) * 512]
                                for kd in range(ND):
                                    tn = B.TT(lambda: nc.tensor.matmul(
                                        ps,
                                        ws[:, (pi * 8 + kd) * 128:(pi * 8 + kd) * 128 + 128],
                                        xnDT[:, kd * TP1 + 1 + ch * 512: kd * TP1 + 1 + (ch + 1) * 512],
                                        start=(kd == 0), stop=(kd == ND - 1)),
                                        tag=(kd == ND - 1))
                                dst = (qP, kP, None)[which]
                                if which == 1:
                                    evV[buf] = B.V(lambda: nc.vector.tensor_scalar(
                                        dst[:, pi * T + ch * 512: pi * T + (ch + 1) * 512],
                                        ps, 0.125, None, ALU.mult), waits=[("T", tn)])
                                elif which == 0:
                                    evV[buf] = B.V(lambda: nc.vector.tensor_copy(
                                        dst[:, pi * T + ch * 512: pi * T + (ch + 1) * 512],
                                        ps), waits=[("T", tn)])
                                else:
                                    evV[buf] = B.V(lambda: nc.vector.tensor_copy(
                                        vscr[:, ch * 512:(ch + 1) * 512],
                                        ps), waits=[("T", tn)])
                        vprev = B.n("V")
                        nc.tensor.wait_ge(B.sem("V"), vprev)
                        tcopyV = {}
                        for tc in range(NT):
                            pbuf = tc % 2
                            nc.tensor.wait_ge(B.sem("V"), tcopyV.get(pbuf, 0))
                            tn = B.TT(lambda: nc.tensor.transpose(
                                ptrv[:, pbuf * 1024: pbuf * 1024 + 128],
                                vscr[:, tc * 128: (tc + 1) * 128],
                                ident[:]))
                            B.V(lambda: nc.vector.tensor_copy(
                                vTN[:, tc * 1040 + (2 * pi) * 65: tc * 1040 + (2 * pi) * 65 + 64],
                                ptrv[:, pbuf * 1024: pbuf * 1024 + 64]),
                                waits=[("T", tn)])
                            tcopyV[pbuf] = B.V(lambda: nc.vector.tensor_copy(
                                vTN[:, tc * 1040 + (2 * pi + 1) * 65: tc * 1040 + (2 * pi + 1) * 65 + 64],
                                ptrv[:, pbuf * 1024 + 64: pbuf * 1024 + 128]))

                    sq_guard = {}
                    for which, src in ((0, qP), (1, kP)):
                        for ch in range(NCH):
                            pbuf = ch % 2
                            ps = pmm[:, pbuf * 512:(pbuf + 1) * 512]
                            nc.tensor.wait_ge(B.sem("V"), sq_guard.get(("ps", pbuf), 0))
                            for pi in range(8):
                                sbuf_i = pi % 2
                                vn = B.V(lambda: nc.vector.tensor_tensor(
                                    sqs[:, sbuf_i * 512:(sbuf_i + 1) * 512],
                                    src[:, pi * T + ch * 512: pi * T + (ch + 1) * 512],
                                    src[:, pi * T + ch * 512: pi * T + (ch + 1) * 512],
                                    ALU.mult),
                                    waits=[("T", sq_guard.get(("sq", sbuf_i), 0))])
                                for hh in range(2):
                                    h = 2 * pi + hh
                                    tn = B.TT(lambda: nc.tensor.matmul(
                                        ps[0:16, :], ohc16[:, h * 16:(h + 1) * 16],
                                        sqs[:, sbuf_i * 512:(sbuf_i + 1) * 512],
                                        start=(pi == 0 and hh == 0),
                                        stop=(pi == 7 and hh == 1)),
                                        waits=([("V", vn)] if hh == 0 else ()),
                                        tag=(hh == 1))
                                sq_guard[("sq", sbuf_i)] = B.n("T")
                            sq_guard[("ps", pbuf)] = B.V(lambda: nc.vector.tensor_reduce(
                                small[0:16, which * 4 + ch: which * 4 + ch + 1],
                                ps[0:16, :], AX.X, ALU.max), waits=[("T", tn)])
                        B.V(lambda: nc.vector.tensor_reduce(
                            small[0:16, 16 + which: 17 + which],
                            small[0:16, which * 4: which * 4 + 4], AX.X, ALU.max))
                    B.V(lambda: nc.vector.tensor_tensor(
                        small[0:16, 18:19], small[0:16, 16:17], small[0:16, 17:18], ALU.mult))
                    vn = B.n("V")
                    sn = B.S(lambda: nc.scalar.activation(small[0:16, 20:21], small[0:16, 18:19], AF.Sqrt),
                             waits=[("V", vn)])
                    vn = B.V(lambda: nc.vector.tensor_scalar(
                        smallb[0:16, 0:1], small[0:16, 20:21], -1.0, None, ALU.mult),
                        waits=[("S", sn)])
                    repV = 0
                    for h in range(16):
                        pbuf = h % 2
                        nc.tensor.wait_ge(B.sem("V"), max(repV, vn))
                        tn = B.TT(lambda: nc.tensor.matmul(
                            pmm[:, pbuf * 512: pbuf * 512 + 1],
                            oh16[:, h * 128:(h + 1) * 128],
                            smallb[0:16, 0:1], start=True, stop=True))
                        repV = B.V(lambda: nc.vector.tensor_copy(
                            negMb[:, h:h + 1], pmm[:, pbuf * 512: pbuf * 512 + 1]),
                            waits=[("T", tn)])
                    nc.tensor.wait_ge(B.sem("V"), B.n("V"))

                if "qkv" in debug:
                    B.dma(nc.gpsimd, "dbg", dbg_tensor("q", [128, 8 * T]), qP[:], waits=[("V", None)])
                    B.dma(nc.gpsimd, "dbg", dbg_tensor("k", [128, 8 * T]), kP[:], waits=[("V", None)])
                    B.dma(nc.gpsimd, "dbg", dbg_tensor("v", [128, NT * 1040]), vTN[:], waits=[("V", None)])
                    B.dma(nc.gpsimd, "dbg", dbg_tensor("negm", [128, 16]), negMb[:], waits=[("V", None)])

                with contextlib.ExitStack() as _st3:
                    en3 = _st3.enter_context
                    psc = en3(nc.psum_tensor("psc", [128, 2 * 512], F32))
                    pscB = en3(nc.psum_tensor("pscB", [128, 2 * 512], F32))
                    pav = en3(nc.psum_tensor("pav", [128, 2 * 512], F32))
                    prb = en3(nc.psum_tensor("prb", [128, 512], F32))
                    etile = en3(nc.sbuf_tensor("etile", [128, 4 * 512], BF16))
                    rrow = en3(nc.sbuf_tensor("rrow", [1, 512], F32))
                    rrowb = en3(nc.sbuf_tensor("rrowb", [1, 512], BF16))
                    rb = en3(nc.sbuf_tensor("rb", [128, 512], F32))
                    zev = en3(nc.sbuf_tensor("zev", [128, 512], BF16))
                    B.wait(nc.tensor, "V")
                    expS = {}
                    avT = {}
                    zread = 0
                    zevG = 0
                    last_tr = 0
                    last_rb = 0
                    for pi in range(8):
                        hA, hB = 2 * pi, 2 * pi + 1
                        for qc in range(NCH):
                            nkc = 4 * (qc + 1)
                            for kc in range(nkc):
                                sb = kc % 2
                                last_mm = {}
                                for half, (base, psall) in ((0, (0, psc)), (1, (64, pscB))):
                                    nc.tensor.wait_ge(B.sem("S"), expS.get((half, sb), 0))
                                    ps = psall[:, sb * 512:(sb + 1) * 512]
                                    diag = kc >= 4 * qc
                                    i1 = nc.tensor.matmul(
                                        ps,
                                        kP[base:base + 64, pi * T + kc * 128: pi * T + (kc + 1) * 128],
                                        qP[base:base + 64, pi * T + qc * 512: pi * T + (qc + 1) * 512],
                                        start=True, stop=not diag,
                                        tile_position=(base, 0))
                                    if diag:
                                        m = kc - 4 * qc
                                        i1 = nc.tensor.matmul(
                                            ps, ident[:],
                                            mskL0[:, m * 512:(m + 1) * 512],
                                            start=False, stop=True)
                                    last_mm[half] = i1
                                tn = B.inc("T", last_mm[1])
                                for half, (psall, hh) in ((0, (psc, hA)), (1, (pscB, hB))):
                                    ps = psall[:, sb * 512:(sb + 1) * 512]
                                    expS[(half, sb)] = B.S(lambda: nc.scalar.activation(
                                        etile[:, (2 * sb + half) * 512:(2 * sb + half + 1) * 512],
                                        ps, AF.Exp, bias=negMb[:, hh:hh + 1]),
                                        waits=[("T", tn)])
                                for half, hh in ((0, hA), (1, hB)):
                                    if kc == 0:
                                        nc.tensor.wait_ge(B.sem("V"), zread)
                                        nc.tensor.wait_ge(B.sem("S"), zevG)
                                    avT[half] = B.TT(lambda: nc.tensor.matmul(
                                        pav[0:65, half * 512:(half + 1) * 512],
                                        vTN[:, kc * 1040 + hh * 65: kc * 1040 + (hh + 1) * 65],
                                        etile[:, (2 * sb + half) * 512:(2 * sb + half + 1) * 512],
                                        start=(kc == 0), stop=(kc == nkc - 1)),
                                        waits=[("S", expS[(half, sb)])],
                                        tag=(kc == nkc - 1))
                            for half, hh in ((0, hA), (1, hB)):
                                pa = pav[0:65, half * 512:(half + 1) * 512]
                                sn = B.S(lambda: nc.scalar.activation(
                                    rrow[0:1, :], pa[64:65, :], AF.Ln),
                                    waits=[("T", avT[half])])
                                zevG = sn
                                sn = B.S(lambda: nc.scalar.activation(
                                    rrowb[0:1, :], rrow[0:1, :], AF.Exp, scale=-1.0),
                                    waits=[("T", last_tr)])
                                tr = B.TT(lambda: nc.tensor.matmul(
                                    prb[:], onesrow[:], rrowb[0:1, :],
                                    start=True, stop=True),
                                    waits=[("S", sn), ("V", last_rb)])
                                last_tr = tr
                                vn = B.V(lambda: nc.vector.tensor_copy(rb[:], prb[:]),
                                         waits=[("T", tr)])
                                last_rb = vn
                                zread = B.V(lambda: nc.vector.tensor_tensor(
                                    zev[0:64, :], pa[0:64, :], rb[0:64, :], ALU.mult),
                                    waits=[("zst", None)])
                                B.dma(nc.gpsimd, "zst",
                                      zscr_h[hh, :, qc * 512:(qc + 1) * 512],
                                      zev[0:64, :], waits=[("V", zread)])

            with contextlib.ExitStack() as _st4:
                en4 = _st4.enter_context
                sDT = en4(nc.sbuf_tensor("sDT", [128, 8 * T], BF16))
                wr0s = en4(nc.sbuf_tensor("wr0s", [128, 64 * 128], BF16))
                wkfs = en4(nc.sbuf_tensor("wkfs", [128, 64 * 128], BF16))
                wvfs = en4(nc.sbuf_tensor("wvfs", [128, 64 * 128], BF16))
                muR = en4(nc.sbuf_tensor("muR", [128, 8], F32))
                muK = en4(nc.sbuf_tensor("muK", [128, 8], F32))
                _st4b = en4(contextlib.ExitStack())
                en4b = _st4b.enter_context
                zP = en4b(nc.sbuf_tensor("zP", [128, 8 * T], BF16))
                wo0s = en4b(nc.sbuf_tensor("wo0s", [128, 64 * 128], BF16))
                pm2 = en4b(nc.psum_tensor("pm2", [128, 2 * 512], F32))
                nT_s3 = B.n("T")
                first = True
                for w_s, w_hh in ((wo0s, wo0_h), (wr0s, wr0_h), (wkfs, wkf_h), (wvfs, wvf_h)):
                    B.dma(nc.gpsimd, "ldw2",
                          w_s[:].rearrange("p (a c) -> p a c", a=8),
                          w_hh.rearrange("(a p) c -> p a c", p=128),
                          waits=([("T", nT_s3)] if first else ()))
                    first = False
                B.dma(nc.sync, "ldw2b", muR[:], mur_h, waits=[("T", nT_s3)])
                B.dma(nc.sync, "ldw2b", muK[:], muk_h)
                B.dma(nc.sync, "zld",
                      zP[:].rearrange("p (a t) -> p a t", a=8),
                      zscr_h.rearrange("(a b) e t -> (b e) a t", b=2),
                      waits=[("zst", None), ("T", nT_s3)])
                if "z" in debug:
                    B.dma(nc.gpsimd, "dbg", dbg_tensor("z", [128, 8 * T]), zP[:],
                          waits=[("zld", None)])
                B.wait(nc.tensor, "ldw2")
                B.wait(nc.tensor, "ldw2b")
                B.wait(nc.tensor, "zld")
                B.wait(nc.tensor, "V")
                B.wait(nc.tensor, "S")
                evV = {}
                for dj in range(ND):
                    for ch in range(NCH):
                        buf = (dj * 4 + ch) % 2
                        nc.tensor.wait_ge(B.sem("V"), evV.get(buf, 0))
                        ps = pm2[:, buf * 512:(buf + 1) * 512]
                        for pi in range(8):
                            tn = B.TT(lambda: nc.tensor.matmul(
                                ps, wo0s[:, (pi * 8 + dj) * 128:(pi * 8 + dj) * 128 + 128],
                                zP[:, pi * T + ch * 512: pi * T + (ch + 1) * 512],
                                start=(pi == 0), stop=(pi == 7)), tag=(pi == 7))
                        evV[buf] = B.V(lambda: nc.vector.tensor_copy(
                            sDT[:, dj * T + ch * 512: dj * T + (ch + 1) * 512], ps),
                            waits=[("T", tn)])

                if "s" in debug:
                    B.dma(nc.gpsimd, "dbg", dbg_tensor("s", [128, 8 * T]), sDT[:],
                          waits=[("V", None)])
                if upto == "s4":
                    B.wait(nc.sync, "V")
                    B.wait(nc.sync, "S")
                    return

                nT_s4 = B.n("T")
                _st4b.close()
                with contextlib.ExitStack() as _st5:
                    en5 = _st5.enter_context
                    tcs = en5(nc.sbuf_tensor("tcs", [128, 8 * 512], BF16))
                    xrc = en5(nc.sbuf_tensor("xrc", [128, 8 * 512], BF16))
                    xkc = en5(nc.sbuf_tensor("xkc", [128, 8 * 512], BF16))
                    rc = en5(nc.sbuf_tensor("rc", [128, 8 * 512], BF16))
                    krel = en5(nc.sbuf_tensor("krel", [128, 8 * 512], BF16))
                    kac = en5(nc.sbuf_tensor("kac", [128, 8 * 512], BF16))
                    gpre = en5(nc.sbuf_tensor("gpre", [128, 8 * 512], BF16))
                    gc = en5(nc.sbuf_tensor("gc", [128, 8 * 512], BF16))
                    ydc = en5(nc.sbuf_tensor("ydc", [128, 8 * 512], BF16))
                    xt2 = en5(nc.sbuf_tensor("xt2", [128, 2 * D], F32))
                    yt2 = en5(nc.sbuf_tensor("yt2", [128, 2 * D], F32))
                    pf = en5(nc.psum_tensor("pf", [128, 2 * 512], F32))
                    ptb = en5(nc.psum_tensor("ptb", [128, 2 * 1024], BF16))
                    for ch in range(NCH):
                        for jd in range(ND):
                            a1 = xnDT[:, jd * TP1 + 1 + ch * 512: jd * TP1 + 1 + (ch + 1) * 512]
                            a0 = xnDT[:, jd * TP1 + 0 + ch * 512: jd * TP1 + 0 + (ch + 1) * 512]
                            B.V(lambda: nc.vector.tensor_tensor(
                                tcs[:, jd * 512:(jd + 1) * 512], a1, a0, ALU.subtract),
                                waits=[("dbg", None)])
                            B.V(lambda: nc.vector.scalar_tensor_tensor(
                                xrc[:, jd * 512:(jd + 1) * 512],
                                tcs[:, jd * 512:(jd + 1) * 512], muR[:, jd:jd + 1], a0,
                                ALU.mult, ALU.add))
                            B.V(lambda: nc.vector.scalar_tensor_tensor(
                                xkc[:, jd * 512:(jd + 1) * 512],
                                tcs[:, jd * 512:(jd + 1) * 512], muK[:, jd:jd + 1], a0,
                                ALU.mult, ALU.add))
                        nc.tensor.wait_ge(B.sem("V"), B.n("V"))
                        for which, (src, ws) in ((0, (xrc, wr0s)), (1, (xkc, wkfs))):
                            for dd in range(ND):
                                buf = dd % 2
                                ps = pf[:, buf * 512:(buf + 1) * 512]
                                nc.tensor.wait_ge(B.sem("S"), B.n("S"))
                                nc.tensor.wait_ge(B.sem("V"), B.n("V"))
                                for jd in range(ND):
                                    tn = B.TT(lambda: nc.tensor.matmul(
                                        ps, ws[:, (jd * 8 + dd) * 128:(jd * 8 + dd) * 128 + 128],
                                        src[:, jd * 512:(jd + 1) * 512],
                                        start=(jd == 0), stop=(jd == 7)), tag=(jd == 7))
                                if which == 0:
                                    B.S(lambda: nc.scalar.activation(
                                        rc[:, dd * 512:(dd + 1) * 512], ps, AF.Sigmoid),
                                        waits=[("T", tn)])
                                else:
                                    vn = B.V(lambda: nc.vector.tensor_relu(
                                        krel[:, dd * 512:(dd + 1) * 512], ps),
                                        waits=[("T", tn)])
                                    B.S(lambda: nc.scalar.activation(
                                        kac[:, dd * 512:(dd + 1) * 512],
                                        krel[:, dd * 512:(dd + 1) * 512], AF.Square),
                                        waits=[("V", vn)])
                        for dd in range(ND):
                            buf = dd % 2
                            ps = pf[:, buf * 512:(buf + 1) * 512]
                            nc.tensor.wait_ge(B.sem("S"), B.n("S"))
                            nc.tensor.wait_ge(B.sem("V"), B.n("V"))
                            for jd in range(ND):
                                tn = B.TT(lambda: nc.tensor.matmul(
                                    ps, wvfs[:, (jd * 8 + dd) * 128:(jd * 8 + dd) * 128 + 128],
                                    kac[:, jd * 512:(jd + 1) * 512],
                                    start=(jd == 0), stop=(jd == 7)), tag=(jd == 7))
                            vn = B.V(lambda: nc.vector.tensor_tensor(
                                gpre[:, dd * 512:(dd + 1) * 512],
                                rc[:, dd * 512:(dd + 1) * 512], ps, ALU.mult),
                                waits=[("T", tn)])
                            sn = B.S(lambda: nc.scalar.activation(
                                gc[:, dd * 512:(dd + 1) * 512],
                                gpre[:, dd * 512:(dd + 1) * 512], AF.Silu),
                                waits=[("V", vn)])
                            B.V(lambda: nc.vector.tensor_tensor(
                                ydc[:, dd * 512:(dd + 1) * 512],
                                sDT[:, dd * T + ch * 512: dd * T + (ch + 1) * 512],
                                gc[:, dd * 512:(dd + 1) * 512], ALU.mult),
                                waits=[("S", sn)])
                        nc.tensor.wait_ge(B.sem("V"), B.n("V"))
                        for tb in range(4):
                            tt = ch * 4 + tb
                            tbuf = tt % 2
                            stysem = "styA" if tbuf == 0 else "styB"
                            ldsem = "ldx2A" if tbuf == 0 else "ldx2B"
                            B.dma(nc.sync, ldsem, xt2[:, tbuf * D:(tbuf + 1) * D],
                                  x_h.rearrange("(t p) d -> p t d", p=128)[:, tt, :],
                                  waits=[(stysem, None)])
                            B.wait(nc.vector, ldsem)
                            for jd in range(ND):
                                pbuf = jd % 2
                                tn = B.TT(lambda: nc.tensor.transpose(
                                    ptb[:, pbuf * 1024: pbuf * 1024 + 128],
                                    ydc[:, jd * 512 + tb * 128: jd * 512 + (tb + 1) * 128],
                                    ident[:]))
                                vn = B.V(lambda: nc.vector.tensor_tensor(
                                    yt2[:, tbuf * D + jd * 128: tbuf * D + (jd + 1) * 128],
                                    xt2[:, tbuf * D + jd * 128: tbuf * D + (jd + 1) * 128],
                                    ptb[:, pbuf * 1024: pbuf * 1024 + 128], ALU.add),
                                    waits=[("T", tn)])
                                nc.tensor.wait_ge(B.sem("V"), vn)
                            B.dma(nc.sync, stysem,
                                  y_h.rearrange("(t p) d -> p t d", p=128)[:, tt, :],
                                  yt2[:, tbuf * D:(tbuf + 1) * D], waits=[("V", None)])
                B.wait(nc.sync, "styA")
                B.wait(nc.sync, "styB")




def build_phase_a2(debug=()):
    nc = bass.Bass("TRN2", target_bir_lowering=False)
    B = Build(nc)

    y_h = nc.dram_tensor("y", [T, D], F32, kind="ExternalInput").ap()
    qkvw_h = nc.dram_tensor("qkvw_i", [D, 1536], F32, kind="ExternalInput").ap()
    qkvb_h = nc.dram_tensor("qkvb_i", [128, 12], F32, kind="ExternalInput").ap()
    projw_h = nc.dram_tensor("projw_i", [D, D], F32, kind="ExternalInput").ap()
    projb_h = nc.dram_tensor("projb_i", [128, 8], F32, kind="ExternalInput").ap()
    sink_h = nc.dram_tensor("sinkr", [1, 16], F32, kind="ExternalInput").ap()
    gatew_h = nc.dram_tensor("gatew_i", [D, 32], F32, kind="ExternalInput").ap()
    gateb_h = nc.dram_tensor("gateb_i", [32, 1], F32, kind="ExternalInput").ap()
    wup_h = nc.dram_tensor("wup_i", [32, D, 2048], F32, kind="ExternalInput").ap()
    wdn_h = nc.dram_tensor("wdn_i", [32, D, D], F32, kind="ExternalInput").ap()
    ffnw_h = nc.dram_tensor("ffnw_i", [128, 8], F32, kind="ExternalInput").ap()
    attnw_h = nc.dram_tensor("attnw_i", [128, 8], F32, kind="ExternalInput").ap()
    cosq_h = nc.dram_tensor("cosq_i", [128, T], F32, kind="ExternalInput").ap()
    sinq_h = nc.dram_tensor("sinq_i", [128, T], F32, kind="ExternalInput").ap()
    ident_h = nc.dram_tensor("identity_i", [128, 128], F32, kind="ExternalInput").ap()
    mwin_h = nc.dram_tensor("maskwin_i", [128, 256], F32, kind="ExternalInput").ap()
    oh32_h = nc.dram_tensor("oh32_i", [32, 32 * 128], F32, kind="ExternalInput").ap()
    onesrow_h = nc.dram_tensor("onesrow_i", [1, 128], F32, kind="ExternalInput").ap()

    g2_h = nc.dram_tensor("g2", [128, 8 * T], F32, kind="ExternalOutput").ap()
    xmid_h = nc.dram_tensor("xmid", [128, 8 * T], F32, kind="Internal").ap()

    def dbg_tensor(name, shape):
        return nc.dram_tensor("dbg_" + name, shape, F32, kind="ExternalOutput").ap()

    with contextlib.ExitStack() as _st:
        en = _st.enter_context
        ident = en(nc.sbuf_tensor("ident", [128, 128], BF16))
        onesrow = en(nc.sbuf_tensor("onesrow", [1, 128], BF16))
        h2 = en(nc.sbuf_tensor("h2", [128, 8 * T], BF16))
        cTb = en(nc.sbuf_tensor("cTb", [32, T], BF16))
        small = en(nc.sbuf_tensor("small", [128, 32], F32))
        ffnw = en(nc.sbuf_tensor("ffnw", [128, 8], F32))
        attnw = en(nc.sbuf_tensor("attnw", [128, 8], F32))
        block = en(nc.Block())

        @block.gpsimd
        def _(gpsimd):
            _emit_phase_a2(
                (nc, B, debug, y_h, qkvw_h, qkvb_h, projw_h, projb_h, sink_h,
                 gatew_h, gateb_h, wup_h, wdn_h, ffnw_h, cosq_h, sinq_h,
                 ident_h, mwin_h, oh32_h, onesrow_h, g2_h,
                 xmid_h, dbg_tensor, ident, onesrow, h2, cTb, small, ffnw,
                 attnw, attnw_h))
    return nc


def _emit_ln_dt(nc, B, src_h, dst, small_eps, ident, dt_out=BF16):
    with contextlib.ExitStack() as st6:
        e6 = st6.enter_context
        lnscr = e6(nc.sbuf_tensor("ln_scr", [128, D], F32))
        st8 = e6(nc.sbuf_tensor("ln_st8", [128, 8], F32))
        xn_t = e6(nc.sbuf_tensor("ln_xnt", [128, 2 * D], dt_out))
        xld = e6(nc.sbuf_tensor("ln_xld", [128, 2 * D], F32))
        ptr = e6(nc.psum_tensor("ln_ptr", [128, 2 * 1024], dt_out))
        copyV = {}
        trT = {}
        applyS = {}
        B.dma(nc.sync, "lnldA", xld[:, 0:D],
              src_h.rearrange("(t p) d -> p t d", p=128)[:, 0, :])
        base_a = B.n("lnldA") - 16
        base_b = B.n("lnldB")
        for it in range(NT):
            buf = it % 2
            if it + 1 < NT:
                B.dma(nc.sync, "lnldA" if (it + 1) % 2 == 0 else "lnldB",
                      xld[:, (1 - buf) * D:(2 - buf) * D],
                      src_h.rearrange("(t p) d -> p t d", p=128)[:, it + 1, :],
                      waits=[("S", applyS.get(1 - buf, 0))])
            xt = xld[:, buf * D:(buf + 1) * D]
            ldsem = "lnldA" if buf == 0 else "lnldB"
            base = base_a if buf == 0 else base_b
            B.wait(nc.vector, ldsem, base + 16 * (it // 2 + 1))
            B.wait(nc.scalar, ldsem, base + 16 * (it // 2 + 1))
            B.V(lambda: nc.vector.tensor_tensor(lnscr[:], xt, xt, ALU.mult))
            B.V(lambda: nc.vector.tensor_reduce(st8[:, 7:8], lnscr[:], AX.X, ALU.add))
            B.V(lambda: nc.vector.tensor_scalar(st8[:, 1:2], st8[:, 7:8], 1.0 / D, None, ALU.mult))
            B.V(lambda: nc.vector.tensor_reduce(st8[:, 0:1], xt, AX.X, ALU.add))
            B.V(lambda: nc.vector.tensor_scalar(st8[:, 2:3], st8[:, 0:1], 1.0 / D, None, ALU.mult))
            B.V(lambda: nc.vector.tensor_tensor(st8[:, 3:4], st8[:, 2:3], st8[:, 2:3], ALU.mult))
            vn = B.V(lambda: nc.vector.tensor_tensor(st8[:, 4:5], st8[:, 1:2], st8[:, 3:4], ALU.subtract))
            B.S(lambda: nc.scalar.activation(st8[:, 5:6], st8[:, 4:5], AF.Ln,
                                             bias=small_eps), waits=[("V", vn)])
            B.S(lambda: nc.scalar.activation(st8[:, 6:7], st8[:, 5:6], AF.Exp, scale=-0.5))
            sn = B.n("S")
            B.V(lambda: nc.vector.tensor_tensor(st8[:, 7:8], st8[:, 2:3], st8[:, 6:7], ALU.mult),
                waits=[("S", sn)])
            vn = B.V(lambda: nc.vector.tensor_scalar(st8[:, 3:4], st8[:, 7:8], -1.0, None, ALU.mult))
            sn = B.S(lambda: nc.scalar.activation(
                xn_t[:, buf * D:(buf + 1) * D], xt, AF.Identity,
                bias=st8[:, 3:4], scale=st8[:, 6:7]),
                waits=[("V", vn), ("T", trT.get(buf, 0))])
            applyS[buf] = sn
            nc.tensor.wait_ge(B.sem("S"), sn)
            if buf in copyV:
                nc.tensor.wait_ge(B.sem("V"), copyV[buf])
            for jd in range(ND):
                tn = B.TT(lambda: nc.tensor.transpose(
                    ptr[:, buf * 1024 + jd * 128: buf * 1024 + (jd + 1) * 128],
                    xn_t[:, buf * D + jd * 128: buf * D + (jd + 1) * 128],
                    ident[:]))
            trT[buf] = tn
            for jd in range(ND):
                vn = B.V(lambda: nc.vector.tensor_copy(
                    dst[:, jd * T + it * 128: jd * T + (it + 1) * 128],
                    ptr[:, buf * 1024 + jd * 128: buf * 1024 + (jd + 1) * 128]),
                    waits=[("T", tn)])
            copyV[buf] = vn


def _emit_phase_a2(E):
    (nc, B, debug, y_h, qkvw_h, qkvb_h, projw_h, projb_h, sink_h,
     gatew_h, gateb_h, wup_h, wdn_h, ffnw_h, cosq_h, sinq_h,
     ident_h, mwin_h, oh32_h, onesrow_h, g2_h,
     xmid_h, dbg_tensor, ident, onesrow, h2, cTb, small, ffnw,
     attnw, attnw_h) = E

    B.dma(nc.gpsimd, "ldc", ident[:], ident_h)
    B.dma(nc.gpsimd, "ldc", onesrow[:], onesrow_h)
    B.dma(nc.sync, "ldcf", ffnw[:], ffnw_h)
    B.dma(nc.sync, "ldcf", attnw[:], attnw_h)
    B.wait(nc.tensor, "ldc")
    B.wait(nc.vector, "ldcf")
    B.V(lambda: nc.vector.memset(small[:, 8:9], 1e-5))

    st69 = contextlib.ExitStack()
    xn2 = st69.enter_context(nc.sbuf_tensor("xn2", [128, 8 * T], BF16))
    _emit_ln_dt(nc, B, y_h, xn2, small[:, 8:9], ident)

    if "xn2" in debug:
        B.dma(nc.gpsimd, "dbg", dbg_tensor("xn2", [128, 8 * T]), xn2[:],
              waits=[("V", None)])

    with contextlib.ExitStack() as st7:
        e7 = st7.enter_context
        qp2 = e7(nc.sbuf_tensor("qp2", [128, 8 * T], BF16))
        kp2 = e7(nc.sbuf_tensor("kp2", [128, 2 * T], BF16))
        vp2 = e7(nc.sbuf_tensor("vp2", [128, 2 * T], BF16))
        vT2 = e7(nc.sbuf_tensor("vT2", [128, NT * 256], BF16))
        z2 = e7(nc.sbuf_tensor("z2", [128, 8 * T], BF16))
        sinkb = e7(nc.sbuf_tensor("sinkb", [128, 16], F32))
        mwin = e7(nc.sbuf_tensor("mwin", [128, 256], BF16))
        pmm = e7(nc.psum_tensor("pmm", [128, 2 * 512], F32))
        ptv = e7(nc.psum_tensor("ptv", [128, 2 * 1024], BF16))
        st7i = e7(contextlib.ExitStack())
        e7i = st7i.enter_context
        qkvs = e7i(nc.sbuf_tensor("qkvs", [128, 8 * 12 * 128], BF16))
        qkvb = e7i(nc.sbuf_tensor("qkvb", [128, 12], F32))
        cosq = e7i(nc.sbuf_tensor("cosq", [128, T], BF16))
        sinq = e7i(nc.sbuf_tensor("sinq", [128, T], BF16))
        ropes = e7i(nc.sbuf_tensor("ropes", [128, 3 * 512], BF16))
        rawc = e7i(nc.sbuf_tensor("rawc", [128, 2 * 512], BF16))
        hatc = e7i(nc.sbuf_tensor("hatc", [128, 8 * 512], BF16))
        sqx = e7i(nc.sbuf_tensor("sqx", [128, 2 * 512], BF16))
        nrow = e7i(nc.sbuf_tensor("nrow", [1, 512], F32))
        nrb = e7i(nc.sbuf_tensor("nrb", [1, 512], BF16))
        nones = e7i(nc.sbuf_tensor("nones", [128, 1], BF16))
        pnm = e7i(nc.psum_tensor("pnm", [128, 512], F32))
        pnr = e7i(nc.psum_tensor("pnr", [128, 512], F32))
        B.dma(nc.gpsimd, "ldw",
              qkvs[:].rearrange("p (k f c) -> p k f c", k=8, f=12),
              qkvw_h.rearrange("(k p) (f c) -> p k f c", p=128, c=128))
        B.dma(nc.gpsimd, "ldw", mwin[:], mwin_h)
        B.dma(nc.sync, "ldcf", qkvb[:], qkvb_h)
        B.dma(nc.gpsimd, "ldc", cosq[:], cosq_h)
        B.dma(nc.gpsimd, "ldc", sinq[:], sinq_h)
        with nc.sbuf_tensor("sinkrow", [1, 16], BF16) as sinkrow:
            B.dma(nc.gpsimd, "ldc", sinkrow[:], sink_h)
            tn = B.TT(lambda: nc.tensor.matmul(
                pmm[:, 0:16], onesrow[:], sinkrow[0:1, :], start=True, stop=True),
                waits=[("ldc", None), ("V", None), ("S", None)])
            B.V(lambda: nc.vector.tensor_copy(sinkb[:], pmm[:, 0:16]), waits=[("T", tn)])

        B.wait(nc.tensor, "ldw")
        B.wait(nc.tensor, "ldcf")
        B.wait(nc.tensor, "V")
        B.V(lambda: nc.vector.memset(nones[:], 1.0))
        B.V(lambda: nc.vector.memset(small[0:1, 10:11], 9.765625e-4))
        evV = {}
        hprevT = {}
        for ch in range(NCH):
            hbuf = 0
            sqT = {}
            for kd in range(ND):
                sb_ = kd % 2
                vq = B.V(lambda: nc.vector.tensor_tensor(
                    sqx[:, sb_ * 512:(sb_ + 1) * 512],
                    xn2[:, kd * T + ch * 512: kd * T + (ch + 1) * 512],
                    xn2[:, kd * T + ch * 512: kd * T + (ch + 1) * 512], ALU.mult),
                    waits=[("T", sqT.get(sb_, 0))])
                tms = B.TT(lambda: nc.tensor.matmul(
                    pnm[0:1, :], nones[:], sqx[:, sb_ * 512:(sb_ + 1) * 512],
                    start=(kd == 0), stop=(kd == ND - 1)),
                    waits=[("V", vq)], tag=True)
                sqT[sb_] = tms
            sr = B.S(lambda: nc.scalar.activation(
                nrow[0:1, :], pnm[0:1, :], AF.Ln,
                bias=small[0:1, 8:9], scale=small[0:1, 10:11]), waits=[("T", tms)])
            sr = B.S(lambda: nc.scalar.activation(
                nrb[0:1, :], nrow[0:1, :], AF.Exp, scale=-0.5))
            trb = B.TT(lambda: nc.tensor.matmul(
                pnr[:], onesrow[:], nrb[0:1, :], start=True, stop=True),
                waits=[("S", sr)])
            for kd in range(ND):
                hv = B.V(lambda: nc.vector.scalar_tensor_tensor(
                    hatc[:, (hbuf * 8 + kd) * 512:(hbuf * 8 + kd + 1) * 512],
                    xn2[:, kd * T + ch * 512: kd * T + (ch + 1) * 512],
                    attnw[:, kd:kd + 1], pnr[:], ALU.mult, ALU.mult),
                    waits=[("T", trb), ("T", hprevT.get(hbuf, 0))])
            nc.tensor.wait_ge(B.sem("V"), hv)
            for fi in range(12):
                isq = fi < 8
                isk = fi in (8, 9)
                buf = (ch * 12 + fi) % 2
                nc.tensor.wait_ge(B.sem("V"), evV.get(buf, 0))
                ps = pmm[:, buf * 512:(buf + 1) * 512]
                for kd in range(ND):
                    tn = B.TT(lambda: nc.tensor.matmul(
                        ps, qkvs[:, (kd * 12 + fi) * 128:(kd * 12 + fi) * 128 + 128],
                        hatc[:, (hbuf * 8 + kd) * 512:(hbuf * 8 + kd + 1) * 512],
                        start=(kd == 0), stop=(kd == ND - 1)), tag=(kd == ND - 1))
                if not (isq or isk):
                    evV[buf] = B.V(lambda: nc.vector.tensor_scalar(
                        vp2[:, (fi - 10) * T + ch * 512:(fi - 10) * T + (ch + 1) * 512],
                        ps, qkvb[:, fi:fi + 1], None, ALU.add), waits=[("T", tn)])
                    continue
                B.V(lambda: nc.vector.tensor_scalar(
                    rawc[:, buf * 512:(buf + 1) * 512], ps, qkvb[:, fi:fi + 1],
                    None, ALU.add), waits=[("T", tn)])
                dst = qp2 if isq else kp2
                dbase = fi * T if isq else (fi - 8) * T
                for b0 in (0, 64):
                    sb0 = slice(b0, b0 + 32)
                    cs = cosq[sb0, ch * 512:(ch + 1) * 512]
                    sn_ = sinq[sb0, ch * 512:(ch + 1) * 512]
                    a1 = rawc[sb0, buf * 512:(buf + 1) * 512]
                    B.V(lambda: nc.vector.tensor_copy(
                        ropes[sb0, 512:1024],
                        rawc[b0 + 32:b0 + 64, buf * 512:(buf + 1) * 512]))
                    B.V(lambda: nc.vector.tensor_tensor(ropes[sb0, 0:512], a1, cs, ALU.mult))
                    B.V(lambda: nc.vector.tensor_tensor(
                        ropes[sb0, 1024:1536], ropes[sb0, 512:1024], sn_, ALU.mult))
                    B.V(lambda: nc.vector.tensor_tensor(
                        dst[sb0, dbase + ch * 512: dbase + (ch + 1) * 512],
                        ropes[sb0, 0:512], ropes[sb0, 1024:1536], ALU.subtract))
                    B.V(lambda: nc.vector.tensor_tensor(ropes[sb0, 0:512], a1, sn_, ALU.mult))
                    B.V(lambda: nc.vector.tensor_tensor(
                        ropes[sb0, 1024:1536], ropes[sb0, 512:1024], cs, ALU.mult))
                    B.V(lambda: nc.vector.tensor_tensor(
                        ropes[sb0, 512:1024],
                        ropes[sb0, 0:512], ropes[sb0, 1024:1536], ALU.add))
                    evV[buf] = B.V(lambda: nc.vector.tensor_copy(
                        dst[b0 + 32:b0 + 64, dbase + ch * 512: dbase + (ch + 1) * 512],
                        ropes[sb0, 512:1024]))
            hprevT[hbuf] = B.n("T")
        tcv = {}
        nc.tensor.wait_ge(B.sem("V"), B.n("V"))
        for pi in range(2):
            for tc in range(NT):
                pbuf = (pi * NT + tc) % 2
                nc.tensor.wait_ge(B.sem("V"), tcv.get(pbuf, 0))
                tn = B.TT(lambda: nc.tensor.transpose(
                    ptv[:, pbuf * 1024: pbuf * 1024 + 128],
                    vp2[:, pi * T + tc * 128: pi * T + (tc + 1) * 128],
                    ident[:]))
                tcv[pbuf] = B.V(lambda: nc.vector.tensor_copy(
                    vT2[:, tc * 256 + pi * 128: tc * 256 + (pi + 1) * 128],
                    ptv[:, pbuf * 1024: pbuf * 1024 + 128]), waits=[("T", tn)])

        if "qkv2" in debug:
            B.dma(nc.gpsimd, "dbg", dbg_tensor("q2", [128, 8 * T]), qp2[:], waits=[("V", None)])
            B.dma(nc.gpsimd, "dbg", dbg_tensor("k2", [128, 2 * T]), kp2[:], waits=[("V", None)])
            B.dma(nc.gpsimd, "dbg", dbg_tensor("v2", [128, 2 * T]), vp2[:], waits=[("V", None)])
        nc.tensor.wait_ge(B.sem("V"), B.n("V"))
        st7i.close()

        with contextlib.ExitStack() as st8b:
            e8 = st8b.enter_context
            kp2d = e8(nc.sbuf_tensor("kp2d", [128, 2 * T], BF16))
            psc = e8(nc.psum_tensor("psc", [128, 2 * 512], F32))
            pav = e8(nc.psum_tensor("pav", [128, 512], F32))
            esc = e8(nc.sbuf_tensor("esc", [128, 2 * 256], BF16))
            attb = e8(nc.sbuf_tensor("attb", [128, 256], BF16))
            attT = e8(nc.sbuf_tensor("attT", [128, 256], BF16))
            arow = e8(nc.sbuf_tensor("arow", [128, 8], F32))
            for tile_ in range(2):
                for p_ in range(2):
                    B.V(lambda: nc.vector.tensor_copy(
                        kp2d[64 * (1 - p_):64 * (1 - p_) + 64, tile_ * T:(tile_ + 1) * T],
                        kp2[64 * p_:64 * p_ + 64, tile_ * T:(tile_ + 1) * T]))
            nc.tensor.wait_ge(B.sem("V"), B.n("V"))
            expS = {}
            pscV = {}
            escV = {}
            attb_gT = 0
            attT_gT = 0
            zevV = 0
            for h in range(16):
                kvh = h // 4
                ktile, kbase = kvh // 2, 64 * (kvh % 2)
                qtile, qbase = h // 2, 64 * (h % 2)
                for qt in range(NT):
                    sb = qt % 2
                    wid = 256 if qt > 0 else 128
                    k0 = (qt - 1) * 128 if qt > 0 else 0
                    nc.tensor.wait_ge(B.sem("S"), expS.get(sb, 0))
                    nc.tensor.wait_ge(B.sem("V"), pscV.get(sb, 0))
                    ps = psc[:, sb * 512: sb * 512 + wid]
                    ksrc = kp2 if kbase == qbase else kp2d
                    nc.tensor.matmul(
                        ps, qp2[qbase:qbase + 64, qtile * T + qt * 128: qtile * T + (qt + 1) * 128],
                        ksrc[qbase:qbase + 64, ktile * T + k0: ktile * T + k0 + wid],
                        start=True, stop=False, tile_position=(qbase, 0))
                    i1 = nc.tensor.matmul(
                        ps, ident[:], mwin[:, 0:wid] if qt > 0 else mwin[:, 128:256],
                        start=False, stop=True)
                    tn = B.inc("T", i1)
                    vm = B.V(lambda: nc.vector.tensor_reduce(
                        arow[:, 0:1], ps, AX.X, ALU.max), waits=[("T", tn)])
                    vm = B.V(lambda: nc.vector.tensor_scalar(
                        arow[:, 1:2], arow[:, 0:1], -0.125, None, ALU.mult))
                    pscV[sb] = vm
                    se = B.S(lambda: nc.scalar.activation(
                        esc[:, sb * 256: sb * 256 + wid], ps, AF.Exp,
                        bias=arow[:, 1:2], scale=0.125),
                        waits=[("V", max(vm, escV.get(sb, 0)))])
                    expS[sb] = se
                    B.V(lambda: nc.vector.tensor_reduce(
                        arow[:, 2:3], esc[:, sb * 256: sb * 256 + wid], AX.X, ALU.add),
                        waits=[("S", se)])
                    ss = B.S(lambda: nc.scalar.activation(
                        arow[:, 3:4], arow[:, 1:2], AF.Exp,
                        bias=sinkb[:, h:h + 1]))
                    B.V(lambda: nc.vector.tensor_tensor(
                        arow[:, 4:5], arow[:, 2:3], arow[:, 3:4], ALU.add),
                        waits=[("S", ss)])
                    B.V(lambda: nc.vector.reciprocal(arow[:, 5:6], arow[:, 4:5]))
                    va = B.V(lambda: nc.vector.tensor_scalar(
                        attb[:, 0:wid], esc[:, sb * 256: sb * 256 + wid],
                        arow[:, 5:6], None, ALU.mult), waits=[("T", attb_gT)])
                    escV[sb] = va
                    nc.tensor.wait_ge(B.sem("V"), va)
                    nblk = wid // 128
                    for kb in range(nblk):
                        tn2 = B.TT(lambda: nc.tensor.transpose(
                            ptv[:, (kb % 2) * 1024 + 512:(kb % 2) * 1024 + 640],
                            attb[:, kb * 128:(kb + 1) * 128], ident[:]),
                            waits=[("V", attT_gT)])
                        vv = B.V(lambda: nc.vector.tensor_copy(
                            attT[:, kb * 128:(kb + 1) * 128],
                            ptv[:, (kb % 2) * 1024 + 512:(kb % 2) * 1024 + 640]),
                            waits=[("T", tn2)])
                    attb_gT = tn2
                    nc.tensor.wait_ge(B.sem("V"), vv)
                    for kb in range(nblk):
                        kc = qt - 1 + kb if qt > 0 else 0
                        tn3 = B.TT(lambda: nc.tensor.matmul(
                            pav[0:64, 0:128],
                            vT2[:, kc * 256 + kvh * 64: kc * 256 + (kvh + 1) * 64],
                            attT[:, kb * 128:(kb + 1) * 128],
                            start=(kb == 0), stop=(kb == nblk - 1)),
                            waits=([("V", zevV)] if kb == 0 else ()),
                            tag=(kb == nblk - 1))
                    attT_gT = tn3
                    zevV = B.V(lambda: nc.vector.tensor_copy(
                        z2[qbase:qbase + 64, qtile * T + qt * 128: qtile * T + (qt + 1) * 128],
                        pav[0:64, 0:128]), waits=[("T", tn3)])

        if "z2" in debug:
            B.dma(nc.gpsimd, "dbg", dbg_tensor("z2d", [128, 8 * T]), z2[:], waits=[("V", None)])

        with contextlib.ExitStack() as st9:
            e9 = st9.enter_context
            projs = e9(nc.sbuf_tensor("projs", [128, 64 * 128], BF16))
            projb = e9(nc.sbuf_tensor("projb", [128, 8], F32))
            B.dma(nc.gpsimd, "ldw2",
                  projs[:].rearrange("p (k f c) -> p k f c", k=8, f=8),
                  projw_h.rearrange("(k p) (f c) -> p k f c", p=128, c=128),
                  waits=[("T", B.n("T")), ("S", B.n("S")), ("V", B.n("V"))])
            B.dma(nc.sync, "ldcf", projb[:], projb_h,
                  waits=[("T", B.n("T")), ("S", B.n("S")), ("V", B.n("V"))])
            B.wait(nc.tensor, "ldw2")
            B.wait(nc.tensor, "ldcf")
            xmc = e9(nc.sbuf_tensor("xmc", [128, 8 * 512], F32))
            sqc = e9(nc.sbuf_tensor("sqc", [128, 8 * 512], BF16))
            rrow = e9(nc.sbuf_tensor("rrow", [1, 512], F32))
            rrb = e9(nc.sbuf_tensor("rrb", [1, 512], BF16))
            onescol = e9(nc.sbuf_tensor("onescol", [128, 1], BF16))
            pms = e9(nc.psum_tensor("pms", [128, 512], F32))
            prb = e9(nc.psum_tensor("prb", [128, 512], F32))
            B.V(lambda: nc.vector.memset(onescol[:], 1.0))
            B.V(lambda: nc.vector.memset(small[0:1, 9:10], 9.765625e-4))
            prev_dma = 0
            prev_h2 = 0
            for ch in range(NCH):
                for dj in range(ND):
                    buf = dj % 2
                    ps = pmm[:, buf * 512:(buf + 1) * 512]
                    nc.tensor.wait_ge(B.sem("V"), B.n("V"))
                    for fk in range(ND):
                        tn = B.TT(lambda: nc.tensor.matmul(
                            ps, projs[:, (fk * 8 + dj) * 128:(fk * 8 + dj) * 128 + 128],
                            z2[:, fk * T + ch * 512: fk * T + (ch + 1) * 512],
                            start=(fk == 0), stop=(fk == ND - 1)), tag=(fk == ND - 1))
                    B.V(lambda: nc.vector.scalar_tensor_tensor(
                        xmc[:, dj * 512:(dj + 1) * 512], ps, projb[:, dj:dj + 1],
                        xn2[:, dj * T + ch * 512: dj * T + (ch + 1) * 512],
                        ALU.add, ALU.add), waits=[("T", tn), ("xms", prev_dma)])
                    vv = B.V(lambda: nc.vector.tensor_tensor(
                        sqc[:, dj * 512:(dj + 1) * 512],
                        xmc[:, dj * 512:(dj + 1) * 512],
                        xmc[:, dj * 512:(dj + 1) * 512], ALU.mult))
                prev_dma = B.dma(nc.sync, "xms",
                                 xmid_h[:].rearrange("p (a c) -> p a c", a=8)[:, :, ch * 512:(ch + 1) * 512],
                                 xmc[:].rearrange("p (a c) -> p a c", a=8),
                                 waits=[("V", vv)])
                nc.tensor.wait_ge(B.sem("V"), vv)
                for kd in range(ND):
                    tms = B.TT(lambda: nc.tensor.matmul(
                        pms[0:1, :], onescol[:],
                        sqc[:, kd * 512:(kd + 1) * 512],
                        start=(kd == 0), stop=(kd == ND - 1)), tag=(kd == ND - 1))
                sr = B.S(lambda: nc.scalar.activation(
                    rrow[0:1, :], pms[0:1, :], AF.Ln,
                    bias=small[0:1, 8:9], scale=small[0:1, 9:10]), waits=[("T", tms)])
                sr = B.S(lambda: nc.scalar.activation(
                    rrb[0:1, :], rrow[0:1, :], AF.Exp, scale=-0.5))
                trb = B.TT(lambda: nc.tensor.matmul(
                    prb[:], onesrow[:], rrb[0:1, :], start=True, stop=True),
                    waits=[("S", sr)])
                for dj in range(ND):
                    prev_h2 = B.V(lambda: nc.vector.scalar_tensor_tensor(
                        h2[:, dj * T + ch * 512: dj * T + (ch + 1) * 512],
                        xmc[:, dj * 512:(dj + 1) * 512], ffnw[:, dj:dj + 1],
                        prb[:], ALU.mult, ALU.mult), waits=[("T", trb)])
                nc.tensor.wait_ge(B.sem("V"), prev_h2)
                nc.tensor.wait_ge(B.sem("S"), B.n("S"))

    st69.close()
    if "h2" in debug:
        B.dma(nc.gpsimd, "dbg", dbg_tensor("h2d", [128, 8 * T]), h2[:], waits=[("V", None)])

    with contextlib.ExitStack() as s11:
        e11 = s11.enter_context
        gatews = e11(nc.sbuf_tensor("gatews", [128, 8 * 32], BF16))
        gateb = e11(nc.sbuf_tensor("gateb", [32, 1], F32))
        identf = e11(nc.sbuf_tensor("identf", [128, 128], F32))
        glog = e11(nc.sbuf_tensor("glog", [32, T], F32))
        glT = e11(nc.sbuf_tensor("glT", [128, NT * 32], F32))
        gw1 = e11(nc.sbuf_tensor("gw1", [128, 4 * 32], F32))
        gcol = e11(nc.sbuf_tensor("gcol", [128, 8], F32))
        pgl = e11(nc.psum_tensor("pgl", [128, 2 * 512], F32))
        gw_ = [("T", B.n("T")), ("V", B.n("V")), ("S", B.n("S")), ("dbg", B.n("dbg"))]
        B.dma(nc.gpsimd, "ldg", gatews[:].rearrange("p (k c) -> p k c", k=8),
              gatew_h.rearrange("(k p) c -> p k c", p=128), waits=gw_)
        B.dma(nc.sync, "ldgf", gateb[:], gateb_h, waits=gw_)
        B.dma(nc.sync, "ldgf", identf[:], ident_h)
        B.wait(nc.tensor, "ldg")
        B.wait(nc.tensor, "ldgf")
        evG = {}
        for ch in range(NCH):
            buf = ch % 2
            ps = pgl[0:32, buf * 512:(buf + 1) * 512]
            nc.tensor.wait_ge(B.sem("V"), evG.get(buf, 0))
            for kd in range(ND):
                tn = B.TT(lambda: nc.tensor.matmul(
                    ps, gatews[:, kd * 32:(kd + 1) * 32],
                    h2[:, kd * T + ch * 512: kd * T + (ch + 1) * 512],
                    start=(kd == 0), stop=(kd == ND - 1)), tag=(kd == ND - 1))
            evG[buf] = B.V(lambda: nc.vector.tensor_scalar(
                glog[:, ch * 512:(ch + 1) * 512], ps, gateb[:, 0:1], None, ALU.add),
                waits=[("T", tn)])
        trG = {}
        for tt in range(NT):
            pb = tt % 2
            nc.tensor.wait_ge(B.sem("V"), B.n("V"))
            tn = B.TT(lambda: nc.tensor.transpose(
                pgl[:, pb * 512: pb * 512 + 32].bitcast(F32) if False else pgl[:, pb * 512: pb * 512 + 32],
                glog[:, tt * 128:(tt + 1) * 128], identf[0:32, 0:32]),
                waits=[("V", trG.get(pb, 0))])
            vv = B.V(lambda: nc.vector.tensor_copy(
                glT[:, tt * 32:(tt + 1) * 32], pgl[:, pb * 512: pb * 512 + 32]),
                waits=[("T", tn)])
            trG[pb] = vv
        for tt in range(NT):
            g0 = glT[:, tt * 32:(tt + 1) * 32]
            B.V(lambda: nc.vector.tensor_reduce(gcol[:, 0:1], g0, AX.X, ALU.max))
            cur = g0
            for rnd in range(3):
                nxt = gw1[:, ((rnd + 1) % 2) * 32:((rnd + 1) % 2) * 32 + 32]
                B.V(lambda: nc.vector.tensor_scalar(
                    gw1[:, 64:96], cur, gcol[:, rnd:rnd + 1], None, ALU.is_ge))
                B.V(lambda: nc.vector.tensor_scalar(
                    gw1[:, 96:128], gw1[:, 64:96], 1e30, None, ALU.mult))
                B.V(lambda: nc.vector.tensor_tensor(nxt, cur, gw1[:, 96:128], ALU.subtract))
                B.V(lambda: nc.vector.tensor_reduce(
                    gcol[:, rnd + 1:rnd + 2], nxt, AX.X, ALU.max))
                cur = nxt
            B.V(lambda: nc.vector.tensor_scalar(
                gw1[:, 64:96], g0, gcol[:, 3:4], None, ALU.is_ge))
            vv = B.V(lambda: nc.vector.tensor_scalar(
                gcol[:, 4:5], gcol[:, 0:1], -1.0, None, ALU.mult))
            sv = B.S(lambda: nc.scalar.activation(
                gw1[:, 96:128], g0, AF.Exp, bias=gcol[:, 4:5]), waits=[("V", vv)])
            B.V(lambda: nc.vector.tensor_tensor(
                gw1[:, 0:32], gw1[:, 64:96], gw1[:, 96:128], ALU.mult),
                waits=[("S", sv)])
            B.V(lambda: nc.vector.tensor_reduce(gcol[:, 5:6], gw1[:, 0:32], AX.X, ALU.add))
            B.V(lambda: nc.vector.reciprocal(gcol[:, 6:7], gcol[:, 5:6]))
            vv = B.V(lambda: nc.vector.tensor_scalar(
                gw1[:, 32:64].bitcast(BF16)[:, 0:32] if False else gw1[:, 32:64],
                gw1[:, 0:32], gcol[:, 6:7], None, ALU.mult))
            pb = tt % 2
            nc.tensor.wait_ge(B.sem("V"), vv)
            tn = B.TT(lambda: nc.tensor.transpose(
                pgl[0:32, pb * 512 + 64: pb * 512 + 64 + 128],
                gw1[:, 32:64], identf[:]))
            vv = B.V(lambda: nc.vector.tensor_copy(
                cTb[0:32, tt * 128:(tt + 1) * 128], pgl[0:32, pb * 512 + 64: pb * 512 + 64 + 128]),
                waits=[("T", tn)])
            trG[pb] = vv

    if "comb" in debug:
        B.dma(nc.gpsimd, "dbg", dbg_tensor("comb", [32, T]), cTb[:], waits=[("V", None)])

    with contextlib.ExitStack() as s12:
        e12 = s12.enter_context
        y2a = e12(nc.sbuf_tensor("y2a", [128, 8 * 1024], F32))
        y2s = e12(nc.sbuf_tensor("y2s", [128, 512], F32))
        acts = e12(nc.sbuf_tensor("acts", [128, 16 * 512], BF16))
        ascr = e12(nc.sbuf_tensor("ascr", [128, 2 * 512], BF16))
        wupP = e12(nc.sbuf_tensor("wupP", [128, 2 * 8 * 1024], BF16))
        wdnP = e12(nc.sbuf_tensor("wdnP", [128, 2 * 8 * 1024], BF16))
        cbs = e12(nc.sbuf_tensor("cbs", [128, 2 * 512], BF16))
        oh32 = e12(nc.sbuf_tensor("oh32", [32, 32 * 128], BF16))
        xml = e12(nc.sbuf_tensor("xml", [128, 8 * 512], F32))
        g2c = e12(nc.sbuf_tensor("g2c", [128, 8 * 512], F32))
        pup = e12(nc.psum_tensor("pup", [128, 4 * 512], F32))
        pdn = e12(nc.psum_tensor("pdn", [128, 2 * 512], F32))
        pcb = e12(nc.psum_tensor("pcb", [128, 512], F32))
        B.dma(nc.gpsimd, "ldg", oh32[:], oh32_h,
              waits=[("T", B.n("T")), ("V", B.n("V")), ("S", B.n("S")),
                     ("dbg", B.n("dbg"))])
        B.wait(nc.tensor, "ldg")
        nc.tensor.wait_ge(B.sem("V"), B.n("V"))
        nc.tensor.wait_ge(B.sem("S"), B.n("S"))
        T0 = B.n("T")
        lastT = {0: None, 1: None}
        upS = {}
        pcbV = B.n("V")
        ascrV = B.n("V")
        wup_r = wup_h.rearrange("e (k p) c -> e p k c", p=128)
        wdn_r = wdn_h.rearrange("e (k p) c -> e p k c", p=128)
        for th in range(2):
            tb0 = th * 2
            B.V(lambda: nc.vector.memset(y2a[:], 0.0))
            for e in range(32):
                ebuf = e % 2
                gw = [("T", T0 if lastT[ebuf] is None else lastT[ebuf])]
                B.dma(nc.gpsimd, "lwu%d" % ebuf,
                      wupP[:, ebuf * 8192: ebuf * 8192 + 4096].rearrange("p (k c) -> p k c", k=8),
                      wup_r[e][:, :, 0:512], waits=gw)
                B.dma(nc.gpsimd, "lwg%d" % ebuf,
                      wupP[:, ebuf * 8192 + 4096:(ebuf + 1) * 8192].rearrange("p (k c) -> p k c", k=8),
                      wup_r[e][:, :, 1024:1536], waits=gw)
                B.dma(nc.gpsimd, "lwd%d" % ebuf,
                      wdnP[:, ebuf * 8192:(ebuf + 1) * 8192].rearrange("p (k c) -> p k c", k=8),
                      wdn_r[e], waits=gw)
                for fp in range(2):
                    if fp == 1:
                        guardT = B.n("T")
                        B.dma(nc.gpsimd, "lwu%d" % ebuf,
                              wupP[:, ebuf * 8192: ebuf * 8192 + 4096].rearrange("p (k c) -> p k c", k=8),
                              wup_r[e][:, :, 512:1024], waits=[("T", guardT)])
                        B.dma(nc.gpsimd, "lwg%d" % ebuf,
                              wupP[:, ebuf * 8192 + 4096:(ebuf + 1) * 8192].rearrange("p (k c) -> p k c", k=8),
                              wup_r[e][:, :, 1536:2048], waits=[("T", guardT)])
                    B.wait(nc.tensor, "lwu%d" % ebuf)
                    B.wait(nc.tensor, "lwg%d" % ebuf)
                    for ch2 in range(2):
                        ch = tb0 + ch2
                        if fp == 0:
                            tn = B.TT(lambda: nc.tensor.matmul(
                                pcb[:], oh32[:, e * 128:(e + 1) * 128],
                                cTb[0:32, ch * 512:(ch + 1) * 512], start=True, stop=True),
                                waits=[("V", pcbV)])
                            pcbV = B.V(lambda: nc.vector.tensor_copy(
                                cbs[:, ch2 * 512:(ch2 + 1) * 512], pcb[:]), waits=[("T", tn)])
                        for i in range(4):
                            ft = fp * 4 + i
                            psA = pup[:, (i % 2) * 512:(i % 2) * 512 + 512]
                            psB = pup[:, 1024 + (i % 2) * 512: 1024 + (i % 2) * 512 + 512]
                            ws = upS.get(i % 2, 0)
                            if ws:
                                nc.tensor.wait_ge(B.sem("S"), ws)
                            for kd in range(ND):
                                nc.tensor.matmul(
                                    psA, wupP[:, ebuf * 8192 + kd * 512 + i * 128:
                                              ebuf * 8192 + kd * 512 + i * 128 + 128],
                                    h2[:, kd * T + ch * 512: kd * T + (ch + 1) * 512],
                                    start=(kd == 0), stop=(kd == ND - 1))
                                tn = B.TT(lambda: nc.tensor.matmul(
                                    psB, wupP[:, ebuf * 8192 + 4096 + kd * 512 + i * 128:
                                              ebuf * 8192 + 4096 + kd * 512 + i * 128 + 128],
                                    h2[:, kd * T + ch * 512: kd * T + (ch + 1) * 512],
                                    start=(kd == 0), stop=(kd == ND - 1)), tag=(kd == ND - 1))
                            sg = B.S(lambda: nc.scalar.activation(
                                ascr[:, 0:512], psB, AF.Silu),
                                waits=[("T", tn), ("V", ascrV)])
                            uc = B.S(lambda: nc.scalar.copy(ascr[:, 512:1024], psA))
                            upS[i % 2] = uc
                            ascrV = B.V(lambda: nc.vector.tensor_tensor(
                                acts[:, (ch2 * 8 + ft) * 512:(ch2 * 8 + ft + 1) * 512],
                                ascr[:, 0:512], ascr[:, 512:1024],
                                ALU.mult), waits=[("S", uc)])
                B.wait(nc.tensor, "lwd%d" % ebuf)
                pdnV = {}
                for ch2 in range(2):
                    for ft in range(8):
                        B.V(lambda: nc.vector.tensor_tensor(
                            acts[:, (ch2 * 8 + ft) * 512:(ch2 * 8 + ft + 1) * 512],
                            acts[:, (ch2 * 8 + ft) * 512:(ch2 * 8 + ft + 1) * 512],
                            cbs[:, ch2 * 512:(ch2 + 1) * 512], ALU.mult))
                    nc.tensor.wait_ge(B.sem("V"), B.n("V"))
                    for dj in range(ND):
                        dsl = dj % 2
                        nc.tensor.wait_ge(B.sem("V"), pdnV.get(dsl, 0))
                        for ft in range(8):
                            tn2 = B.TT(lambda: nc.tensor.matmul(
                                pdn[:, dsl * 512: dsl * 512 + 512],
                                wdnP[:, ebuf * 8192 + ft * 1024 + dj * 128:
                                     ebuf * 8192 + ft * 1024 + dj * 128 + 128],
                                acts[:, (ch2 * 8 + ft) * 512:(ch2 * 8 + ft + 1) * 512],
                                start=(ft == 0), stop=(ft == 7)), tag=(ft == 7))
                        pdnV[dsl] = B.V(lambda: nc.vector.tensor_tensor(
                            y2a[:, dj * 1024 + ch2 * 512: dj * 1024 + (ch2 + 1) * 512],
                            y2a[:, dj * 1024 + ch2 * 512: dj * 1024 + (ch2 + 1) * 512],
                            pdn[:, dsl * 512: dsl * 512 + 512], ALU.add),
                            waits=[("T", tn2)])
                nc.tensor.wait_ge(B.sem("V"), B.n("V"))
                lastT[ebuf] = B.n("T")
            for ch2 in range(2):
                ch = tb0 + ch2
                dv = B.dma(nc.sync, "xml", xml[:].rearrange("p (a c) -> p a c", a=8),
                           xmid_h[:].rearrange("p (a c) -> p a c", a=8)[:, :, ch * 512:(ch + 1) * 512],
                           waits=[("V", B.n("V")), ("g2s", None), ("xms", None),
                                  ("S", B.n("S"))])
                B.wait(nc.vector, "xml", dv)
                for dj in range(ND):
                    vv = B.V(lambda: nc.vector.tensor_tensor(
                        g2c[:, dj * 512:(dj + 1) * 512],
                        xml[:, dj * 512:(dj + 1) * 512],
                        y2a[:, dj * 1024 + ch2 * 512: dj * 1024 + (ch2 + 1) * 512],
                        ALU.add))
                    sv = B.S(lambda: nc.scalar.activation(
                        xml[:, dj * 512:(dj + 1) * 512],
                        g2c[:, dj * 512:(dj + 1) * 512], AF.Silu), waits=[("V", vv)])
                B.dma(nc.sync, "g2s",
                      g2_h.rearrange("p (a t) -> p a t", a=8)[:, :, ch * 512:(ch + 1) * 512],
                      xml[:].rearrange("p (a c) -> p a c", a=8),
                      waits=[("S", B.n("S"))])
        B.wait(nc.sync, "g2s")
        B.wait(nc.sync, "xms")




def build_phase_b(debug=()):
    nc = bass.Bass("TRN2", target_bir_lowering=False)
    B = Build(nc)

    y_h = nc.dram_tensor("yb", [T, D], F32, kind="ExternalInput").ap()
    g2_h = nc.dram_tensor("g2b", [T, D], F32, kind="ExternalInput").ap()
    wq_h = nc.dram_tensor("bWq", [2, D, D], F32, kind="ExternalInput").ap()
    wk_h = nc.dram_tensor("bWk", [2, D, D], F32, kind="ExternalInput").ap()
    wv_h = nc.dram_tensor("bWv", [2, D, D], F32, kind="ExternalInput").ap()
    wb_h = nc.dram_tensor("bWb", [2, D, 8], F32, kind="ExternalInput").ap()
    wo_h = nc.dram_tensor("bWo", [2, D, D], F32, kind="ExternalInput").ap()
    wg_h = nc.dram_tensor("bWg", [2, D, 5632], F32, kind="ExternalInput").ap()
    wd_h = nc.dram_tensor("bWd", [2, 2816, D], F32, kind="ExternalInput").ap()
    n1w_h = nc.dram_tensor("n1w", [128, 16], F32, kind="ExternalInput").ap()
    n2w_h = nc.dram_tensor("n2w", [128, 16], F32, kind="ExternalInput").ap()
    onw_h = nc.dram_tensor("onw", [2, 128], F32, kind="ExternalInput").ap()
    fnw_h = nc.dram_tensor("fnw", [128, 8], F32, kind="ExternalInput").ap()
    ident_h = nc.dram_tensor("identity_i", [128, 128], F32, kind="ExternalInput").ap()
    onesrow_h = nc.dram_tensor("onesrow_i", [1, 128], F32, kind="ExternalInput").ap()
    mU_h = nc.dram_tensor("maskU_i", [128, 128], F32, kind="ExternalInput").ap()
    mP_h = nc.dram_tensor("maskP_i", [128, 128], F32, kind="ExternalInput").ap()
    oh8_h = nc.dram_tensor("oh8_i", [8, 8 * 128], F32, kind="ExternalInput").ap()
    ohc8_h = nc.dram_tensor("ohc8_i", [128, 64], F32, kind="ExternalInput").ap()

    out_h = nc.dram_tensor("outb", [T, D], F32, kind="ExternalOutput").ap()

    def dbg_tensor(name, shape):
        return nc.dram_tensor("dbg_" + name, shape, F32, kind="ExternalOutput").ap()

    with contextlib.ExitStack() as _st:
        en = _st.enter_context
        xR = en(nc.sbuf_tensor("xR", [128, 8 * T], F32))
        ident = en(nc.sbuf_tensor("identB", [128, 128], BF16))
        identf = en(nc.sbuf_tensor("identfB", [128, 128], F32))
        onesrow = en(nc.sbuf_tensor("onesrowB", [1, 128], BF16))
        mU = en(nc.sbuf_tensor("mUS", [128, 128], BF16))
        mP = en(nc.sbuf_tensor("mPS", [128, 128], BF16))
        oh8 = en(nc.sbuf_tensor("oh8S", [8, 8 * 128], BF16))
        ohc8 = en(nc.sbuf_tensor("ohc8S", [128, 64], BF16))
        n1w = en(nc.sbuf_tensor("n1wS", [128, 16], F32))
        n2w = en(nc.sbuf_tensor("n2wS", [128, 16], F32))
        fnw = en(nc.sbuf_tensor("fnwS", [128, 8], F32))
        onwb = en(nc.sbuf_tensor("onwbS", [128, 2 * 128], BF16))
        smallB = en(nc.sbuf_tensor("smallBS", [128, 16], F32))
        block = en(nc.Block())

        @block.gpsimd
        def _(gpsimd):
            _emit_phase_b(
                (nc, B, debug, y_h, g2_h, wq_h, wk_h, wv_h, wb_h, wo_h, wg_h,
                 wd_h, n1w_h, n2w_h, onw_h, fnw_h, ident_h, onesrow_h, mU_h,
                 mP_h, oh8_h, ohc8_h, out_h, dbg_tensor, xR, ident, identf,
                 onesrow, mU, mP, oh8, ohc8, n1w, n2w, fnw, onwb, smallB))
    return nc


def _rms_chunk_b(nc, B, xR, ch, dst_fn, wcol, eps_ap, scale_ap, sqx, nrow, nrb,
                 nones, onesrow, pE):
    sqT = {}
    tms = 0
    for kd in range(ND):
        sb_ = kd % 2
        vq = B.V(lambda: nc.vector.tensor_tensor(
            sqx[:, sb_ * 512:(sb_ + 1) * 512],
            xR[:, kd * T + ch * 512: kd * T + (ch + 1) * 512],
            xR[:, kd * T + ch * 512: kd * T + (ch + 1) * 512], ALU.mult),
            waits=[("T", sqT.get(sb_, 0))])
        tms = B.TT(lambda: nc.tensor.matmul(
            pE[0:1, :], nones[:], sqx[:, sb_ * 512:(sb_ + 1) * 512],
            start=(kd == 0), stop=(kd == ND - 1)),
            waits=[("V", vq)], tag=True)
        sqT[sb_] = tms
    sr = B.S(lambda: nc.scalar.activation(
        nrow[0:1, :], pE[0:1, :], AF.Ln, bias=eps_ap, scale=scale_ap),
        waits=[("T", tms)])
    sr = B.S(lambda: nc.scalar.activation(
        nrb[0:1, :], nrow[0:1, :], AF.Exp, scale=-0.5))
    trb = B.TT(lambda: nc.tensor.matmul(
        pE[:], onesrow[:], nrb[0:1, :], start=True, stop=True),
        waits=[("S", sr)])
    hv = 0
    for kd in range(ND):
        hv = B.V(lambda: nc.vector.scalar_tensor_tensor(
            dst_fn(kd),
            xR[:, kd * T + ch * 512: kd * T + (ch + 1) * 512],
            wcol[:, kd:kd + 1], pE[:], ALU.mult, ALU.mult),
            waits=[("T", trb)])
    nc.tensor.wait_ge(B.sem("V"), hv)
    return hv


def _emit_phase_b(E):
    (nc, B, debug, y_h, g2_h, wq_h, wk_h, wv_h, wb_h, wo_h, wg_h,
     wd_h, n1w_h, n2w_h, onw_h, fnw_h, ident_h, onesrow_h, mU_h,
     mP_h, oh8_h, ohc8_h, out_h, dbg_tensor, xR, ident, identf,
     onesrow, mU, mP, oh8, ohc8, n1w, n2w, fnw, onwb, smallB) = E

    for t_, s_ in ((ident, ident_h), (onesrow, onesrow_h), (mU, mU_h),
                   (mP, mP_h), (oh8, oh8_h), (ohc8, ohc8_h)):
        B.dma(nc.gpsimd, "ldc", t_[:], s_)
    for t_, s_ in ((identf, ident_h), (n1w, n1w_h), (n2w, n2w_h), (fnw, fnw_h)):
        B.dma(nc.sync, "ldcf", t_[:], s_)
    B.wait(nc.tensor, "ldc")
    B.wait(nc.tensor, "ldcf")
    B.wait(nc.vector, "ldcf")
    B.V(lambda: nc.vector.memset(smallB[:, 0:1], 1e-5))
    B.V(lambda: nc.vector.memset(smallB[:, 1:2], 1e-6))
    B.V(lambda: nc.vector.memset(smallB[:, 2:3], 9.765625e-4))
    B.V(lambda: nc.vector.memset(smallB[:, 3:4], 7.8125e-3))

    _emit_ln_dt(nc, B, y_h, xR, smallB[:, 0:1], identf, F32)

    with contextlib.ExitStack() as stb:
        eb = stb.enter_context
        qF = eb(nc.sbuf_tensor("qF", [128, 8 * T], BF16))
        kF = eb(nc.sbuf_tensor("kF", [128, 8 * T], BF16))
        vF = eb(nc.sbuf_tensor("vF", [128, 8 * T], BF16))
        ws = eb(nc.sbuf_tensor("wsB", [128, 64 * 128], BF16))
        wbs = eb(nc.sbuf_tensor("wbsB", [128, 8 * 8], BF16))
        hc = eb(nc.sbuf_tensor("hcB", [128, 8 * 512], BF16))
        sqx = eb(nc.sbuf_tensor("sqxB", [128, 2 * 512], BF16))
        bsc = eb(nc.sbuf_tensor("bscB", [8, 512], BF16))
        nrow = eb(nc.sbuf_tensor("nrowB", [1, 512], F32))
        nrb = eb(nc.sbuf_tensor("nrbB", [1, 512], BF16))
        nones = eb(nc.sbuf_tensor("nonesB", [128, 1], BF16))
        betaT = eb(nc.sbuf_tensor("betaT", [128, NT * 8], F32))
        rsq = eb(nc.sbuf_tensor("rsqB", [8, 2 * T], BF16))
        qn = eb(nc.sbuf_tensor("qnB", [128, 128], BF16))
        kn = eb(nc.sbuf_tensor("knB", [128, 128], BF16))
        mbeta = eb(nc.sbuf_tensor("mbetaB", [128, 1024], BF16))
        u0 = hc[:, 0:1024]
        dch = hc[:, 1024:2048]
        t1s = hc[:, 2048:3072]
        aU = hc[:, 3072:4096]
        pmS = mbeta
        t2s = dch
        osb = u0
        orr = eb(nc.sbuf_tensor("orrB", [128, 8], F32))
        orb = eb(nc.sbuf_tensor("orbB", [128, 8], F32))
        xscr = eb(nc.sbuf_tensor("xscrB", [128, 512], F32))
        pA = eb(nc.psum_tensor("pA", [128, 1024], F32))
        pB = eb(nc.psum_tensor("pB", [128, 1024], F32))
        pC = eb(nc.psum_tensor("pC", [128, 1024], F32))
        pD = eb(nc.psum_tensor("pD", [128, 1024], BF16))
        pE = eb(nc.psum_tensor("pE", [128, 512], F32))
        B.V(lambda: nc.vector.memset(nones[:], 1.0))

        onrow = sqx[0:1, 0:256]
        B.dma(nc.gpsimd, "ldc", onrow.rearrange("o (l c) -> o l c", l=2),
              onw_h.rearrange("(o l) c -> o l c", o=1))
        for l in range(2):
            tn = B.TT(lambda: nc.tensor.matmul(
                pE[:, 0:128], onesrow[:], onrow[0:1, l * 128:(l + 1) * 128],
                start=True, stop=True),
                waits=[("ldc", None), ("V", B.n("V"))])
            B.V(lambda: nc.vector.tensor_copy(
                onwb[:, l * 128:(l + 1) * 128], pE[:, 0:128]), waits=[("T", tn)])

        for l in range(2):
            for which, (w_h, dstF) in enumerate(((wq_h, qF), (wk_h, kF), (wv_h, vF))):
                B.dma(nc.gpsimd, "lwB",
                      ws[:].rearrange("p (kk f c) -> p kk f c", kk=8, c=128),
                      w_h[l].rearrange("(kk p) (f c) -> p kk f c", p=128, c=128),
                      waits=[("T", B.n("T")), ("V", B.n("V")), ("S", B.n("S"))])
                B.wait(nc.tensor, "lwB")
                evV = {}
                for ch in range(NCH):
                    _rms_chunk_b(nc, B, xR, ch,
                                 lambda kd: hc[:, kd * 512:(kd + 1) * 512],
                                 n1w[:, l * 8:l * 8 + 8],
                                 smallB[0:1, 1:2], smallB[0:1, 2:3],
                                 sqx, nrow, nrb, nones, onesrow, pE)
                    for fi in range(ND):
                        buf = (ch * ND + fi) % 2
                        nc.tensor.wait_ge(B.sem("S"), evV.get(buf, 0))
                        ps = pA[:, buf * 512:(buf + 1) * 512]
                        for kd in range(ND):
                            tn = B.TT(lambda: nc.tensor.matmul(
                                ps, ws[:, (kd * 8 + fi) * 128:(kd * 8 + fi) * 128 + 128],
                                hc[:, kd * 512:(kd + 1) * 512],
                                start=(kd == 0), stop=(kd == ND - 1)), tag=(kd == ND - 1))
                        evV[buf] = B.S(lambda: nc.scalar.activation(
                            dstF[:, fi * T + ch * 512: fi * T + (ch + 1) * 512],
                            ps, AF.Silu), waits=[("T", tn)])
                nc.tensor.wait_ge(B.sem("S"), B.n("S"))
            B.dma(nc.gpsimd, "lwB", wbs[:].rearrange("p (kk c) -> p kk c", kk=8),
                  wb_h[l].rearrange("(kk p) c -> p kk c", p=128),
                  waits=[("T", B.n("T")), ("S", B.n("S"))])
            B.wait(nc.tensor, "lwB")
            for ch in range(NCH):
                _rms_chunk_b(nc, B, xR, ch,
                             lambda kd: hc[:, kd * 512:(kd + 1) * 512],
                             n1w[:, l * 8:l * 8 + 8],
                             smallB[0:1, 1:2], smallB[0:1, 2:3],
                             sqx, nrow, nrb, nones, onesrow, pE)
                for kd in range(ND):
                    tn = B.TT(lambda: nc.tensor.matmul(
                        pA[0:8, 0:512], wbs[:, kd * 8:(kd + 1) * 8],
                        hc[:, kd * 512:(kd + 1) * 512],
                        start=(kd == 0), stop=(kd == ND - 1)),
                        waits=([("V", B.n("V"))] if kd == 0 else ()),
                        tag=(kd == ND - 1))
                sv = B.S(lambda: nc.scalar.activation(
                    bsc[0:8, :], pA[0:8, 0:512], AF.Sigmoid), waits=[("T", tn)])
                for tb in range(4):
                    tt = ch * 4 + tb
                    tn = B.TT(lambda: nc.tensor.transpose(
                        pD[0:128, 0:8], bsc[0:8, tb * 128:(tb + 1) * 128],
                        ident[0:8, 0:8]),
                        waits=[("S", sv), ("V", B.n("V"))])
                    B.V(lambda: nc.vector.tensor_scalar(
                        betaT[:, tt * 8:(tt + 1) * 8], pD[0:128, 0:8],
                        2.0, None, ALU.mult), waits=[("T", tn)])
            nc.tensor.wait_ge(B.sem("V"), B.n("V"))

            if l == 0 and "bqkv" in debug:
                B.dma(nc.gpsimd, "dbg", dbg_tensor("bq", [128, 8 * T]), qF[:],
                      waits=[("S", None), ("V", None)])
                B.dma(nc.gpsimd, "dbg", dbg_tensor("bk", [128, 8 * T]), kF[:], waits=[("S", None)])
                B.dma(nc.gpsimd, "dbg", dbg_tensor("bv", [128, 8 * T]), vF[:], waits=[("S", None)])
                B.dma(nc.gpsimd, "dbg", dbg_tensor("bbt", [128, NT * 8]), betaT[:],
                      waits=[("V", None)])
            sqg = {}
            for w2, srcF in ((0, qF), (1, kF)):
                for ch in range(NCH):
                    for h in range(8):
                        sb_ = h % 2
                        vq = B.V(lambda: nc.vector.tensor_tensor(
                            sqx[:, sb_ * 512:(sb_ + 1) * 512],
                            srcF[:, h * T + ch * 512: h * T + (ch + 1) * 512],
                            srcF[:, h * T + ch * 512: h * T + (ch + 1) * 512],
                            ALU.mult), waits=[("T", sqg.get(sb_, 0))])
                        tn = B.TT(lambda: nc.tensor.matmul(
                            pE[0:8, :], ohc8[:, h * 8:(h + 1) * 8],
                            sqx[:, sb_ * 512:(sb_ + 1) * 512],
                            start=(h == 0), stop=(h == 7)),
                            waits=[("V", vq)], tag=True)
                        sqg[sb_] = tn
                    sv = B.S(lambda: nc.scalar.activation(
                        bsc[0:8, :], pE[0:8, :], AF.Ln, bias=smallB[0:8, 1:2]),
                        waits=[("T", tn)])
                    sv = B.S(lambda: nc.scalar.activation(
                        rsq[0:8, w2 * T + ch * 512: w2 * T + (ch + 1) * 512],
                        bsc[0:8, :], AF.Exp, scale=-0.5))
                    nc.tensor.wait_ge(B.sem("S"), sv)

            B.dma(nc.gpsimd, "lwB",
                  ws[:].rearrange("p (kk f c) -> p kk f c", kk=8, c=128),
                  wo_h[l].rearrange("(kk p) (f c) -> p kk f c", p=128, c=128),
                  waits=[("T", B.n("T")), ("S", B.n("S"))])
            B.wait(nc.tensor, "lwB")

            def app(src, dst_sbuf, bview, scr, extra=None, op=None):
                for h in range(8):
                    tn = B.TT(lambda: nc.tensor.matmul(
                        pA[:, h * 128:(h + 1) * 128], aU[:, h * 128:(h + 1) * 128],
                        src[:, h * 128:(h + 1) * 128], start=True, stop=True),
                        waits=([("V", B.n("V"))] if h == 0 else ()),
                        tag=(h == 7))
                if op is None:
                    return B.V(lambda: nc.vector.tensor_tensor(
                        dst_sbuf[:].rearrange("p (a c) -> p a c", a=8),
                        pA[:].rearrange("p (a c) -> p a c", a=8), bview, ALU.mult),
                        waits=[("T", tn)])
                B.V(lambda: nc.vector.tensor_tensor(
                    scr[:].rearrange("p (a c) -> p a c", a=8),
                    pA[:].rearrange("p (a c) -> p a c", a=8), bview, ALU.mult),
                    waits=[("T", tn)])
                return B.V(lambda: nc.vector.tensor_tensor(
                    dst_sbuf[:], extra[:], scr[:], op))

            for tt in range(NT):
                tsl = slice(tt * 128, (tt + 1) * 128)
                for h in range(8):
                    tn = B.TT(lambda: nc.tensor.transpose(
                        pD[:, h * 128:(h + 1) * 128],
                        vF[:, h * T + tt * 128: h * T + (tt + 1) * 128], ident[:]),
                        waits=([("V", B.n("V"))] if h == 0 else ()), tag=(h == 7))
                bview = betaT[:, tt * 8:(tt + 1) * 8].unsqueeze(2).broadcast_to([128, 8, 128])
                vu0 = B.V(lambda: nc.vector.tensor_tensor(
                    u0[:].rearrange("p (a c) -> p a c", a=8),
                    pD[:].rearrange("p (a c) -> p a c", a=8), bview, ALU.mult),
                    waits=[("T", tn), ("dbg", None)])
                for h in range(8):
                    tr1 = B.TT(lambda: nc.tensor.matmul(
                        pE[:, 0:128], oh8[:, h * 128:(h + 1) * 128],
                        rsq[0:8, 0 * T + tt * 128: 0 * T + (tt + 1) * 128],
                        start=True, stop=True),
                        waits=[("V", B.n("V"))], tag=True)
                    tr2 = B.TT(lambda: nc.tensor.matmul(
                        pE[:, 128:256], oh8[:, h * 128:(h + 1) * 128],
                        rsq[0:8, 1 * T + tt * 128: 1 * T + (tt + 1) * 128],
                        start=True, stop=True), tag=True)
                    vq1 = B.V(lambda: nc.vector.tensor_tensor(
                        qn[:], qF[:, h * T + tt * 128: h * T + (tt + 1) * 128],
                        pE[:, 0:128], ALU.mult), waits=[("T", tr2)])
                    vq2 = B.V(lambda: nc.vector.tensor_tensor(
                        kn[:], kF[:, h * T + tt * 128: h * T + (tt + 1) * 128],
                        pE[:, 128:256], ALU.mult))
                    tn = B.TT(lambda: nc.tensor.matmul(
                        pA[:, h * 128:(h + 1) * 128], kn[:], kn[:],
                        start=True, stop=True), waits=[("V", vq2)], tag=False)
                    tn = B.TT(lambda: nc.tensor.matmul(
                        pB[:, h * 128:(h + 1) * 128], kn[:], qn[:],
                        start=True, stop=True), tag=(h == 7))
                mview = mU[:].unsqueeze(1).broadcast_to([128, 8, 128])
                vau = B.V(lambda: nc.vector.tensor_tensor(
                    aU[:].rearrange("p (a c) -> p a c", a=8),
                    pA[:].rearrange("p (a c) -> p a c", a=8), mview, ALU.mult),
                    waits=[("T", tn)])
                nc.tensor.wait_ge(B.sem("V"), vau)
                app(u0, dch, bview, mbeta)
                app(dch, t1s, bview, mbeta)
                app(t1s, dch, bview, mbeta)
                app(dch, t1s, bview, mbeta, extra=u0, op=ALU.add)
                app(t1s, dch, bview, mbeta)
                app(dch, dch, bview, mbeta, extra=t1s, op=ALU.add)
                vsol = app(dch, t1s, bview, mbeta, extra=dch, op=ALU.subtract)
                mpview = mP[:].unsqueeze(1).broadcast_to([128, 8, 128])
                vpm = B.V(lambda: nc.vector.tensor_tensor(
                    pmS[:].rearrange("p (a c) -> p a c", a=8),
                    pB[:].rearrange("p (a c) -> p a c", a=8), mpview, ALU.mult))
                nc.tensor.wait_ge(B.sem("V"), vpm)
                if l == 0 and tt == 0 and "btile" in debug:
                    B.dma(nc.gpsimd, "dbg", dbg_tensor("bu0", [128, 1024]), u0[:],
                          waits=[("V", None), ("S", None)])
                    B.dma(nc.gpsimd, "dbg", dbg_tensor("bau", [128, 1024]), aU[:])
                    B.dma(nc.gpsimd, "dbg", dbg_tensor("bpm", [128, 1024]), pmS[:])
                    B.dma(nc.gpsimd, "dbg", dbg_tensor("bU", [128, 1024]), t1s[:])
                    nc.tensor.wait_ge(B.sem("dbg"), B.n("dbg"))
                for h in range(8):
                    tn = B.TT(lambda: nc.tensor.matmul(
                        pB[:, h * 128:(h + 1) * 128], pmS[:, h * 128:(h + 1) * 128],
                        t1s[:, h * 128:(h + 1) * 128], start=True, stop=True),
                        tag=(h == 7))
                vo = B.V(lambda: nc.vector.tensor_copy(osb[:], pB[:]), waits=[("T", tn)])
                vm = B.V(lambda: nc.vector.tensor_tensor(
                    dch[:], osb[:], osb[:], ALU.mult))
                vm = B.V(lambda: nc.vector.tensor_reduce(
                    orr[:, 0:8], dch[:].rearrange("p (a c) -> p a c", a=8),
                    AX.X, ALU.add))
                so = B.S(lambda: nc.scalar.activation(
                    orb[:, 0:8], orr[:, 0:8], AF.Ln, bias=smallB[:, 1:2],
                    scale=smallB[:, 3:4]), waits=[("V", vm)])
                so = B.S(lambda: nc.scalar.activation(
                    orr[:, 0:8], orb[:, 0:8], AF.Exp, scale=-0.5))
                oview = orr[:, 0:8].unsqueeze(2).broadcast_to([128, 8, 128])
                vn1 = B.V(lambda: nc.vector.tensor_tensor(
                    dch[:].rearrange("p (a c) -> p a c", a=8),
                    osb[:].rearrange("p (a c) -> p a c", a=8), oview, ALU.mult),
                    waits=[("S", so)])
                wview = onwb[:, l * 128:(l + 1) * 128].unsqueeze(1).broadcast_to([128, 8, 128])
                vn2 = B.V(lambda: nc.vector.tensor_tensor(
                    t1s[:].rearrange("p (a c) -> p a c", a=8),
                    dch[:].rearrange("p (a c) -> p a c", a=8), wview, ALU.mult))
                nc.tensor.wait_ge(B.sem("V"), vn2)
                for h in range(8):
                    tn = B.TT(lambda: nc.tensor.transpose(
                        pD[:, h * 128:(h + 1) * 128],
                        t1s[:, h * 128:(h + 1) * 128], ident[:]), tag=(h == 7))
                vt = B.V(lambda: nc.vector.tensor_copy(dch[:], pD[:]), waits=[("T", tn)])
                nc.tensor.wait_ge(B.sem("V"), vt)
                for dj in range(ND):
                    for h in range(8):
                        tn = B.TT(lambda: nc.tensor.matmul(
                            pC[:, dj * 128:(dj + 1) * 128],
                            ws[:, (h * 8 + dj) * 128:(h * 8 + dj) * 128 + 128],
                            dch[:, h * 128:(h + 1) * 128],
                            start=(h == 0), stop=(h == 7)), tag=(h == 7 and dj == ND - 1))
                for hf in range(2):
                    xview = xR[:].rearrange("p (a t) -> p a t", a=8)\
                        [:, hf * 4:(hf + 1) * 4, tt * 128:(tt + 1) * 128]
                    vx = B.V(lambda: nc.vector.tensor_tensor(
                        xscr[:].rearrange("p (a c) -> p a c", a=4), xview,
                        pC[:, hf * 512:(hf + 1) * 512].rearrange("p (a c) -> p a c", a=4),
                        ALU.add), waits=([("T", tn)] if hf == 0 else ()))
                    vx = B.V(lambda: nc.vector.tensor_copy(
                        xview, xscr[:].rearrange("p (a c) -> p a c", a=4)))
                nc.tensor.wait_ge(B.sem("V"), vx)

            if l == 0 and "bdelta" in debug:
                B.dma(nc.gpsimd, "dbg", dbg_tensor("brsq", [8, 2 * T]), rsq[:],
                      waits=[("S", None), ("V", None), ("T", None)])
                B.dma(nc.gpsimd, "dbg", dbg_tensor("bx1", [128, 8 * T]), xR[:],
                      waits=[("V", None)])
            _emit_b_glu(nc, B, l, wg_h, wd_h, xR, qF, kF, vF, n2w, smallB,
                        nones, onesrow, sqx, nrow, nrb, xscr, pA, pB, pE)

        _emit_b_final(nc, B, y_h, g2_h, out_h, xR, fnw, smallB, nones, onesrow,
                      sqx, nrow, nrb, hc, ident, pA, pD, pE, xscr,
                      qF[:, 0:8192].bitcast(F32), kF[:, 0:8192].bitcast(F32),
                      vF[:, 0:8192].bitcast(F32))


def _emit_b_glu(nc, B, l, wg_h, wd_h, xR, h2buf, actbuf, wpbuf, n2w, smallB,
                nones, onesrow, sqx, nrow, nrb, xscr, pA, pB, pE):
    nc.vector.wait_ge(B.sem("T"), B.n("T"))
    nc.scalar.wait_ge(B.sem("T"), B.n("T"))
    B.wait(nc.vector, "dbg")
    for ch in range(NCH):
        _rms_chunk_b(nc, B, xR, ch,
                     lambda kd: h2buf[:, kd * T + ch * 512: kd * T + (ch + 1) * 512],
                     n2w[:, l * 8:l * 8 + 8],
                     smallB[0:1, 1:2], smallB[0:1, 2:3],
                     sqx, nrow, nrb, nones, onesrow, pE)
    wg_r = wg_h[l].rearrange("(kk p) c -> p kk c", p=128)
    wd_r = wd_h[l].rearrange("(kk p) (dj c) -> kk p dj c", p=128, c=128)
    for f0, nf in ((0, 6), (6, 6), (12, 5), (17, 5)):
        parG = {}
        upT = {}
        for gi in range(nf):
            ft = f0 + gi
            par = gi % 2
            gslot = wpbuf[:, 8192 + par * 2048: 8192 + par * 2048 + 1024]
            yslot = wpbuf[:, 8192 + par * 2048 + 1024: 8192 + (par + 1) * 2048]
            wdslot = wpbuf[:, gi * 1024:(gi + 1) * 1024]
            gw = [("T", max(parG.get(par, 0), B.n("T") if gi < 2 else 0)),
                  ("V", B.n("V")), ("S", B.n("S"))]
            B.dma(nc.gpsimd, "lgg%d" % par, gslot.rearrange("p (kk c) -> p kk c", kk=8),
                  wg_r[:, :, ft * 128:(ft + 1) * 128], waits=gw)
            B.dma(nc.gpsimd, "lgy%d" % par, yslot.rearrange("p (kk c) -> p kk c", kk=8),
                  wg_r[:, :, 2816 + ft * 128: 2816 + (ft + 1) * 128])
            B.dma(nc.gpsimd, "lgd%d" % par, wdslot.rearrange("p (dj c) -> p dj c", dj=8),
                  wd_r[ft])
            B.wait(nc.tensor, "lgg%d" % par)
            B.wait(nc.tensor, "lgy%d" % par)
            for ch in range(NCH):
                for kd in range(ND):
                    nc.tensor.matmul(
                        pA[:, 0:512], gslot[:, kd * 128:(kd + 1) * 128],
                        h2buf[:, kd * T + ch * 512: kd * T + (ch + 1) * 512],
                        start=(kd == 0), stop=(kd == ND - 1))
                    tn = B.TT(lambda: nc.tensor.matmul(
                        pB[:, 0:512], yslot[:, kd * 128:(kd + 1) * 128],
                        h2buf[:, kd * T + ch * 512: kd * T + (ch + 1) * 512],
                        start=(kd == 0), stop=(kd == ND - 1)), tag=(kd == ND - 1))
                sg = B.S(lambda: nc.scalar.activation(
                    sqx[:, 0:512], pA[:, 0:512], AF.Silu), waits=[("T", tn)])
                va = B.V(lambda: nc.vector.tensor_tensor(
                    actbuf[:, (gi * 4 + ch) * 512:(gi * 4 + ch + 1) * 512],
                    sqx[:, 0:512], pB[:, 0:512], ALU.mult), waits=[("S", sg)])
                nc.tensor.wait_ge(B.sem("S"), sg)
                nc.tensor.wait_ge(B.sem("V"), va)
            parG[par] = B.n("T")
        B.wait(nc.tensor, "lgd0")
        B.wait(nc.tensor, "lgd1")
        pdV = {}
        for ch in range(NCH):
            for dj in range(ND):
                psl = dj % 2
                nc.tensor.wait_ge(B.sem("V"), pdV.get(psl, 0))
                for gi in range(nf):
                    tn = B.TT(lambda: nc.tensor.matmul(
                        pA[:, psl * 512:(psl + 1) * 512],
                        wpbuf[:, gi * 1024 + dj * 128: gi * 1024 + dj * 128 + 128],
                        actbuf[:, (gi * 4 + ch) * 512:(gi * 4 + ch + 1) * 512],
                        start=(gi == 0), stop=(gi == nf - 1)), tag=(gi == nf - 1))
                B.V(lambda: nc.vector.tensor_tensor(
                    xscr[:, 0:512], xR[:, dj * T + ch * 512: dj * T + (ch + 1) * 512],
                    pA[:, psl * 512:(psl + 1) * 512], ALU.add), waits=[("T", tn)])
                pdV[psl] = B.V(lambda: nc.vector.tensor_copy(
                    xR[:, dj * T + ch * 512: dj * T + (ch + 1) * 512], xscr[:, 0:512]))
        nc.tensor.wait_ge(B.sem("V"), B.n("V"))


def _emit_b_final(nc, B, y_h, g2_h, out_h, xR, fnw, smallB, nones, onesrow,
                  sqx, nrow, nrb, hc, ident, pA, pD, pE, xscr, ytl, gtl, otl):
    if True:
        oV = {}
        pDg = 0
        g0 = [("T", None), ("V", None), ("S", None)]
        for ch in range(NCH):
            _rms_chunk_b(nc, B, xR, ch,
                         lambda kd: hc[:, kd * 512:(kd + 1) * 512],
                         fnw[:, 0:8],
                         smallB[0:1, 1:2], smallB[0:1, 2:3],
                         sqx, nrow, nrb, nones, onesrow, pE)
            for tb in range(4):
                tt = ch * 4 + tb
                par = tt % 2
                sy = "ly%d" % par
                B.dma(nc.sync, sy, ytl[:, par * D:(par + 1) * D],
                      y_h.rearrange("(t p) d -> p t d", p=128)[:, tt, :],
                      waits=[("ot%d" % par, None)] + g0)
                B.dma(nc.sync, "lg%d" % par, gtl[:, par * D:(par + 1) * D],
                      g2_h.rearrange("(t p) d -> p t d", p=128)[:, tt, :],
                      waits=[("V", oV.get(par, 0) or 0), ("ot%d" % par, None)] + g0)
                for dj in range(ND):
                    tn = B.TT(lambda: nc.tensor.transpose(
                        pD[:, dj * 128:(dj + 1) * 128],
                        hc[:, dj * 512 + tb * 128: dj * 512 + (tb + 1) * 128],
                        ident[:]),
                        waits=([("V", max(oV.get(par, 0), pDg))] if dj == 0 else ()),
                        tag=(dj == ND - 1))
                B.wait(nc.vector, sy)
                B.wait(nc.vector, "lg%d" % par)
                pDg = B.V(lambda: nc.vector.tensor_tensor(
                    otl[:, par * D:(par + 1) * D].rearrange("p (a c) -> p a c", a=8),
                    gtl[:, par * D:(par + 1) * D].rearrange("p (a c) -> p a c", a=8),
                    pD[:].rearrange("p (a c) -> p a c", a=8), ALU.mult),
                    waits=[("T", tn)])
                oV[par] = B.V(lambda: nc.vector.tensor_add(
                    otl[:, par * D:(par + 1) * D], ytl[:, par * D:(par + 1) * D],
                    otl[:, par * D:(par + 1) * D])) if False else None
                oV[par] = B.V(lambda: nc.vector.tensor_tensor(
                    gtl[:, par * D:(par + 1) * D],
                    ytl[:, par * D:(par + 1) * D],
                    otl[:, par * D:(par + 1) * D], ALU.add))
                B.dma(nc.sync, "ot%d" % par,
                      out_h.rearrange("(t p) d -> p t d", p=128)[:, tt, :],
                      gtl[:, par * D:(par + 1) * D], waits=[("V", oV[par])])
        B.wait(nc.sync, "ot0")
        B.wait(nc.sync, "ot1")



def _pack_col(v):
    return np.ascontiguousarray(v.reshape(8, 128).T)


def _consts_a():
    ident = np.eye(128, dtype=np.float32)
    onesrow = np.ones((1, 128), np.float32)
    m = np.zeros((4, 128, 512), np.float32)
    r = np.arange(128)[:, None]
    c = np.arange(512)[None, :]
    for mm in range(4):
        m[mm] = np.where(128 * mm + r > c, NEG, 0.0)
    ohc = np.zeros((128, 16, 16), np.float32)
    for h in range(16):
        rows = slice(0, 64) if h % 2 == 0 else slice(64, 128)
        ohc[rows, h, h] = 1.0
    oh16 = np.zeros((16, 16, 128), np.float32)
    for h in range(16):
        oh16[h, h, :] = 1.0
    return dict(identity_i=ident, onesrow_i=onesrow, maskL0T=m,
                ohc16_i=np.ascontiguousarray(ohc.reshape(128, 256)),
                oh16_i=np.ascontiguousarray(oh16.reshape(16, 2048)))


def _inputs_a(inp, c):
    t = np.ascontiguousarray
    pair = lambda w: t(w.transpose(0, 2, 1).reshape(8, 2, 1024, 64)
                       .transpose(0, 2, 1, 3).reshape(8, 1024, 128))
    d = {
        "x": t(inp["L0_attn_x"][c]),
        "wq0": pair(inp["L0_attn_W_Q"]),
        "wk0": pair(inp["L0_attn_W_K"]),
        "wv0": pair(inp["L0_attn_W_V"]),
        "wo0t": t(inp["L0_attn_W_O"].T),
        "wr0t": t(inp["L0_ffn_W_r"].T),
        "wk0ft": t(inp["L0_ffn_W_k"].T),
        "wv0ft": t(inp["L0_ffn_W_v"].T),
        "mur": _pack_col(inp["L0_ffn_mu_r"]),
        "muk": _pack_col(inp["L0_ffn_mu_k"]),
    }
    d.update(_consts_a())
    return d




def _consts_a2():
    DH = 64
    pos = np.arange(T, dtype=np.float64) / 32.0
    inv = 1.0 / (150000.0 ** (np.arange(0, DH, 2, dtype=np.float64) / DH))
    fr = inv[:, None] * pos[None, :]
    cos = np.ascontiguousarray(np.tile(np.cos(fr).astype(np.float32), (4, 1)))
    sin = np.ascontiguousarray(np.tile(np.sin(fr).astype(np.float32), (4, 1)))
    r = np.arange(128)[:, None]
    c = np.arange(128)[None, :]
    sub = np.where(c >= r, 0.0, NEG).astype(np.float32)
    diag = np.where(c <= r, 0.0, NEG).astype(np.float32)
    mwin = np.concatenate([sub, diag], axis=1)
    oh32 = np.zeros((32, 32, 128), np.float32)
    for e in range(32):
        oh32[e, e, :] = 1.0
    return dict(cosq_i=cos, sinq_i=sin, maskwin_i=mwin,
                oh32_i=np.ascontiguousarray(oh32.reshape(32, 4096)))


_PERM64 = np.concatenate([np.arange(0, 64, 2), np.arange(1, 64, 2)])


def _inputs_a2(inp, y_c):
    t = np.ascontiguousarray
    qkvw = inp["L1_ffn_qkv_w"].copy()
    qkvb = inp["L1_ffn_qkv_b"].copy()
    for h in range(16):
        qkvw[:, h * 64:(h + 1) * 64] = qkvw[:, h * 64 + _PERM64]
        qkvb[h * 64:(h + 1) * 64] = qkvb[h * 64 + _PERM64]
    for kv in range(4):
        b0 = 1024 + kv * 64
        qkvw[:, b0:b0 + 64] = qkvw[:, b0 + _PERM64]
        qkvb[b0:b0 + 64] = qkvb[b0 + _PERM64]
    d = {
        "y": t(y_c),
        "qkvw_i": t(qkvw),
        "qkvb_i": t(qkvb.reshape(12, 128).T),
        "projw_i": t(inp["L1_ffn_proj_w"]),
        "projb_i": _pack_col(inp["L1_ffn_proj_b"]),
        "sinkr": t(inp["L1_ffn_attn_sink"].reshape(1, 16)),
        "gatew_i": t(inp["L1_ffn_gate_w"]),
        "gateb_i": t(inp["L1_ffn_gate_b"].reshape(32, 1)),
        "wup_i": t(inp["L1_ffn_W_up"]),
        "wdn_i": t(inp["L1_ffn_W_down"]),
        "ffnw_i": _pack_col(inp["L1_ffn_ffn_norm_w"]),
        "attnw_i": _pack_col(inp["L1_ffn_attn_norm_w"]),
    }
    d.update(_consts_a2())
    c1 = _consts_a()
    d["identity_i"] = c1["identity_i"]
    d["onesrow_i"] = c1["onesrow_i"]
    return d



def _consts_b():
    ij = np.arange(128)
    sameg = (ij[:, None] // 8) == (ij[None, :] // 8)
    mU = (sameg & ((ij[:, None] % 8) < (ij[None, :] % 8))).astype(np.float32)
    mP = (sameg & ((ij[:, None] % 8) <= (ij[None, :] % 8))).astype(np.float32)
    oh8 = np.zeros((8, 8, 128), np.float32)
    for h in range(8):
        oh8[h, h, :] = 1.0
    ohc8 = np.zeros((128, 8, 8), np.float32)
    for h in range(8):
        ohc8[:, h, h] = 1.0
    return dict(maskU_i=mU, maskP_i=mP,
                oh8_i=np.ascontiguousarray(oh8.reshape(8, 1024)),
                ohc8_i=np.ascontiguousarray(ohc8.reshape(128, 64)))


def _inputs_b(inp, y_j, g2_j):
    t = np.ascontiguousarray
    d = {
        "yb": t(y_j.astype(np.float32)),
        "g2b": t(g2_j.astype(np.float32)),
        "bWq": t(inp["L1_attn_Wq"]),
        "bWk": t(inp["L1_attn_Wk"]),
        "bWv": t(inp["L1_attn_Wv"]),
        "bWb": t(inp["L1_attn_Wb"]),
        "bWo": t(inp["L1_attn_Wo"]),
        "bWg": t(inp["L1_attn_Wg"]),
        "bWd": t(inp["L1_attn_Wd"]),
        "n1w": t(np.stack([_pack_col(inp["L1_attn_norm1_w"][l]) for l in range(2)], 1)
                 .reshape(128, 16)),
        "n2w": t(np.stack([_pack_col(inp["L1_attn_norm2_w"][l]) for l in range(2)], 1)
                 .reshape(128, 16)),
        "onw": t(inp["L1_attn_onorm_w"]),
        "fnw": _pack_col(inp["L1_attn_final_norm_w"]),
    }
    d.update(_consts_b())
    c1 = _consts_a()
    d["identity_i"] = c1["identity_i"]
    d["onesrow_i"] = c1["onesrow_i"]
    return d

def _dt_rows(a):
    return a.reshape(128, 8, T).transpose(2, 1, 0).reshape(T, D)


def kernel(**inputs):
    inputs = {k: np.asarray(v) for k, v in inputs.items()}
    cores = list(range(8))

    nc_a = build_phase_a()
    res_a = run_bass_kernel_spmd(nc_a, [_inputs_a(inputs, c) for c in cores],
                                 core_ids=cores)
    y = np.stack([np.asarray(r["y"]) for r in res_a.results], 0)

    nc_a2 = build_phase_a2()
    res_a2 = run_bass_kernel_spmd(nc_a2, [_inputs_a2(inputs, y[c]) for c in cores],
                                  core_ids=cores)
    g2 = np.stack([_dt_rows(np.asarray(r["g2"])) for r in res_a2.results], 0)

    def resl(a, j):
        return np.ascontiguousarray(
            a[:, j * 256:(j + 1) * 256, :].transpose(1, 0, 2).reshape(T, D))

    nc_b = build_phase_b()
    res_b = run_bass_kernel_spmd(
        nc_b, [_inputs_b(inputs, resl(y, j), resl(g2, j)) for j in cores],
        core_ids=cores)
    out = np.empty((8, T, D), np.float32)
    for j in cores:
        o = np.asarray(res_b.results[j]["outb"]).reshape(256, 8, D).transpose(1, 0, 2)
        out[:, j * 256:(j + 1) * 256, :] = o
    return out
